# version 37
# baseline (speedup 1.0000x reference)
import sys
if '/opt/trn_rl_repo' not in sys.path:
    sys.path.insert(0, '/opt/trn_rl_repo')
import numpy as np
from contextlib import ExitStack

import concourse.bass as bass
import concourse.mybir as mybir
from concourse.bass_utils import run_bass_kernel_spmd

F32 = mybir.dt.float32
BF16 = mybir.dt.bfloat16
AF = mybir.ActivationFunctionType
ALU = mybir.AluOpType

T = 4096
TC = 512
NCH = T // TC
DT = 8

ENG = ("sync", "vector", "scalar", "tensor", "gpsimd")
QUEUE = {"sp": ("sync", "dsp"), "pl": ("gpsimd", "dpl")}
ESEM = {"vector": "ve", "scalar": "se", "tensor": "pe", "gpsimd": "pl"}


class Buf:
    __slots__ = ("ev", "readers", "name", "serial")
    def __init__(self, name=""):
        self.ev = None
        self.readers = []
        self.name = name
        self.serial = name.startswith("ps")


NSLOT = 8

class Prog:
    def __init__(self, nc, stack):
        self.strict = True
        self.nc = nc
        self.stack = stack
        self.ops = {e: [] for e in ENG}
        self.sem = {}
        self.count = {}
        self.last_wait = {}
        for s in ("ve", "se", "pe", "pl"):
            self.sem[s] = stack.enter_context(nc.semaphore(s))
            self.count[s] = 0
        self.qslot = {}
        for q in ("dsp", "dpl"):
            for i in range(NSLOT):
                s = "%s%d" % (q, i)
                self.sem[s] = stack.enter_context(nc.semaphore(s))
                self.count[s] = 0
            self.qslot[q] = 0
        self.n_instr = 0

    def _emit(self, eng, fn, deps, inc_sem, inc_n):
        need = {}
        for s, c in list(deps) + getattr(self, "floor", []):
            if c is None or c <= 0:
                continue
            if s in ("ve", "se", "pe", "pl") and ESEM.get(eng) == s and not self.strict:
                continue
            if need.get(s, 0) < c:
                need[s] = c
        waits = []
        for s, c in need.items():
            k = (eng, s)
            if self.last_wait.get(k, 0) >= c:
                continue
            self.last_wait[k] = c
            waits.append((self.sem[s], c))
        if inc_sem is not None:
            self.count[inc_sem] += inc_n
            tgt = (self.sem[inc_sem], inc_n)
            ev = (inc_sem, self.count[inc_sem])
        else:
            tgt = None
            ev = None
        class _Rec:
            __slots__ = ("calls",)
            def __init__(self):
                self.calls = []
            def __getattr__(self, name):
                def m(*a, **k):
                    self.calls.append((name, a, k))
                    return None
                return m
        rec = _Rec()
        fn(rec)
        assert rec.calls, "emission recorded no instruction"
        def closure(h, waits=waits, calls=rec.calls, tgt=tgt):
            for sh, c in waits:
                h.wait_ge(sh, c)
            ins = None
            for (name, a, k) in calls:
                ins = getattr(h, name)(*a, **k)
            if tgt is not None:
                ins.then_inc(tgt[0], tgt[1])
        self.ops[eng].append(closure)
        self.n_instr += len(rec.calls) + len(waits)
        return ev

    @staticmethod
    def _collect(items):
        deps = []
        for b in items:
            if b is None:
                continue
            if isinstance(b, tuple):
                deps.append(b)
            else:
                if b.ev is not None:
                    deps.append(b.ev)
        return deps

    def rw(self, eng, fn, reads=(), writes=(), extra=()):
        reads = list(reads)
        writes = list(writes)
        for b in list(reads):
            if b is not None and not isinstance(b, tuple) and b.serial:
                reads.remove(b)
                writes.append(b)
        deps = self._collect(reads)
        for b in writes:
            if b is None:
                continue
            if b.ev is not None:
                deps.append(b.ev)
            deps.extend(b.readers)
        deps.extend(self._collect(extra))
        ev = self._emit(eng, fn, deps, ESEM[eng], 1)
        for b in reads:
            if b is not None and not isinstance(b, tuple):
                b.readers.append(ev)
        for b in writes:
            if b is not None:
                b.ev = ev
                b.readers = []
        return ev

    def dma(self, q, fn, reads=(), writes=(), extra=()):
        reads = list(reads)
        writes = list(writes)
        for b in list(reads):
            if b is not None and not isinstance(b, tuple) and b.serial:
                reads.remove(b)
                writes.append(b)
        eng, qname = QUEUE[q]
        slot = self.qslot[qname]
        self.qslot[qname] = (slot + 1) % NSLOT
        semn = "%s%d" % (qname, slot)
        deps = self._collect(reads)
        deps.append((semn, self.count[semn]))
        for b in writes:
            if b is None:
                continue
            if b.ev is not None:
                deps.append(b.ev)
            deps.extend(b.readers)
        deps.extend(self._collect(extra))
        ev = self._emit(eng, fn, deps, semn, 16)
        for b in reads:
            if b is not None and not isinstance(b, tuple):
                b.readers.append(ev)
        for b in writes:
            if b is not None:
                b.ev = ev
                b.readers = []
        return ev

    def barrier(self):
        evs = [(s, self.count[s]) for s in self.sem if self.count[s] > 0]
        return evs

    def run(self):
        with self.nc.Block() as block:
            for e in ENG:
                getattr(block, e)(lambda h, e=e: [f(h) for f in self.ops[e]])


def sb(nc, stack, name, shape, dtype):
    return stack.enter_context(nc.sbuf_tensor(name, shape, dtype))


def ps_(nc, stack, name, shape, dtype=F32):
    return stack.enter_context(nc.psum_tensor(name, shape, dtype))


def build(debug=0):
    PH = 6
    SUB = 9
    nc = bass.Bass("TRN2", target_bir_lowering=False, debug=False,
                   detect_race_conditions=False)
    d = {}
    def din(name, shape, dtype=F32):
        d[name] = nc.dram_tensor(name, shape, dtype, kind="ExternalInput").ap()
    def dout(name, shape, dtype=F32):
        d[name] = nc.dram_tensor(name, shape, dtype, kind="ExternalOutput").ap()
    def dint(name, shape, dtype=F32):
        kind = "ExternalOutput" if debug else "Internal"
        d[name] = nc.dram_tensor(name, shape, dtype, kind=kind).ap()

    din("xT", [1024, T])
    for w in ("wrT", "wkT", "wvT", "woT"):
        din(w, [1024, 1024])
    for v in ("tmr", "tmk", "tmv", "decay", "first", "gnw", "gnb"):
        din(v, [128, 8])
    for l in range(2):
        din("wqkv%d" % l, [1024, 1536])
        din("wol%d" % l, [512, 1024])
        din("w1l%d" % l, [1024, 5460])
        din("w2l%d" % l, [2730, 1024])
        din("qs%d" % l, [128, 1])
        din("ks%d" % l, [128, 1])
        din("lg%d" % l, [128, 8])
        din("lb%d" % l, [128, 8])
        din("b1l%d" % l, [5460, 1])
        din("b2l%d" % l, [128, 8])
    din("identb", [128, 128], BF16)
    din("identf", [128, 128])
    din("identd", [128, 64])
    for l in range(2):
        din("b1a%d" % l, [128, 22])
        din("b1g%d" % l, [128, 22])
    din("permb", [128, 128], BF16)
    din("onesf", [128, 1])
    din("halvesf", [128, 2])
    din("og4", [128, 32])
    din("og4b", [128, 32], BF16)
    din("onesb1", [128, 1], BF16)
    din("maskl", [128, 128], BF16)
    din("maskr", [128, 128], BF16)
    din("bg4", [8, 512], BF16)
    din("bonef", [1, 128])
    din("halves2", [2, 128])
    for t_ in ("c1", "s1", "c2", "s2"):
        din(t_, [128, T])
    dout("outT", [1024, T])
    dint("sc_out", [1024, T], BF16)
    dint("sc_x1", [1024, T])
    dint("sc_xf0", [1024, T])
    dint("sc_xm", [1024, T])
    dint("sc_xf1", [1024, T])
    d["sc_stats"] = nc.dram_tensor("sc_stats", [2, T], F32, kind="Internal").ap()
    nc_sums = nc.dram_tensor("sc_sums", [8 * 512], F32, kind="Internal"); d["sc_sums"] = nc_sums.ap()
    if debug:
        dout("dbg_out", [1024, T])
        dout("dbg_sb", [128, 64])

    with ExitStack() as st:
      try:
        P = Prog(nc, st)
        cpool = sb(nc, st, "sb_cpool", [128, 8], F32)
        ident = sb(nc, st, "sb_ident", [128, 128], BF16)
        perm = sb(nc, st, "sb_perm", [128, 128], BF16)
        onesf = sb(nc, st, "sb_onesf", [128, 1], F32)
        halvesf = sb(nc, st, "sb_halvesf", [128, 2], F32)
        og4 = sb(nc, st, "sb_og4", [128, 32], F32)
        og4b = sb(nc, st, "sb_og4b", [128, 32], BF16)
        onesb1 = sb(nc, st, "sb_onesb1", [128, 1], BF16)
        maskl = sb(nc, st, "sb_maskl", [128, 128], BF16)
        maskr = sb(nc, st, "sb_maskr", [128, 128], BF16)
        bg4 = sb(nc, st, "sb_bg4", [8, 512], BF16)
        bonef = sb(nc, st, "sb_bonef", [1, 128], F32)
        halves2 = sb(nc, st, "sb_halves2", [2, 128], F32)
        smallv = {}
        for v in ("tmr", "tmk", "tmv", "decay", "first", "gnw", "gnb"):
            smallv[v] = sb(nc, st, "s_" + v, [128, 8], F32)
        xin = [[sb(nc, st, "xin%d_%d" % (p, g), [128, TC + 1], F32) for g in range(DT)] for p in range(2)]
        hbf = [sb(nc, st, "hbf%d" % g, [128, TC + 1], BF16) for g in range(DT)]
        stt_m = sb(nc, st, "stt_m", [1, TC], F32)
        stt_i = sb(nc, st, "stt_i", [1, TC], F32)
        fin8 = sb(nc, st, "fin8", [128, 48], F32)
        asum = [sb(nc, st, "asum%d" % g, [128, NCH], F32) for g in range(DT)]
        qsum = [sb(nc, st, "qsum%d" % g, [128, NCH], F32) for g in range(DT)]
        scale_sb = sb(nc, st, "scale_sb", [128, 8], F32)
        bias_sb = sb(nc, st, "bias_sb", [128, 8], F32)
        w_eu = sb(nc, st, "w_eu", [128, 16], F32)

        B = {}
        def buf(name):
            if name not in B:
                B[name] = Buf(name)
            return B[name]

        P.dma("sp", lambda h: h.dma_start(out=ident[:, :], in_=d["identb"]), writes=[buf("ident")])
        P.dma("sp", lambda h: h.dma_start(out=perm[:, :], in_=d["permb"]), writes=[buf("perm")])
        P.dma("sp", lambda h: h.dma_start(out=onesf[:, :], in_=d["onesf"]), writes=[buf("onesf")])
        P.dma("sp", lambda h: h.dma_start(out=halvesf[:, :], in_=d["halvesf"]), writes=[buf("halvesf")])
        P.dma("sp", lambda h: h.dma_start(out=og4[:, :], in_=d["og4"]), writes=[buf("og4")])
        P.dma("sp", lambda h: h.dma_start(out=og4b[:, :], in_=d["og4b"]), writes=[buf("og4b")])
        P.dma("sp", lambda h: h.dma_start(out=onesb1[:, :], in_=d["onesb1"]), writes=[buf("onesb1")])
        P.dma("sp", lambda h: h.dma_start(out=maskl[:, :], in_=d["maskl"]), writes=[buf("maskl")])
        P.dma("sp", lambda h: h.dma_start(out=maskr[:, :], in_=d["maskr"]), writes=[buf("maskr")])
        P.dma("sp", lambda h: h.dma_start(out=bg4[:, :], in_=d["bg4"]), writes=[buf("bg4")])
        P.dma("sp", lambda h: h.dma_start(out=bonef[:, :], in_=d["bonef"]), writes=[buf("bonef")])
        P.dma("sp", lambda h: h.dma_start(out=halves2[:, :], in_=d["halves2"]), writes=[buf("halves2")])
        for v in ("tmr", "tmk", "tmv", "decay", "first", "gnw", "gnb"):
            P.dma("sp", lambda h, v=v: h.dma_start(out=smallv[v][:, :], in_=d[v]), writes=[buf("sv_" + v)])
        P.rw("gpsimd", lambda h: h.memset(cpool[:, 0:1], -8.0), writes=[buf("cpool")])
        P.rw("gpsimd", lambda h: h.memset(fin8[:, :], 0.0), writes=[buf("fin8")])
        P.rw("gpsimd", lambda h: h.memset(cpool[:, 1:2], 1e-5), writes=[buf("cpool")])
        P.rw("scalar", lambda h: h.activation(w_eu[:, 0:8], smallv["decay"][:, :], AF.Exp),
             reads=[buf("sv_decay")], writes=[buf("w_eu")])
        P.rw("scalar", lambda h: h.activation(w_eu[:, 0:8], w_eu[:, 0:8], AF.Exp, scale=-1.0),
             writes=[buf("w_eu")])
        P.rw("scalar", lambda h: h.activation(w_eu[:, 8:16], smallv["first"][:, :], AF.Exp),
             reads=[buf("sv_first")], writes=[buf("w_eu")])

        with ExitStack() as stA:
            wr = [sb(nc, stA, "wr%d" % g, [128, 1024], BF16) for g in range(DT)]
            wk = [sb(nc, stA, "wk%d" % g, [128, 1024], BF16) for g in range(DT)]
            wv = [sb(nc, stA, "wv%d" % g, [128, 1024], BF16) for g in range(DT)]
            hb = [sb(nc, stA, "hb%d" % g, [128, TC + 1], BF16) for g in range(DT)]
            numb = [sb(nc, stA, "numb%d" % g, [128, TC + 1], F32) for g in range(DT)]
            denb = [sb(nc, stA, "denb%d" % g, [128, TC + 1], F32) for g in range(DT)]
            NS = 2
            S1 = [sb(nc, stA, "Sa%d" % g, [128, TC], F32) for g in range(NS)]
            S2 = [sb(nc, stA, "Sb%d" % g, [128, TC], F32) for g in range(NS)]
            S3 = [sb(nc, stA, "Sc%d" % g, [128, TC], F32) for g in range(NS)]
            Bx = [[[sb(nc, stA, "Bx%d_%d_%d" % (p, i, g), [128, TC], BF16) for g in range(DT)] for i in range(3)] for p in range(2)]
            ddb = [sb(nc, stA, "ddb0", [128, TC], BF16)]
            sqs = sb(nc, stA, "sqs", [128, TC], BF16)
            rtmpA = sb(nc, stA, "rtmpA", [128, TC], BF16)
            ps_stat = ps_(nc, stA, "ps_stat", [1, TC])
            ps_stat2 = ps_(nc, stA, "ps_stat2", [1, TC])
            ps_b1 = ps_(nc, stA, "ps_b1", [128, TC])
            ps_b2 = ps_(nc, stA, "ps_b2", [128, TC])
            ps_mm = [ps_(nc, stA, "ps_mm%d" % i, [128, TC]) for i in range(4)]

            for g in range(DT):
                for nm, dst in (("wrT", wr), ("wkT", wk), ("wvT", wv)):
                    P.dma("pl", lambda h, nm=nm, dst=dst, g=g: h.dma_start(
                        out=dst[g][:, :], in_=d[nm][128 * g:128 * (g + 1), :]),
                        writes=[buf("w_%s%d" % (nm, g))])
            for g in range(DT):
                P.rw("gpsimd", lambda h, g=g: h.memset(hb[g][:, TC:TC + 1], 0.0), writes=[buf("hb%d" % g)])
                P.rw("gpsimd", lambda h, g=g: h.memset(numb[g][:, TC:TC + 1], 0.0), writes=[buf("numb%d" % g)])
                P.rw("gpsimd", lambda h, g=g: h.memset(denb[g][:, TC:TC + 1], 0.0), writes=[buf("denb%d" % g)])

            def frontA_thunks(c):
                t0 = c * TC
                par = c % 2
                xb = xin[par]
                th = []
                def t_loads():
                    for g in range(DT):
                        P.dma("sp", lambda h, g=g, t0=t0, xb=xb: h.dma_start(
                            out=xb[g][:, 1:TC + 1], in_=d["xT"][128 * g:128 * (g + 1), t0:t0 + TC]),
                            writes=[buf("xin%d_%d" % (par, g))])
                th.append(t_loads)
                def t_stat(g):
                    bx = buf("xin%d_%d" % (par, g))
                    bsq = buf("sqs")
                    P.rw("scalar", lambda h, g=g, xb=xb: h.activation(sqs[:, :], xb[g][:, 1:TC + 1], AF.Square),
                         reads=[bx], writes=[bsq])
                    P.rw("tensor", lambda h, g=g, xb=xb: h.matmul(
                        ps_stat[:, :], onesf[:, :], xb[g][:, 1:TC + 1], start=(g == 0), stop=(g == DT - 1)),
                        reads=[bx, buf("onesf")], writes=[buf("ps_stat")])
                    P.rw("tensor", lambda h, g=g: h.matmul(
                        ps_stat2[:, :], onesb1[:, :], sqs[:, :], start=(g == 0), stop=(g == DT - 1)),
                        reads=[bsq, buf("onesb1")], writes=[buf("ps_stat2")])
                for g in range(DT):
                    th.append(lambda g=g: t_stat(g))
                def t_fin():
                    P.rw("vector", lambda h: h.tensor_scalar(stt_m[:, :], ps_stat[:, :], 1.0 / 1024, None, ALU.mult),
                         reads=[buf("ps_stat")], writes=[buf("stt_m")])
                    P.rw("vector", lambda h: h.tensor_scalar(stt_i[:, :], ps_stat2[:, :], 1.0 / 1024, None, ALU.mult),
                         reads=[buf("ps_stat2")], writes=[buf("stt_i")])
                    P.rw("vector", lambda h: h.scalar_tensor_tensor(
                        sqs[0:1, 0:TC], stt_m[:, :], 1.0, stt_m[:, :], ALU.mult, ALU.mult),
                        reads=[buf("stt_m")], writes=[buf("sqs")])
                    P.rw("vector", lambda h: h.tensor_sub(stt_i[:, :], stt_i[:, :], sqs[0:1, 0:TC]),
                         reads=[buf("sqs")], writes=[buf("stt_i")])
                    P.rw("scalar", lambda h: h.activation(stt_i[:, :], stt_i[:, :], AF.Ln, bias=cpool[0:1, 1:2]),
                         reads=[buf("cpool")], writes=[buf("stt_i")])
                    P.rw("scalar", lambda h: h.activation(stt_i[:, :], stt_i[:, :], AF.Exp, scale=-0.5),
                         writes=[buf("stt_i")])
                    P.rw("tensor", lambda h: h.matmul(ps_b1[:, :], bonef[:, :], stt_m[:, :], start=True, stop=True),
                         reads=[buf("stt_m"), buf("bonef")], writes=[buf("ps_b1")])
                    P.rw("tensor", lambda h: h.matmul(ps_b2[:, :], bonef[:, :], stt_i[:, :], start=True, stop=True),
                         reads=[buf("stt_i"), buf("bonef")], writes=[buf("ps_b2")])
                th.append(t_fin)
                def t_h(g):
                    bh = buf("hb%d" % g)
                    dd = ddb[0]
                    bdd = buf("ddb0")
                    P.rw("vector", lambda h, g=g: h.tensor_copy(hb[g][:, 0:1], hb[g][:, TC:TC + 1]),
                         writes=[bh])
                    P.rw("vector", lambda h, g=g, xb=xb: h.tensor_sub(
                        hb[g][:, 1:TC + 1], xb[g][:, 1:TC + 1], ps_b1[:, :]),
                        reads=[buf("xin%d_%d" % (par, g)), buf("ps_b1")], writes=[bh])
                    P.rw("vector", lambda h, g=g: h.tensor_mul(
                        hb[g][:, 1:TC + 1], hb[g][:, 1:TC + 1], ps_b2[:, :]),
                        reads=[buf("ps_b2")], writes=[bh])
                    P.rw("gpsimd", lambda h, g=g, dd=dd: h.tensor_sub(
                        dd[:, :], hb[g][:, 1:TC + 1], hb[g][:, 0:TC]),
                        reads=[bh], writes=[bdd])
                    for i, tm in enumerate(("tmr", "tmk", "tmv")):
                        P.rw("gpsimd", lambda h, g=g, tm=tm, dd=dd: h.tensor_scalar(
                            rtmpA[:, :], dd[:, :], smallv[tm][:, g:g + 1], None, ALU.mult),
                            reads=[bdd, buf("sv_" + tm)], writes=[buf("rtmpA")])
                        P.rw("gpsimd", lambda h, g=g, i=i: h.tensor_add(
                            Bx[par][i][g][:, :], rtmpA[:, :], hb[g][:, 0:TC]),
                            reads=[buf("rtmpA"), bh],
                            writes=[buf("Bx%d_%d_%d" % (par, i, g))])
                for g in range(DT):
                    th.append(lambda g=g: t_h(g))
                return th

            def scanA(c, th):
                par = c % 2
                t0 = c * TC
                def mk_mm(dst_ps, wts, bxi, g, par=par):
                    def f(h):
                        ins = None
                        for gg in range(DT):
                            ins = h.matmul(dst_ps[:, :], wts[gg][:, 128 * g:128 * (g + 1)],
                                           Bx[par][bxi][gg][:, :], start=(gg == 0), stop=(gg == DT - 1))
                        return ins
                    return f
                def emit_k(g):
                    sg = g % NS
                    kp = ps_mm[(g % 2) * 3]
                    bk_ps = buf("ps_mm%d" % ((g % 2) * 3))
                    P.rw("tensor", mk_mm(kp, wk, 1, g),
                         reads=[buf("Bx%d_1_%d" % (par, gg)) for gg in range(DT)] + [buf("w_wkT%d" % gg) for gg in range(DT)],
                         writes=[bk_ps])
                    P.rw("scalar", lambda h, g=g, sg=sg, kp=kp: h.activation(S2[sg][:, :], kp[:, :], AF.Exp),
                         reads=[bk_ps], writes=[buf("S2%d" % sg)])
                emit_k(0)
                for g in range(DT):
                    sg = g % NS
                    bv_ps, br_ps = buf("ps_mm1"), buf("ps_mm2")
                    P.rw("tensor", mk_mm(ps_mm[1], wv, 2, g),
                         reads=[buf("Bx%d_2_%d" % (par, gg)) for gg in range(DT)] + [buf("w_wvT%d" % gg) for gg in range(DT)],
                         writes=[bv_ps])
                    P.rw("vector", lambda h, g=g, sg=sg: h.tensor_mul(S3[sg][:, :], S2[sg][:, :], ps_mm[1][:, :]),
                         reads=[buf("S2%d" % sg), bv_ps], writes=[buf("S3%d" % sg)])
                    if g + 1 < DT:
                        emit_k(g + 1)
                    bn, bd = buf("numb%d" % g), buf("denb%d" % g)
                    P.rw("vector", lambda h, g=g: h.tensor_copy(numb[g][:, 0:1], numb[g][:, TC:TC + 1]),
                         writes=[bn])
                    P.rw("vector", lambda h, g=g: h.tensor_copy(denb[g][:, 0:1], denb[g][:, TC:TC + 1]),
                         writes=[bd])
                    P.rw("vector", lambda h, g=g, sg=sg: h.tensor_tensor_scan(
                        numb[g][:, 1:TC + 1], w_eu[:, g:g + 1].broadcast_to((128, TC)), S3[sg][:, :],
                        numb[g][:, 0:1], ALU.mult, ALU.add),
                        reads=[buf("w_eu"), buf("S3%d" % sg)], writes=[bn])
                    P.rw("vector", lambda h, g=g, sg=sg: h.tensor_tensor_scan(
                        denb[g][:, 1:TC + 1], w_eu[:, g:g + 1].broadcast_to((128, TC)), S2[sg][:, :],
                        denb[g][:, 0:1], ALU.mult, ALU.add),
                        reads=[buf("w_eu"), buf("S2%d" % sg)], writes=[bd])
                    P.rw("vector", lambda h, g=g, sg=sg: h.scalar_tensor_tensor(
                        S3[sg][:, :], S3[sg][:, :], w_eu[:, 8 + g:9 + g], numb[g][:, 0:TC], ALU.mult, ALU.add),
                        reads=[buf("w_eu"), bn], writes=[buf("S3%d" % sg)])
                    P.rw("vector", lambda h, g=g, sg=sg: h.tensor_scalar(
                        S2[sg][:, :], S2[sg][:, :], w_eu[:, 8 + g:9 + g], 1e-9, ALU.mult, ALU.add),
                        reads=[buf("w_eu")], writes=[buf("S2%d" % sg)])
                    P.rw("vector", lambda h, g=g, sg=sg: h.tensor_add(S2[sg][:, :], S2[sg][:, :], denb[g][:, 0:TC]),
                         reads=[bd], writes=[buf("S2%d" % sg)])
                    P.rw("tensor", mk_mm(ps_mm[2], wr, 0, g),
                         reads=[buf("Bx%d_0_%d" % (par, gg)) for gg in range(DT)] + [buf("w_wrT%d" % gg) for gg in range(DT)],
                         writes=[br_ps])
                    P.rw("scalar", lambda h, g=g, sg=sg: h.activation(S1[sg][:, :], ps_mm[2][:, :], AF.Exp, scale=-1.0),
                         reads=[br_ps], writes=[buf("S1%d" % sg)])
                    P.rw("vector", lambda h, g=g, sg=sg: h.scalar_tensor_tensor(
                        S2[sg][:, :], S1[sg][:, :], 1.0, S2[sg][:, :], ALU.add, ALU.mult),
                        reads=[buf("S1%d" % sg)], writes=[buf("S2%d" % sg)])
                    P.rw("scalar", lambda h, g=g, sg=sg: h.activation(S2[sg][:, :], S2[sg][:, :], AF.Ln),
                         writes=[buf("S2%d" % sg)])
                    P.rw("scalar", lambda h, g=g, sg=sg: h.activation(S2[sg][:, :], S2[sg][:, :], AF.Exp, scale=-1.0),
                         writes=[buf("S2%d" % sg)])
                    P.rw("vector", lambda h, g=g, c=c, sg=sg: h.scalar_tensor_tensor(
                        hbf[g][:, 0:TC], S3[sg][:, :], 1.0, S2[sg][:, :], ALU.mult, ALU.mult,
                        accum_out=asum[g][:, c:c + 1]),
                        reads=[buf("S3%d" % sg), buf("S2%d" % sg)],
                        writes=[buf("hbf%d" % g), buf("asum%d" % g)])
                    P.rw("scalar", lambda h, g=g, c=c: h.activation(
                        sqs[:, :], hbf[g][:, 0:TC], AF.Square, accum_out=qsum[g][:, c:c + 1]),
                        reads=[buf("hbf%d" % g)], writes=[buf("sqs"), buf("qsum%d" % g)])
                    P.dma("pl", lambda h, g=g, t0=t0: h.dma_start(
                        out=d["sc_out"][128 * g:128 * (g + 1), t0:t0 + TC], in_=hbf[g][:, 0:TC]),
                        reads=[buf("hbf%d" % g)])
                    if th:
                        for _ in range((3, 3, 3, 2, 2, 2, 2, 1)[g]):
                            if th:
                                th.pop(0)()

            for th_ in frontA_thunks(0):
                th_()
            for c in range(NCH):
                tb = frontA_thunks(c + 1) if c + 1 < NCH else []
                scanA(c, tb)
                for th_ in tb:
                    th_()
                tb.clear()
            for g in range(DT):
                P.rw("vector", lambda h, g=g: h.reduce_sum(
                    scale_sb[:, g:g + 1], asum[g][:, :], mybir.AxisListType.X),
                    reads=[buf("asum%d" % g)], writes=[buf("scale_sb")])
                P.rw("vector", lambda h, g=g: h.reduce_sum(
                    bias_sb[:, g:g + 1], qsum[g][:, :], mybir.AxisListType.X),
                    reads=[buf("qsum%d" % g)], writes=[buf("bias_sb")])
            P.rw("tensor", lambda h: h.matmul(ps_b1[0:2, 0:8],
                                              halvesf[:, :], scale_sb[:, :], start=True, stop=True),
                 reads=[buf("scale_sb"), buf("halvesf")], writes=[buf("ps_b1")])
            P.rw("tensor", lambda h: h.matmul(ps_b2[0:2, 0:8], halvesf[:, :], bias_sb[:, :], start=True, stop=True),
                 reads=[buf("bias_sb"), buf("halvesf")], writes=[buf("ps_b2")])
            if debug:
                P.rw("vector", lambda h: h.tensor_copy(fin8[0:2, 40:48], ps_b2[0:2, 0:8]),
                     reads=[buf("ps_b2")], writes=[buf("fin8")])
                P.rw("vector", lambda h: h.tensor_copy(fin8[0:2, 24:32], ps_b1[0:2, 0:8]),
                     reads=[buf("ps_b1")], writes=[buf("fin8")])
                P.dma("pl", lambda h: h.dma_start(out=d["dbg_sb"][:, 24:32], in_=bias_sb[:, :]),
                      reads=[buf("bias_sb")])
            ninv = 1.0 / (64.0 * T)
            P.rw("vector", lambda h: h.tensor_scalar(fin8[0:2, 0:8],
                                                     ps_b1[0:2, 0:8], ninv, None, ALU.mult),
                 reads=[buf("ps_b1")], writes=[buf("fin8")])
            P.rw("vector", lambda h: h.tensor_scalar(fin8[0:2, 8:16], ps_b2[0:2, 0:8], ninv, None, ALU.mult),
                 reads=[buf("ps_b2")], writes=[buf("fin8")])
            if debug:
                P.dma("pl", lambda h: h.dma_start(out=d["dbg_sb"][0:2, 48:56], in_=fin8[0:2, 8:16]),
                      reads=[buf("fin8")])
            P.rw("vector", lambda h: h.tensor_mul(fin8[0:2, 16:24], fin8[0:2, 0:8], fin8[0:2, 0:8]),
                reads=[buf("fin8")], writes=[buf("fin8")])
            P.rw("vector", lambda h: h.tensor_sub(fin8[0:2, 8:16], fin8[0:2, 8:16], fin8[0:2, 16:24]),
                 reads=[buf("fin8")], writes=[buf("fin8")])
            if debug:
                P.dma("pl", lambda h: h.dma_start(out=d["dbg_sb"][0:2, 56:64], in_=fin8[0:2, 8:16]),
                      reads=[buf("fin8")])
            P.rw("scalar", lambda h: h.activation(fin8[0:2, 8:16], fin8[0:2, 8:16], AF.Ln, bias=cpool[0:2, 1:2]),
                 reads=[buf("cpool")], writes=[buf("fin8")])
            P.rw("scalar", lambda h: h.activation(fin8[0:2, 8:16], fin8[0:2, 8:16], AF.Exp, scale=-0.5),
                 writes=[buf("fin8")])
            P.rw("tensor", lambda h: h.matmul(ps_b1[:, 0:8], halves2[:, :], fin8[0:2, 0:8], start=True, stop=True),
                 reads=[buf("fin8"), buf("halves2")], writes=[buf("ps_b1")])
            P.rw("tensor", lambda h: h.matmul(ps_b2[:, 0:8], halves2[:, :], fin8[0:2, 8:16], start=True, stop=True),
                 reads=[buf("fin8"), buf("halves2")], writes=[buf("ps_b2")])
            P.rw("vector", lambda h: h.tensor_mul(scale_sb[:, :], ps_b2[:, 0:8], smallv["gnw"][:, :]),
                 reads=[buf("ps_b2"), buf("sv_gnw")], writes=[buf("scale_sb")])
            P.rw("vector", lambda h: h.tensor_mul(fin8[:, 32:40], ps_b1[:, 0:8], scale_sb[:, :]),
                 reads=[buf("ps_b1"), buf("scale_sb")], writes=[buf("fin8")])
            P.rw("vector", lambda h: h.tensor_sub(bias_sb[:, :], smallv["gnb"][:, :], fin8[:, 32:40]),
                 reads=[buf("sv_gnb"), buf("fin8")], writes=[buf("bias_sb")])
            if debug:
                P.dma("pl", lambda h: h.dma_start(out=d["dbg_sb"][:, 0:8], in_=scale_sb[:, :]),
                      reads=[buf("scale_sb")])
                P.dma("pl", lambda h: h.dma_start(out=d["dbg_sb"][:, 8:16], in_=bias_sb[:, :]),
                      reads=[buf("bias_sb")])
                P.dma("pl", lambda h: h.dma_start(out=d["dbg_sb"][:, 16:24], in_=fin8[:, 0:8]),
                      reads=[buf("fin8")])
                P.dma("pl", lambda h: h.dma_start(out=d["dbg_sb"][:, 32:40], in_=fin8[:, 40:48]),
                      reads=[buf("fin8")])
                P.dma("pl", lambda h: h.dma_start(out=d["dbg_sb"][:, 40:48], in_=fin8[:, 24:32]),
                      reads=[buf("fin8")])


        if PH < 2:
            raise StopIteration
        P.floor = P.barrier()
        with ExitStack() as stB:
            wob = [sb(nc, stB, "wob%d" % g, [128, 1024], BF16) for g in range(DT)]
            yb = [sb(nc, stB, "yb%d" % g, [128, TC], BF16) for g in range(DT)]
            SB1 = [[sb(nc, stB, "SB1_%d_%d" % (p, g), [128, TC], F32) for g in range(DT)] for p in range(2)]
            sqB = sb(nc, stB, "sqB", [128, TC], BF16)
            x1b = sb(nc, stB, "x1b", [128, TC], BF16)
            ps_stat = ps_(nc, stB, "psB_stat", [1, TC])
            ps_stat2 = ps_(nc, stB, "psB_stat2", [1, TC])
            ps_b1 = ps_(nc, stB, "psB_b1", [128, TC])
            ps_b2 = ps_(nc, stB, "psB_b2", [128, TC])
            ps_mm = [ps_(nc, stB, "psB_mm%d" % i, [128, TC]) for i in range(3)]
            for g in range(DT):
                P.dma("pl", lambda h, g=g: h.dma_start(
                    out=wob[g][:, :], in_=d["woT"][128 * g:128 * (g + 1), :]),
                    writes=[buf("wob%d" % g)])
            def woB(c):
                t0 = c * TC
                par = c % 2
                xb = xin[par]
                S_ = SB1[par]
                for g in range(DT):
                    P.dma("sp", lambda h, g=g, t0=t0, xb=xb: h.dma_start(
                        out=xb[g][:, 1:TC + 1], in_=d["xT"][128 * g:128 * (g + 1), t0:t0 + TC]),
                        writes=[buf("xin%d_%d" % (par, g))])
                    P.dma("sp", lambda h, g=g, t0=t0: h.dma_start(
                        out=hbf[g][:, 0:TC], in_=d["sc_out"][128 * g:128 * (g + 1), t0:t0 + TC]),
                        writes=[buf("hbf%d" % g)])
                    P.rw("vector", lambda h, g=g: h.tensor_scalar(
                        yb[g][:, :], hbf[g][:, 0:TC], scale_sb[:, g:g + 1], bias_sb[:, g:g + 1],
                        ALU.mult, ALU.add),
                        reads=[buf("hbf%d" % g), buf("scale_sb"), buf("bias_sb")],
                        writes=[buf("yb%d" % g)])
                for et in range(DT):
                    pm = ps_mm[et % 2]
                    bpm = buf("psB_mm%d" % (et % 2))
                    def wo_mm(h, et=et, pm=pm):
                        ins = None
                        for gg in range(DT):
                            ins = h.matmul(pm[:, :], wob[gg][:, 128 * et:128 * (et + 1)],
                                           yb[gg][:, :], start=(gg == 0), stop=(gg == DT - 1))
                        return ins
                    P.rw("tensor", wo_mm,
                         reads=[buf("yb%d" % gg) for gg in range(DT)] + [buf("wob%d" % gg) for gg in range(DT)],
                         writes=[bpm])
                    P.rw("vector", lambda h, et=et, pm=pm, xb=xb, S_=S_: h.tensor_add(
                        S_[et][:, :], xb[et][:, 1:TC + 1], pm[:, :]),
                        reads=[buf("xin%d_%d" % (par, et)), bpm], writes=[buf("SB1_%d_%d" % (par, et))])
                    P.dma("pl", lambda h, et=et, t0=t0, S_=S_: h.dma_start(
                        out=d["sc_x1"][128 * et:128 * (et + 1), t0:t0 + TC], in_=S_[et][:, :]),
                        reads=[buf("SB1_%d_%d" % (par, et))])
            def lnB(c):
                t0 = c * TC
                par = c % 2
                S_ = SB1[par]
                for g in range(DT):
                    P.rw("scalar", lambda h, g=g, S_=S_: h.activation(sqB[:, :], S_[g][:, :], AF.Square),
                         reads=[buf("SB1_%d_%d" % (par, g))], writes=[buf("sqB")])
                    P.rw("tensor", lambda h, g=g, S_=S_: h.matmul(
                        ps_stat[:, :], onesf[:, :], S_[g][:, :], start=(g == 0), stop=(g == DT - 1)),
                        reads=[buf("SB1_%d_%d" % (par, g)), buf("onesf")], writes=[buf("psB_stat")])
                    P.rw("tensor", lambda h, g=g: h.matmul(
                        ps_stat2[:, :], onesb1[:, :], sqB[:, :], start=(g == 0), stop=(g == DT - 1)),
                        reads=[buf("sqB"), buf("onesb1")], writes=[buf("psB_stat2")])
                P.rw("vector", lambda h: h.tensor_scalar(stt_m[:, :], ps_stat[:, :], 1.0 / 1024, None, ALU.mult),
                     reads=[buf("psB_stat")], writes=[buf("stt_m")])
                P.rw("vector", lambda h: h.tensor_scalar(stt_i[:, :], ps_stat2[:, :], 1.0 / 1024, None, ALU.mult),
                     reads=[buf("psB_stat2")], writes=[buf("stt_i")])
                P.rw("vector", lambda h: h.tensor_mul(sqB[0:1, 0:TC], stt_m[:, :], stt_m[:, :]),
                     reads=[buf("stt_m")], writes=[buf("sqB")])
                P.rw("vector", lambda h: h.tensor_sub(stt_i[:, :], stt_i[:, :], sqB[0:1, 0:TC]),
                     reads=[buf("sqB")], writes=[buf("stt_i")])
                P.rw("scalar", lambda h: h.activation(stt_i[:, :], stt_i[:, :], AF.Ln, bias=cpool[0:1, 1:2]),
                     reads=[buf("cpool")], writes=[buf("stt_i")])
                P.rw("scalar", lambda h: h.activation(stt_i[:, :], stt_i[:, :], AF.Exp, scale=-0.5),
                     writes=[buf("stt_i")])
                P.rw("tensor", lambda h: h.matmul(ps_b1[:, :], bonef[:, :], stt_m[:, :], start=True, stop=True),
                     reads=[buf("stt_m"), buf("bonef")], writes=[buf("psB_b1")])
                P.rw("tensor", lambda h: h.matmul(ps_b2[:, :], bonef[:, :], stt_i[:, :], start=True, stop=True),
                     reads=[buf("stt_i"), buf("bonef")], writes=[buf("psB_b2")])
                for g in range(DT):
                    P.rw("vector", lambda h, g=g, S_=S_: h.tensor_sub(S_[g][:, :], S_[g][:, :], ps_b1[:, :]),
                         reads=[buf("psB_b1")], writes=[buf("SB1_%d_%d" % (par, g))])
                    P.rw("vector", lambda h, g=g, S_=S_: h.tensor_mul(S_[g][:, :], S_[g][:, :], ps_b2[:, :]),
                         reads=[buf("psB_b2")], writes=[buf("SB1_%d_%d" % (par, g))])
                    P.dma("pl", lambda h, g=g, t0=t0, S_=S_: h.dma_start(
                        out=d["sc_xf0"][128 * g:128 * (g + 1), t0:t0 + TC], in_=S_[g][:, :]),
                        reads=[buf("SB1_%d_%d" % (par, g))])
            woB(0)
            for c in range(NCH):
                if c + 1 < NCH:
                    woB(c + 1)
                lnB(c)
        if PH < 3:
            raise StopIteration
        for l in range(2):
            src_xf = "sc_xf0" if l == 0 else "sc_xf1"
            P.floor = P.barrier()
            if PH < 3 + 2 * l:
                raise StopIteration
            with ExitStack() as sA:
                wqb = [sb(nc, sA, "wqb%d_%d" % (l, g), [128, 1536], BF16) for g in range(DT)]
                wlb = [sb(nc, sA, "wlb%d_%d" % (l, g), [128, 1024], BF16) for g in range(4)]
                qw = [[sb(nc, sA, "qw%d_%d_%d" % (l, p, g), [128, TC], BF16) for g in range(4)] for p in range(2)]
                kw = [[sb(nc, sA, "kw%d_%d_%d" % (l, p, g), [128, TC], BF16) for g in range(4)] for p in range(3)]
                vt = [sb(nc, sA, "vt%d_%d" % (l, p), [128, 2080], BF16) for p in range(3)]
                atT = [sb(nc, sA, "atT%d_%d" % (l, p), [128, 4096], BF16) for p in range(2)]
                oT = [sb(nc, sA, "oT%d_%d" % (l, g), [128, TC], BF16) for g in range(4)]
                vraw = [sb(nc, sA, "vraw%d_%d" % (l, g), [128, TC], F32) for g in range(4)]
                qraw = [sb(nc, sA, "qraw%d_%d" % (l, g), [128, TC], BF16) for g in range(4)]
                kraw = [sb(nc, sA, "kraw%d_%d" % (l, g), [128, TC], BF16) for g in range(4)]
                tbl = [[sb(nc, sA, "tbl%d_%d_%d" % (l, p, i), [128, TC], BF16) for i in range(4)] for p in range(2)]
                rtmp = [sb(nc, sA, "rtmp%d_%d" % (l, i), [128, TC], BF16) for i in range(2)]
                SM = [sb(nc, sA, "SM%d_%d" % (l, g), [128, TC + 1], F32) for g in range(DT)]
                statm = sb(nc, sA, "statm%d" % l, [1, TC], F32)
                stati = sb(nc, sA, "stati%d" % l, [1, TC], F32)
                sqA = sb(nc, sA, "sqA%d" % l, [128, TC], F32)
                sqAb = sb(nc, sA, "sqAb%d" % l, [128, TC], BF16)
                nrm = sb(nc, sA, "nrm%d" % l, [8, 2 * TC], F32)
                nrmb = sb(nc, sA, "nrmb%d" % l, [8, 2 * TC], BF16)
                sums0 = sb(nc, sA, "sums0%d" % l, [1, 8 * TC], F32)
                sums = sb(nc, sA, "sums%d" % l, [8, TC], F32)
                sumsb = sb(nc, sA, "sumsb%d" % l, [8, TC], BF16)
                identd = sb(nc, sA, "identd%d" % l, [128, 64], F32)
                qs8 = sb(nc, sA, "qs8_%d" % l, [128, 1], F32)
                ks1 = sb(nc, sA, "ks1_%d" % l, [128, 1], F32)
                ps_sim = [ps_(nc, sA, "psA_sim%d_%d" % (l, i), [128, 1024]) for i in range(2)]
                ps_mmA = [ps_(nc, sA, "psA_mm%d_%d" % (l, i), [128, TC]) for i in range(2)]
                ps_on = ps_(nc, sA, "psA_on%d" % l, [128, 1024])

                P.dma("sp", lambda h: h.dma_start(out=identd[:, :], in_=d["identd"]),
                      writes=[buf("identd%d" % l)])
                P.dma("sp", lambda h: h.dma_start(out=qs8[:, :], in_=d["qs%d" % l]), writes=[buf("qs8_%d" % l)])
                P.dma("sp", lambda h: h.dma_start(out=ks1[:, :], in_=d["ks%d" % l]), writes=[buf("ks1_%d" % l)])
                P.rw("vector", lambda h: h.tensor_scalar(qs8[:, :], qs8[:, :], 8.0, None, ALU.mult),
                     writes=[buf("qs8_%d" % l)])
                for g in range(DT):
                    P.dma("pl", lambda h, g=g: h.dma_start(
                        out=wqb[g][:, :], in_=d["wqkv%d" % l][128 * g:128 * (g + 1), :]),
                        writes=[buf("wqb%d" % g)])
                for g in range(4):
                    P.dma("pl", lambda h, g=g: h.dma_start(
                        out=wlb[g][:, :], in_=d["wol%d" % l][128 * g:128 * (g + 1), :]),
                        writes=[buf("wlb%d" % g)])
                for p in range(2):
                    for jc in range(8):
                        for ic in range(4):
                            if ic <= jc <= ic + 4:
                                continue
                            P.rw("gpsimd", lambda h, p=p, jc=jc, ic=ic: h.memset(
                                atT[p][:, jc * 512 + ic * 128: jc * 512 + (ic + 1) * 128], 0.0),
                                writes=[buf("atT%d" % p)])
                for p in range(3):
                    P.rw("gpsimd", lambda h, p=p: h.memset(
                        bass.AP(vt[p], 64, [[vt[p].ap().tensor.shape[1], 128], [65, 32], [1, 1]]), 1.0),
                        writes=[buf("vt%d" % p)])

                def produce_a(c):
                    t0 = c * TC
                    par = c % 2
                    xb = xin[par]
                    for g in range(DT):
                        if c == 0:
                            P.rw("gpsimd", lambda h, g=g, xb=xb: h.memset(xb[g][:, 0:1], 0.0),
                                 writes=[buf("xin%d_%d" % (par, g))])
                            P.dma("sp", lambda h, g=g, xb=xb: h.dma_start(
                                out=xb[g][:, 1:TC + 1], in_=d[src_xf][128 * g:128 * (g + 1), 0:TC]),
                                writes=[buf("xin%d_%d" % (par, g))])
                        else:
                            P.dma("sp", lambda h, g=g, t0=t0, xb=xb: h.dma_start(
                                out=xb[g][:, :], in_=d[src_xf][128 * g:128 * (g + 1), t0 - 1:t0 + TC]),
                                writes=[buf("xin%d_%d" % (par, g))])
                    for i, tn in enumerate(("c1", "s1", "c2", "s2")):
                        P.dma("pl", lambda h, i=i, tn=tn, t0=t0, par=par: h.dma_start(
                            out=tbl[par][i][:, :], in_=d[tn][:, t0:t0 + TC]),
                            writes=[buf("tbl%d_%d" % (par, i))])
                    for g in range(DT):
                        sl = (1, TC + 1) if g < 4 else (0, TC)
                        P.rw("vector", lambda h, g=g, sl=sl, xb=xb: h.tensor_copy(
                            hbf[g][:, 0:TC], xb[g][:, sl[0]:sl[1]]),
                            reads=[buf("xin%d_%d" % (par, g))], writes=[buf("hbf%d" % g)])
                    for et in range(12):
                        pm = ps_mmA[et % 2]
                        bpm = buf("psA_mm%d" % (et % 2))
                        def qkv_mm(h, et=et, pm=pm):
                            ins = None
                            for gg in range(DT):
                                ins = h.matmul(pm[:, :], wqb[gg][:, 128 * et:128 * (et + 1)],
                                               hbf[gg][:, 0:TC], start=(gg == 0), stop=(gg == DT - 1))
                            return ins
                        P.rw("tensor", qkv_mm,
                             reads=[buf("hbf%d" % gg) for gg in range(DT)] + [buf("wqb%d" % gg) for gg in range(DT)],
                             writes=[bpm])

                        if et < 4:
                            P.rw("scalar", lambda h, pm=pm: h.activation(sqAb[:, :], pm[:, :], AF.Square),
                                 reads=[bpm], writes=[buf("sqA")])
                            P.rw("tensor", lambda h, et=et: h.matmul(
                                ps_on[0:8, 0:TC], og4b[:, 8 * et:8 * (et + 1)], sqAb[:, :],
                                start=(et == 0), stop=(et == 3)),
                                reads=[buf("sqA"), buf("og4b")], writes=[buf("ps_on1")])
                            P.rw("vector", lambda h, et=et, pm=pm: h.tensor_copy(qraw[et][:, :], pm[:, :]),
                                 reads=[bpm], writes=[buf("qraw%d" % et)])
                        elif et < 8:
                            P.rw("scalar", lambda h, pm=pm: h.activation(sqAb[:, :], pm[:, :], AF.Square),
                                 reads=[bpm], writes=[buf("sqA")])
                            P.rw("tensor", lambda h, et=et: h.matmul(
                                ps_on[0:8, TC:2 * TC], og4b[:, 8 * (et - 4):8 * (et - 3)], sqAb[:, :],
                                start=(et == 4), stop=(et == 7)),
                                reads=[buf("sqA"), buf("og4b")], writes=[buf("ps_on2")])
                            P.rw("vector", lambda h, et=et, pm=pm: h.tensor_copy(kraw[et - 4][:, :], pm[:, :]),
                                 reads=[bpm], writes=[buf("kraw%d" % (et - 4))])
                        else:
                            P.rw("scalar", lambda h, et=et, pm=pm: h.copy(vraw[et - 8][:, :], pm[:, :]),
                                 reads=[bpm], writes=[buf("vraw%d" % (et - 8))])
                    for coff, bn in ((0, "ps_on1"), (TC, "ps_on2")):
                        P.rw("scalar", lambda h, coff=coff: h.activation(
                            nrm[0:8, coff:coff + TC], ps_on[0:8, coff:coff + TC], AF.Ln),
                            reads=[buf(bn)], writes=[buf("nrm")])
                        P.rw("scalar", lambda h, coff=coff: h.activation(
                            nrm[0:8, coff:coff + TC], nrm[0:8, coff:coff + TC], AF.Exp, scale=-0.5),
                            writes=[buf("nrm")])
                        P.rw("vector", lambda h, coff=coff: h.tensor_scalar_min(
                            nrm[0:8, coff:coff + TC], nrm[0:8, coff:coff + TC], 1e12),
                            writes=[buf("nrm")])
                        P.rw("vector", lambda h, coff=coff: h.tensor_copy(
                            nrmb[0:8, coff:coff + TC], nrm[0:8, coff:coff + TC]),
                            reads=[buf("nrm")], writes=[buf("nrmb")])
                def produce_b(c):
                    t0 = c * TC
                    par = c % 2
                    par3 = c % 3
                    xb = xin[par]
                    for et in range(4):
                        for (raw, wbufs, scv, scb, tci, tsi, nbase) in (
                            (qraw, qw, qs8, "qs8_%d" % l, 0, 1, 0),
                            (kraw, kw, ks1, "ks1_%d" % l, 2, 3, TC),
                        ):
                            pm = ps_mmA[0]
                            bpm = buf("psA_mm0")
                            P.rw("tensor", lambda h, et=et, nbase=nbase, pm=pm: h.matmul(
                                pm[:, :], bg4[:, 128 * et:128 * (et + 1)], nrmb[0:8, nbase:nbase + TC],
                                start=True, stop=True),
                                reads=[buf("nrmb"), buf("bg4")], writes=[bpm])
                            P.rw("vector", lambda h, et=et, raw=raw, scv=scv, pm=pm: h.scalar_tensor_tensor(
                                raw[et][:, :], raw[et][:, :], scv[:, :], pm[:, :], ALU.mult, ALU.mult),
                                reads=[buf(scb), bpm],
                                writes=[buf("%s%d" % ("qraw" if raw is qraw else "kraw", et))])
                            pm2 = ps_mmA[1]
                            bpm2 = buf("psA_mm1")
                            P.rw("tensor", lambda h, et=et, raw=raw, pm2=pm2: h.matmul(
                                pm2[:, :], perm[:, :], raw[et][:, :], start=True, stop=True),
                                reads=[buf("%s%d" % ("qraw" if raw is qraw else "kraw", et)), buf("perm")],
                                writes=[bpm2])
                            P.rw("gpsimd", lambda h, et=et, raw=raw, tci=tci, par=par: h.tensor_mul(
                                rtmp[0][:, :], raw[et][:, :], tbl[par][tci][:, :]),
                                reads=[buf("%s%d" % ("qraw" if raw is qraw else "kraw", et)),
                                       buf("tbl%d_%d" % (par, tci))],
                                writes=[buf("rtmp0")])
                            P.rw("vector", lambda h, tsi=tsi, par=par, pm2=pm2: h.tensor_mul(
                                rtmp[1][:, :], pm2[:, :], tbl[par][tsi][:, :]),
                                reads=[bpm2, buf("tbl%d_%d" % (par, tsi))], writes=[buf("rtmp1")])
                            wi = par if wbufs is qw else par3
                            P.rw("gpsimd", lambda h, et=et, wbufs=wbufs, wi=wi: h.tensor_add(
                                wbufs[wi][et][:, :], rtmp[0][:, :], rtmp[1][:, :]),
                                reads=[buf("rtmp0"), buf("rtmp1")],
                                writes=[buf("%sw%d_%d" % ("q" if wbufs is qw else "k", wi, et))])
                    for hh in range(8):
                        vcol = (hh % 2) * TC
                        vbuf = "ps_on%d" % (1 + hh % 2)
                        def tr_f(h, hh=hh, par=par, vcol=vcol):
                            ins = None
                            for sub in range(4):
                                ins = h.transpose(ps_on[:, vcol + 64 * sub:vcol + 64 * (sub + 1)],
                                                  vraw[hh // 2][64 * (hh % 2):64 * (hh % 2) + 64,
                                                                128 * sub:128 * (sub + 1)],
                                                  identd[64 * (hh % 2):64 * (hh % 2) + 64, :])
                            return ins
                        P.rw("tensor", tr_f,
                             reads=[buf("vraw%d" % (hh // 2)), buf("identd%d" % l)],
                             writes=[buf(vbuf)])
                        P.rw("vector", lambda h, hh=hh, par3=par3, vcol=vcol: h.tensor_copy(
                            bass.AP(vt[par3], hh * 260, [[vt[par3].ap().tensor.shape[1], 128], [65, 4], [1, 64]]),
                            ps_on[:, vcol:vcol + 256]),
                            reads=[buf(vbuf)], writes=[buf("vt%d" % par3)])
                def consume_heads(c, hrng):
                    t0 = c * TC
                    par = c % 2
                    par3 = c % 3
                    pv3 = (c - 1) % 3
                    xb = xin[par]
                    for hh in hrng:
                        tile = hh // 2
                        rbase = 64 * (hh % 2)
                        for pi, pair in enumerate(((0, 1), (2, 3))):
                            plo, phi = pair
                            jlo = max(4, plo) if c == 0 else plo
                            plan_mm = []
                            for jc in range(jlo, phi + 4 + 1):
                                for ic in pair:
                                    if ic <= jc <= ic + 4 and not (c == 0 and jc < 4):
                                        plan_mm.append(("qk", jc, ic, jc - ic))
                            for ic in pair:
                                if c > 0:
                                    plan_mm.append(("ml", ic, ic, 0))
                                plan_mm.append(("mr", ic + 4, ic, 4))
                            seen = set()
                            flags = []
                            for (kind, jc, ic, r) in plan_mm:
                                bk = (ic, 0 if r < 4 else 1)
                                flags.append(bk not in seen)
                                seen.add(bk)
                            last_of = {}
                            for idx, (kind, jc, ic, r) in enumerate(plan_mm):
                                last_of[(ic, 0 if r < 4 else 1)] = idx
                            lastf = [idx in last_of.values() for idx in range(len(plan_mm))]
                            def qk_f(h, plan_mm=plan_mm, flags=flags, lastf=lastf,
                                     tile=tile, rbase=rbase, par=par, par3=par3, pv3=pv3):
                                ins = None
                                for idx, (kind, jc, ic, r) in enumerate(plan_mm):
                                    if kind == "qk":
                                        kwsrc = kw[par3] if jc >= 4 else kw[pv3]
                                        kcol = (jc % 4) * 128
                                        ins = h.matmul(
                                            ps_sim[ic % 2][:, r * 128:(r + 1) * 128],
                                            kwsrc[tile][rbase:rbase + 64, kcol:kcol + 128],
                                            qw[par][tile][rbase:rbase + 64, ic * 128:(ic + 1) * 128],
                                            start=flags[idx], stop=lastf[idx])
                                    else:
                                        msk = maskl if kind == "ml" else maskr
                                        ins = h.matmul(
                                            ps_sim[ic % 2][:, r * 128:(r + 1) * 128],
                                            ident[:, :], msk[:, :],
                                            start=flags[idx], stop=lastf[idx])
                                return ins
                            reads_qk = [buf("qw%d_%d" % (par, tile)), buf("maskl"), buf("maskr"), buf("ident")]
                            if jlo < 4:
                                reads_qk.append(buf("kw%d_%d" % (pv3, tile)))
                            if phi + 4 >= 4:
                                reads_qk.append(buf("kw%d_%d" % (par3, tile)))
                            P.rw("tensor", qk_f, reads=reads_qk,
                                 writes=[buf("psA_sim0"), buf("psA_sim1")])
                            for ic in pair:
                                jlo_ic = max(max(4, ic) if c == 0 else ic, ic)
                                r0 = jlo_ic - ic
                                nr = 5 - r0
                                def exp_f(h, ic=ic, r0=r0, nr=nr, par=par):
                                    return h.activation(
                                        bass.AP(atT[par], (ic + r0) * 512 + ic * 128,
                                                [[atT[par].ap().tensor.shape[1], 128], [512, nr], [1, 128]]),
                                        ps_sim[ic % 2][:, r0 * 128:640],
                                        AF.Exp, bias=cpool[:, 0:1])
                                P.rw("scalar", exp_f,
                                     reads=[buf("psA_sim%d" % (ic % 2)), buf("cpool")],
                                     writes=[buf("atT%d_%d" % (par, ic))])
                        jset = range(4, 8) if c == 0 else range(8)
                        avcol = (hh % 2) * TC
                        avbuf = "ps_on%d" % (1 + hh % 2)
                        def av_f(h, hh=hh, par=par, par3=par3, pv3=pv3, jset=tuple(jset), avcol=avcol):
                            ins = None
                            for n_i, jc in enumerate(jset):
                                vsrc = vt[par3] if jc >= 4 else vt[pv3]
                                col = hh * 260 + (jc % 4) * 65
                                lo = max(0, jc - 4)
                                hi = min(3, jc)
                                ins = h.matmul(ps_on[0:65, avcol + lo * 128:avcol + (hi + 1) * 128],
                                               vsrc[:, col:col + 65],
                                               atT[par][:, jc * 512 + lo * 128:jc * 512 + (hi + 1) * 128],
                                               start=(n_i == 0), stop=(n_i == len(jset) - 1))
                            return ins
                        reads_av = [buf("atT%d_%d" % (par, ic)) for ic in range(4)]
                        reads_av.append(buf("atT%d" % par))
                        reads_av.append(buf("vt%d" % par3))
                        if c > 0:
                            reads_av.append(buf("vt%d" % pv3))
                        P.rw("tensor", av_f, reads=reads_av, writes=[buf(avbuf)])
                        P.rw("vector", lambda h, hh=hh, avcol=avcol: h.tensor_copy(
                            oT[hh // 2][64 * (hh % 2):64 * (hh % 2) + 64, :], ps_on[0:64, avcol:avcol + TC]),
                            reads=[buf(avbuf)], writes=[buf("oT%d" % (hh // 2))])
                        P.rw("vector", lambda h, hh=hh, avcol=avcol: h.tensor_copy(
                            sums0[0:1, hh * TC:(hh + 1) * TC], ps_on[64:65, avcol:avcol + TC]),
                            reads=[buf(avbuf)], writes=[buf("sums0")])
                def consume_tail(c):
                    t0 = c * TC
                    par = c % 2
                    xb = xin[par]
                    P.dma("pl", lambda h: h.dma_start(out=d["sc_sums"], in_=sums0[0:1, :]),
                          reads=[buf("sums0")], writes=[buf("sc_sums")])
                    P.dma("pl", lambda h: h.dma_start(out=sums[:, :], in_=d["sc_sums"].rearrange("(h t) -> h t", h=8)),
                          reads=[buf("sc_sums")], writes=[buf("sums")])
                    P.rw("scalar", lambda h: h.activation(sums[:, :], sums[:, :], AF.Ln),
                         writes=[buf("sums")])
                    P.rw("scalar", lambda h: h.activation(sumsb[:, :], sums[:, :], AF.Exp, scale=-1.0),
                         reads=[buf("sums")], writes=[buf("sumsb")])
                    for g in range(4):
                        pm = ps_mmA[g % 2]
                        bpm = buf("psA_mm%d" % (g % 2))
                        P.rw("tensor", lambda h, g=g, pm=pm: h.matmul(
                            pm[:, :], bg4[:, 128 * g:128 * (g + 1)], sumsb[:, :], start=True, stop=True),
                            reads=[buf("sumsb"), buf("bg4")], writes=[bpm])
                        P.rw("vector", lambda h, g=g, pm=pm: h.tensor_mul(oT[g][:, :], oT[g][:, :], pm[:, :]),
                             reads=[bpm], writes=[buf("oT%d" % g)])
                    for et in range(DT):
                        if c == 0:
                            P.rw("gpsimd", lambda h, et=et: h.memset(SM[et][:, 0:1], 0.0),
                                 writes=[buf("SM%d" % et)])
                        else:
                            P.rw("vector", lambda h, et=et: h.tensor_copy(SM[et][:, 0:1], SM[et][:, TC:TC + 1]),
                                 writes=[buf("SM%d" % et)])
                        pm = ps_mmA[et % 2]
                        bpm = buf("psA_mm%d" % (et % 2))
                        def wo_mm(h, et=et, pm=pm):
                            ins = None
                            for gg in range(4):
                                ins = h.matmul(pm[:, :], wlb[gg][:, 128 * et:128 * (et + 1)],
                                               oT[gg][:, :], start=(gg == 0), stop=(gg == 3))
                            return ins
                        P.rw("tensor", wo_mm,
                             reads=[buf("oT%d" % gg) for gg in range(4)] + [buf("wlb%d" % gg) for gg in range(4)],
                             writes=[bpm])
                        P.rw("vector", lambda h, et=et, pm=pm, xb=xb: h.tensor_add(
                            SM[et][:, 1:TC + 1], xb[et][:, 1:TC + 1], pm[:, :]),
                            reads=[buf("xin%d_%d" % (par, et)), bpm], writes=[buf("SM%d" % et)])
                        P.dma("pl", lambda h, et=et, t0=t0: h.dma_start(
                            out=d["sc_xm"][128 * et:128 * (et + 1), t0:t0 + TC], in_=SM[et][:, 1:TC + 1]),
                            reads=[buf("SM%d" % et)])
                    for et in range(DT):
                        sl = (1, TC + 1) if et < 4 else (0, TC)
                        P.rw("scalar", lambda h, et=et, sl=sl: h.activation(
                            sqAb[:, :], SM[et][:, sl[0]:sl[1]], AF.Square),
                            reads=[buf("SM%d" % et)], writes=[buf("sqA")])
                        P.rw("tensor", lambda h, et=et, sl=sl: h.matmul(
                            ps_sim[0][0:1, 0:TC], onesf[:, :], SM[et][:, sl[0]:sl[1]],
                            start=(et == 0), stop=(et == DT - 1)),
                            reads=[buf("SM%d" % et), buf("onesf")], writes=[buf("psA_sim0")])
                        P.rw("tensor", lambda h, et=et: h.matmul(
                            ps_sim[0][0:1, TC:2 * TC], onesb1[:, :], sqAb[:, :],
                            start=(et == 0), stop=(et == DT - 1)),
                            reads=[buf("sqA"), buf("onesb1")], writes=[buf("psA_sim0")])
                    P.rw("vector", lambda h: h.tensor_scalar(statm[:, :], ps_sim[0][0:1, 0:TC], 1.0 / 1024, None, ALU.mult),
                         reads=[buf("psA_sim0")], writes=[buf("statm")])
                    P.rw("vector", lambda h: h.tensor_scalar(stati[:, :], ps_sim[0][0:1, TC:2 * TC], 1.0 / 1024, None, ALU.mult),
                         reads=[buf("psA_sim0")], writes=[buf("stati")])
                    P.rw("vector", lambda h: h.tensor_mul(sqA[0:1, 0:TC], statm[:, :], statm[:, :]),
                         reads=[buf("statm")], writes=[buf("sqA")])
                    P.rw("vector", lambda h: h.tensor_sub(stati[:, :], stati[:, :], sqA[0:1, 0:TC]),
                         reads=[buf("sqA")], writes=[buf("stati")])
                    P.rw("scalar", lambda h: h.activation(stati[:, :], stati[:, :], AF.Ln, bias=cpool[0:1, 1:2]),
                         reads=[buf("cpool")], writes=[buf("stati")])
                    P.rw("scalar", lambda h: h.activation(stati[:, :], stati[:, :], AF.Exp, scale=-0.5),
                         writes=[buf("stati")])
                    P.dma("pl", lambda h, t0=t0: h.dma_start(
                        out=d["sc_stats"][0:1, t0:t0 + TC], in_=statm[:, :]),
                        reads=[buf("statm")])
                    P.dma("pl", lambda h, t0=t0: h.dma_start(
                        out=d["sc_stats"][1:2, t0:t0 + TC], in_=stati[:, :]),
                        reads=[buf("stati")])
                produce_a(0)
                produce_b(0)
                for c in range(NCH):
                    if c + 1 < NCH:
                        produce_a(c + 1)
                    consume_heads(c, range(0, 4))
                    if c + 1 < NCH:
                        produce_b(c + 1)
                    consume_heads(c, range(4, 8))
                    consume_tail(c)

            if PH < 4 + 2 * l:
                raise StopIteration
            P.floor = P.barrier()
            TC2 = 256
            NJ = 22
            JW = [128] * 21 + [42]
            with ExitStack() as sM:
                w1b = [sb(nc, sM, "w1b%d_%d" % (l, g), [128, 5460], BF16) for g in range(DT)]
                w2b = [sb(nc, sM, "w2b%d_%d" % (l, j), [JW[j], 1024], BF16) for j in range(NJ)]
                smallv2 = {}
                for v in ("lg", "lb", "b2"):
                    smallv2[v] = sb(nc, sM, "sv2_%s_%d" % (v, l), [128, 8], F32)
                    P.dma("sp", lambda h, v=v: h.dma_start(
                        out=smallv2[v][:, :], in_=d["%sl%d" % (v, l) if v != "b2" else "b2l%d" % l]
                        if False else d[{"lg": "lg%d", "lb": "lb%d", "b2": "b2l%d"}[v] % l]),
                        writes=[buf("sv2_" + v)])
                b1a = sb(nc, sM, "b1a_%d" % l, [128, NJ], F32)
                b1g = sb(nc, sM, "b1g_%d" % l, [128, NJ], F32)
                fsc = [sb(nc, sM, "fsc%d_%d" % (l, i), [128, TC2], BF16) for i in range(3)]
                glx = [sb(nc, sM, "glx_%d_%d" % (l, i), [128, TC2], BF16) for i in range(3)]
                sqM = sb(nc, sM, "sqM_%d" % l, [128, TC2], F32)
                rowm = [sb(nc, sM, "rowm_%d_%d" % (l, i), [1, 2 * TC2], F32) for i in range(2)]
                xz = [sb(nc, sM, "xz%d_%d" % (l, g), [128, TC2], F32) for g in range(4)]
                xz2 = [[sb(nc, sM, "xz2%d_%d_%d" % (l, p, g), [128, TC2], F32) for g in range(DT)] for p in range(2)] if l == 1 else None
                ps_acc = [ps_(nc, sM, "psM_acc%d_%d" % (l, i), [128, 512]) for i in range(4)]
                ps_ag = [ps_(nc, sM, "psM_ag%d_%d" % (l, i), [128, 512]) for i in range(2)]
                ps_aux = ps_(nc, sM, "psM_aux%d" % l, [128, 1024])
                for g in range(DT):
                    P.dma("pl", lambda h, g=g: h.dma_start(
                        out=w1b[g][:, :], in_=d["w1l%d" % l][128 * g:128 * (g + 1), :]),
                        writes=[buf("w1b%d" % g)])
                for j in range(NJ):
                    P.dma("pl", lambda h, j=j: h.dma_start(
                        out=w2b[j][:, :], in_=d["w2l%d" % l][128 * j:128 * j + JW[j], :]),
                        writes=[buf("w2b%d" % j)])
                P.dma("sp", lambda h: h.dma_start(out=b1a[:, :], in_=d["b1a%d" % l]), writes=[buf("b1a")])
                P.dma("sp", lambda h: h.dma_start(out=b1g[:, :], in_=d["b1g%d" % l]), writes=[buf("b1g")])
                NCH2 = T // TC2
                def prep(c2):
                    t0 = c2 * TC2
                    par = c2 % 2
                    xb = xin[par]
                    co = par * TC2
                    for g in range(DT):
                        if c2 == 0:
                            P.rw("gpsimd", lambda h, g=g, xb=xb: h.memset(xb[g][:, 0:1], 0.0),
                                 writes=[buf("xin%d_%d" % (par, g))])
                            P.dma("sp", lambda h, g=g, xb=xb: h.dma_start(
                                out=xb[g][:, 1:TC2 + 1], in_=d["sc_xm"][128 * g:128 * (g + 1), 0:TC2]),
                                writes=[buf("xin%d_%d" % (par, g))])
                        else:
                            P.dma("sp", lambda h, g=g, t0=t0, xb=xb: h.dma_start(
                                out=xb[g][:, 0:TC2 + 1], in_=d["sc_xm"][128 * g:128 * (g + 1), t0 - 1:t0 + TC2]),
                                writes=[buf("xin%d_%d" % (par, g))])
                        if l == 1:
                            P.dma("sp", lambda h, g=g, t0=t0, par=par: h.dma_start(
                                out=xz2[par][g][:, :], in_=d["sc_x1"][128 * g:128 * (g + 1), t0:t0 + TC2]),
                                writes=[buf("xz2_%d_%d" % (par, g))])
                    P.dma("sp", lambda h, t0=t0, par=par: h.dma_start(
                        out=rowm[par][:, :], in_=d["sc_stats"][:, t0:t0 + TC2]),
                        writes=[buf("rowm%d" % par)])
                    aco = par * 2 * TC2
                    P.rw("tensor", lambda h, par=par, aco=aco: h.matmul(
                        ps_aux[:, aco:aco + TC2], bonef[:, :], rowm[par][0:1, 0:TC2], start=True, stop=False),
                        reads=[buf("rowm%d" % par), buf("bonef")], writes=[buf("psM_aux%d" % par)])
                    P.rw("tensor", lambda h, par=par, aco=aco: h.matmul(
                        ps_aux[:, aco + TC2:aco + 2 * TC2], bonef[:, :], rowm[par][0:1, TC2:2 * TC2], start=False, stop=True),
                        reads=[buf("rowm%d" % par), buf("bonef")], writes=[buf("psM_aux%d" % par)])
                    for g in range(DT):
                        sl = (1, TC2 + 1) if g < 4 else (0, TC2)
                        gx = glx[g % 2]
                        gbuf = buf("glx%d" % (g % 2))
                        P.rw("vector", lambda h, g=g, sl=sl, xb=xb, gx=gx, aco=aco, par=par: h.tensor_sub(
                            gx[:, :], xb[g][:, sl[0]:sl[1]], ps_aux[:, aco:aco + TC2]),
                            reads=[buf("xin%d_%d" % (par, g)), buf("psM_aux%d" % par)], writes=[gbuf])
                        P.rw("vector", lambda h, g=g, gx=gx, aco=aco, par=par: h.tensor_mul(
                            gx[:, :], gx[:, :], ps_aux[:, aco + TC2:aco + 2 * TC2]),
                            reads=[buf("psM_aux%d" % par)], writes=[gbuf])
                        P.rw("vector", lambda h, g=g, gx=gx, co=co, par=par: h.tensor_scalar(
                            hbf[g][:, co:co + TC2], gx[:, :], smallv2["lg"][:, g:g + 1], smallv2["lb"][:, g:g + 1],
                            ALU.mult, ALU.add),
                            reads=[gbuf, buf("sv2_lg"), buf("sv2_lb")], writes=[buf("hbf_%d_%d" % (par, g))])
                def emit_ag(c2, j):
                    par = c2 % 2
                    co = par * TC2
                    wj = JW[j]
                    pag = ps_ag[j % 2]
                    pagbuf = buf("psM_ag%d" % (j % 2))
                    def ag_mm(h, j=j, wj=wj, pag=pag, co=co):
                        ins = None
                        for gg in range(DT):
                            ins = h.matmul(pag[0:wj, 0:TC2], w1b[gg][:, 128 * j:128 * j + wj],
                                           hbf[gg][:, co:co + TC2], start=(gg == 0), stop=False)
                        for gg in range(DT):
                            ins = h.matmul(pag[0:wj, TC2:2 * TC2], w1b[gg][:, 2730 + 128 * j:2730 + 128 * j + wj],
                                           hbf[gg][:, co:co + TC2], start=False, stop=(gg == DT - 1))
                        return ins
                    P.rw("tensor", ag_mm,
                         reads=[buf("hbf_%d_%d" % (par, gg)) for gg in range(DT)] + [buf("w1b%d" % gg) for gg in range(DT)],
                         writes=[pagbuf])
                    fb_ = fsc[j % 3]
                    bfb = buf("fsc%d" % (j % 3))
                    gx = glx[j % 3]
                    gbuf = buf("glx%d" % (j % 3))
                    P.rw("scalar", lambda h, j=j, wj=wj, pag=pag, gx=gx: h.activation(
                        gx[0:wj, :], pag[0:wj, TC2:2 * TC2], AF.Gelu, bias=b1g[0:wj, j:j + 1]),
                        reads=[pagbuf, buf("b1g")], writes=[gbuf])
                    P.rw("vector", lambda h, j=j, wj=wj, fb_=fb_, pag=pag, gx=gx: h.scalar_tensor_tensor(
                        fb_[0:wj, :], pag[0:wj, 0:TC2], b1a[0:wj, j:j + 1], gx[0:wj, :],
                        ALU.add, ALU.mult),
                        reads=[pagbuf, buf("b1a"), gbuf], writes=[bfb])
                def emit_w2(j):
                    wj = JW[j]
                    fb_ = fsc[j % 3]
                    bfb = buf("fsc%d" % (j % 3))
                    def w2_mm(h, j=j, wj=wj, fb_=fb_):
                        ins = None
                        for et in range(DT):
                            ins = h.matmul(ps_acc[et // 2][:, (et % 2) * TC2:(et % 2) * TC2 + TC2],
                                           w2b[j][0:wj, 128 * et:128 * (et + 1)], fb_[0:wj, :],
                                           start=(j == 0 and et % 2 == 0), stop=(j == NJ - 1 and et % 2 == 1))
                        return ins
                    P.rw("tensor", w2_mm,
                         reads=[bfb, buf("w2b%d" % j)],
                         writes=[buf("psM_acc0"), buf("psM_acc1"), buf("psM_acc2"), buf("psM_acc3")])
                def evac(c2):
                    t0 = c2 * TC2
                    par = c2 % 2
                    xb = xin[par]
                    for et in range(DT):
                        xe = xz[et % 4]
                        xbuf = buf("xz%d" % (et % 4))
                        P.rw("vector", lambda h, et=et, xb=xb, xe=xe: h.scalar_tensor_tensor(
                            xe[:, :], ps_acc[et // 2][:, (et % 2) * TC2:(et % 2) * TC2 + TC2],
                            smallv2["b2"][:, et:et + 1], xb[et][:, 1:TC2 + 1], ALU.add, ALU.add),
                            reads=[buf("psM_acc%d" % (et // 2)), buf("sv2_b2"), buf("xin%d_%d" % (par, et))],
                            writes=[xbuf])
                        if l == 0:
                            P.dma("pl", lambda h, et=et, t0=t0, xe=xe: h.dma_start(
                                out=d["sc_xf1"][128 * et:128 * (et + 1), t0:t0 + TC2], in_=xe[:, :]),
                                reads=[xbuf])
                        else:
                            P.rw("vector", lambda h, et=et, par=par, xe=xe: h.tensor_add(
                                xe[:, :], xe[:, :], xz2[par][et][:, :]),
                                reads=[buf("xz2_%d_%d" % (par, et))], writes=[xbuf])
                            P.dma("pl", lambda h, et=et, t0=t0, xe=xe: h.dma_start(
                                out=d["outT"][128 * et:128 * (et + 1), t0:t0 + TC2], in_=xe[:, :]),
                                reads=[xbuf])
                prep(0)
                for c2 in range(NCH2):
                    emit_ag(c2, 0)
                    emit_ag(c2, 1)
                    for j in range(2, NJ):
                        emit_ag(c2, j)
                        if j == 4 and c2 + 1 < NCH2:
                            prep(c2 + 1)
                        emit_w2(j - 2)
                    emit_w2(NJ - 2)
                    emit_w2(NJ - 1)
                    evac(c2)

      except StopIteration:
        pass
      fin = P.barrier()
      P._emit("sync", lambda h: h.nop(), fin, None, 0)
      P.run()
    return nc, d


def stage_inputs(ins, b):
    m = {}
    m["xT"] = np.ascontiguousarray(ins["L0_attn_x"][b].T.astype(np.float32))
    m["wrT"] = np.ascontiguousarray(ins["L0_attn_W_r"].T.astype(np.float32))
    m["wkT"] = np.ascontiguousarray(ins["L0_attn_W_k"].T.astype(np.float32))
    m["wvT"] = np.ascontiguousarray(ins["L0_attn_W_v"].T.astype(np.float32))
    m["woT"] = np.ascontiguousarray(ins["L0_attn_W_o"].T.astype(np.float32))
    for k, src in (("tmr", "time_mix_r"), ("tmk", "time_mix_k"), ("tmv", "time_mix_v")):
        m[k] = np.ascontiguousarray(ins["L0_attn_" + src].astype(np.float32).reshape(8, 128).T)
    m["decay"] = np.ascontiguousarray(ins["L0_attn_time_decay"].astype(np.float32).reshape(-1).reshape(8, 128).T)
    m["first"] = np.ascontiguousarray(ins["L0_attn_time_first"].astype(np.float32).reshape(-1).reshape(8, 128).T)
    m["gnw"] = np.ascontiguousarray(ins["L0_attn_gn_weight"].astype(np.float32).reshape(8, 128).T)
    m["gnb"] = np.ascontiguousarray(ins["L0_attn_gn_bias"].astype(np.float32).reshape(8, 128).T)
    import ml_dtypes
    for l in range(2):
        m["wqkv%d" % l] = np.ascontiguousarray(ins["L0_ffn_wqkv"][l].astype(np.float32))
        m["wol%d" % l] = np.ascontiguousarray(ins["L0_ffn_wo"][l].astype(np.float32))
        m["w1l%d" % l] = np.ascontiguousarray(ins["L0_ffn_w1"][l].astype(np.float32))
        m["w2l%d" % l] = np.ascontiguousarray(ins["L0_ffn_w2"][l].astype(np.float32))
        m["qs%d" % l] = np.ascontiguousarray(np.tile(ins["L0_ffn_q_scale"][l].astype(np.float32), 2)[:, None])
        m["ks%d" % l] = np.ascontiguousarray(np.tile(ins["L0_ffn_k_scale"][l].astype(np.float32), 2)[:, None])
        m["lg%d" % l] = np.ascontiguousarray(ins["L0_ffn_ln_g"][l].astype(np.float32).reshape(8, 128).T)
        m["lb%d" % l] = np.ascontiguousarray(ins["L0_ffn_ln_b"][l].astype(np.float32).reshape(8, 128).T)
        m["b1l%d" % l] = np.ascontiguousarray(ins["L0_ffn_b1"][l].astype(np.float32)[:, None])
        _b1 = ins["L0_ffn_b1"][l].astype(np.float32)
        _pa = np.zeros(22 * 128, np.float32); _pa[:2730] = _b1[:2730]
        _pg = np.zeros(22 * 128, np.float32); _pg[:2730] = _b1[2730:]
        m["b1a%d" % l] = np.ascontiguousarray(_pa.reshape(22, 128).T)
        m["b1g%d" % l] = np.ascontiguousarray(_pg.reshape(22, 128).T)
        m["b2l%d" % l] = np.ascontiguousarray(ins["L0_ffn_b2"][l].astype(np.float32).reshape(8, 128).T)
    i128 = np.arange(128)
    m["identb"] = np.eye(128, dtype=ml_dtypes.bfloat16)
    m["identf"] = np.eye(128, dtype=np.float32)
    m["identd"] = np.ascontiguousarray(np.tile(np.eye(64, dtype=np.float32), (2, 1)))
    perm = np.zeros((128, 128), np.float32)
    for po in range(128):
        if po % 64 < 32:
            perm[po + 32, po] = -1.0
        else:
            perm[po - 32, po] = 1.0
    m["permb"] = perm.astype(ml_dtypes.bfloat16)
    m["onesf"] = np.ones((128, 1), np.float32)
    hv = np.zeros((128, 2), np.float32)
    hv[:64, 0] = 1.0
    hv[64:, 1] = 1.0
    m["halvesf"] = hv
    og = np.zeros((128, 32), np.float32)
    for g in range(4):
        for p in range(128):
            og[p, g * 8 + 2 * g + (1 if p >= 64 else 0)] = 1.0
    m["og4"] = og
    m["og4b"] = og.astype(ml_dtypes.bfloat16)
    m["onesb1"] = np.ones((128, 1), ml_dtypes.bfloat16)
    _p = np.arange(128)[:, None]; _f = np.arange(128)[None, :]
    m["maskl"] = np.where(_p >= _f, 0.0, -3e38).astype(ml_dtypes.bfloat16)
    m["maskr"] = np.where(_p <= _f, 0.0, -3e38).astype(ml_dtypes.bfloat16)
    bg = np.zeros((8, 512), np.float32)
    for g in range(4):
        bg[2 * g, g * 128: g * 128 + 64] = 1.0
        bg[2 * g + 1, g * 128 + 64: g * 128 + 128] = 1.0
    m["bg4"] = bg.astype(ml_dtypes.bfloat16)
    m["bonef"] = np.ones((1, 128), np.float32)
    h2 = np.zeros((2, 128), np.float32)
    h2[0, :64] = 1.0
    h2[1, 64:] = 1.0
    m["halves2"] = h2
    DIM_HEAD, WINDOW = 64, 512
    half_idx = np.arange(0, DIM_HEAD, 2, dtype=np.float64)
    inv_freq = 1.0 / 10000.0 ** (half_idx / DIM_HEAD)
    tt = np.arange(T, dtype=np.float64)
    freqs = np.concatenate([tt[:, None] * inv_freq[None, :]] * 2, axis=-1)
    base_ = (half_idx + 0.4 * DIM_HEAD) / (1.4 * DIM_HEAD)
    power = (tt - T // 2) / float(WINDOW // 2)
    scale = np.concatenate([base_[None, :] ** power[:, None]] * 2, axis=-1)
    cos, sin = np.cos(freqs), np.sin(freqs)
    m["c1"] = np.ascontiguousarray(np.tile((cos * scale).T, (2, 1)).astype(np.float32))
    m["s1"] = np.ascontiguousarray(np.tile((sin * scale).T, (2, 1)).astype(np.float32))
    m["c2"] = np.ascontiguousarray(np.tile((cos / scale).T, (2, 1)).astype(np.float32))
    m["s2"] = np.ascontiguousarray(np.tile((sin / scale).T, (2, 1)).astype(np.float32))
    return m




_CACHE = {}

def kernel(**inputs):
    if "nc" not in _CACHE:
        nc, d = build(debug=0)
        _CACHE["nc"] = nc
    nc = _CACHE["nc"]
    base = stage_inputs(inputs, 0)
    in_maps = []
    for b in range(8):
        m = dict(base)
        m["xT"] = np.ascontiguousarray(inputs["L0_attn_x"][b].T.astype(np.float32))
        in_maps.append(m)
    res = run_bass_kernel_spmd(nc, in_maps, core_ids=list(range(8)))
    out = np.stack([np.asarray(r["outT"]).T for r in res.results], axis=0)
    return np.ascontiguousarray(out.astype(np.float32))


# version 38
# speedup vs baseline: 1.0078x; 1.0078x over previous
import sys
if '/opt/trn_rl_repo' not in sys.path:
    sys.path.insert(0, '/opt/trn_rl_repo')
import numpy as np
from contextlib import ExitStack

import concourse.bass as bass
import concourse.mybir as mybir
from concourse.bass_utils import run_bass_kernel_spmd

F32 = mybir.dt.float32
BF16 = mybir.dt.bfloat16
AF = mybir.ActivationFunctionType
ALU = mybir.AluOpType

T = 4096
TC = 512
NCH = T // TC
DT = 8

ENG = ("sync", "vector", "scalar", "tensor", "gpsimd")
QUEUE = {"sp": ("sync", "dsp"), "pl": ("gpsimd", "dpl")}
ESEM = {"vector": "ve", "scalar": "se", "tensor": "pe", "gpsimd": "pl"}


class Buf:
    __slots__ = ("ev", "readers", "name", "serial")
    def __init__(self, name=""):
        self.ev = None
        self.readers = []
        self.name = name
        self.serial = name.startswith("ps")


NSLOT = 8

class Prog:
    def __init__(self, nc, stack):
        self.strict = True
        self.nc = nc
        self.stack = stack
        self.ops = {e: [] for e in ENG}
        self.sem = {}
        self.count = {}
        self.last_wait = {}
        for s in ("ve", "se", "pe", "pl"):
            self.sem[s] = stack.enter_context(nc.semaphore(s))
            self.count[s] = 0
        self.qslot = {}
        for q in ("dsp", "dpl"):
            for i in range(NSLOT):
                s = "%s%d" % (q, i)
                self.sem[s] = stack.enter_context(nc.semaphore(s))
                self.count[s] = 0
            self.qslot[q] = 0
        self.n_instr = 0

    def _emit(self, eng, fn, deps, inc_sem, inc_n):
        need = {}
        for s, c in list(deps) + getattr(self, "floor", []):
            if c is None or c <= 0:
                continue
            if s in ("ve", "se", "pe", "pl") and ESEM.get(eng) == s and not self.strict:
                continue
            if need.get(s, 0) < c:
                need[s] = c
        waits = []
        for s, c in need.items():
            k = (eng, s)
            if self.last_wait.get(k, 0) >= c:
                continue
            self.last_wait[k] = c
            waits.append((self.sem[s], c))
        if inc_sem is not None:
            self.count[inc_sem] += inc_n
            tgt = (self.sem[inc_sem], inc_n)
            ev = (inc_sem, self.count[inc_sem])
        else:
            tgt = None
            ev = None
        class _Rec:
            __slots__ = ("calls",)
            def __init__(self):
                self.calls = []
            def __getattr__(self, name):
                def m(*a, **k):
                    self.calls.append((name, a, k))
                    return None
                return m
        rec = _Rec()
        fn(rec)
        assert rec.calls, "emission recorded no instruction"
        def closure(h, waits=waits, calls=rec.calls, tgt=tgt):
            for sh, c in waits:
                h.wait_ge(sh, c)
            ins = None
            for (name, a, k) in calls:
                ins = getattr(h, name)(*a, **k)
            if tgt is not None:
                ins.then_inc(tgt[0], tgt[1])
        self.ops[eng].append(closure)
        self.n_instr += len(rec.calls) + len(waits)
        return ev

    @staticmethod
    def _collect(items):
        deps = []
        for b in items:
            if b is None:
                continue
            if isinstance(b, tuple):
                deps.append(b)
            else:
                if b.ev is not None:
                    deps.append(b.ev)
        return deps

    def rw(self, eng, fn, reads=(), writes=(), extra=()):
        reads = list(reads)
        writes = list(writes)
        for b in list(reads):
            if b is not None and not isinstance(b, tuple) and b.serial:
                reads.remove(b)
                writes.append(b)
        deps = self._collect(reads)
        for b in writes:
            if b is None:
                continue
            if b.ev is not None:
                deps.append(b.ev)
            deps.extend(b.readers)
        deps.extend(self._collect(extra))
        ev = self._emit(eng, fn, deps, ESEM[eng], 1)
        for b in reads:
            if b is not None and not isinstance(b, tuple):
                b.readers.append(ev)
        for b in writes:
            if b is not None:
                b.ev = ev
                b.readers = []
        return ev

    def dma(self, q, fn, reads=(), writes=(), extra=()):
        reads = list(reads)
        writes = list(writes)
        for b in list(reads):
            if b is not None and not isinstance(b, tuple) and b.serial:
                reads.remove(b)
                writes.append(b)
        eng, qname = QUEUE[q]
        slot = self.qslot[qname]
        self.qslot[qname] = (slot + 1) % NSLOT
        semn = "%s%d" % (qname, slot)
        deps = self._collect(reads)
        deps.append((semn, self.count[semn]))
        for b in writes:
            if b is None:
                continue
            if b.ev is not None:
                deps.append(b.ev)
            deps.extend(b.readers)
        deps.extend(self._collect(extra))
        ev = self._emit(eng, fn, deps, semn, 16)
        for b in reads:
            if b is not None and not isinstance(b, tuple):
                b.readers.append(ev)
        for b in writes:
            if b is not None:
                b.ev = ev
                b.readers = []
        return ev

    def barrier(self):
        evs = [(s, self.count[s]) for s in self.sem if self.count[s] > 0]
        return evs

    def run(self):
        with self.nc.Block() as block:
            for e in ENG:
                getattr(block, e)(lambda h, e=e: [f(h) for f in self.ops[e]])


def sb(nc, stack, name, shape, dtype):
    return stack.enter_context(nc.sbuf_tensor(name, shape, dtype))


def ps_(nc, stack, name, shape, dtype=F32):
    return stack.enter_context(nc.psum_tensor(name, shape, dtype))


def build(debug=0):
    PH = 6
    SUB = 9
    nc = bass.Bass("TRN2", target_bir_lowering=False, debug=False,
                   detect_race_conditions=False)
    d = {}
    def din(name, shape, dtype=F32):
        d[name] = nc.dram_tensor(name, shape, dtype, kind="ExternalInput").ap()
    def dout(name, shape, dtype=F32):
        d[name] = nc.dram_tensor(name, shape, dtype, kind="ExternalOutput").ap()
    def dint(name, shape, dtype=F32):
        kind = "ExternalOutput" if debug else "Internal"
        d[name] = nc.dram_tensor(name, shape, dtype, kind=kind).ap()

    din("xT", [1024, T])
    for w in ("wrT", "wkT", "wvT", "woT"):
        din(w, [1024, 1024])
    for v in ("tmr", "tmk", "tmv", "decay", "first", "gnw", "gnb"):
        din(v, [128, 8])
    for l in range(2):
        din("wqkv%d" % l, [1024, 1536])
        din("wol%d" % l, [512, 1024])
        din("w1l%d" % l, [1024, 5460])
        din("w2l%d" % l, [2730, 1024])
        din("qs%d" % l, [128, 1])
        din("ks%d" % l, [128, 1])
        din("lg%d" % l, [128, 8])
        din("lb%d" % l, [128, 8])
        din("b1l%d" % l, [5460, 1])
        din("b2l%d" % l, [128, 8])
    din("identb", [128, 128], BF16)
    din("identf", [128, 128])
    din("identd", [128, 64])
    for l in range(2):
        din("b1a%d" % l, [128, 22])
        din("b1g%d" % l, [128, 22])
    din("permb", [128, 128], BF16)
    din("onesf", [128, 1])
    din("halvesf", [128, 2])
    din("og4", [128, 32])
    din("og4b", [128, 32], BF16)
    din("onesb1", [128, 1], BF16)
    din("maskl", [128, 128], BF16)
    din("maskr", [128, 128], BF16)
    din("bg4", [8, 512], BF16)
    din("bonef", [1, 128])
    din("halves2", [2, 128])
    for t_ in ("c1", "s1", "c2", "s2"):
        din(t_, [128, T])
    dout("outT", [1024, T])
    dint("sc_out", [1024, T], BF16)
    dint("sc_x1", [1024, T])
    dint("sc_xf0", [1024, T])
    dint("sc_xm", [1024, T])
    dint("sc_xf1", [1024, T])
    d["sc_stats"] = nc.dram_tensor("sc_stats", [2, T], F32, kind="Internal").ap()
    nc_sums = nc.dram_tensor("sc_sums", [8 * 512], F32, kind="Internal"); d["sc_sums"] = nc_sums.ap()
    if debug:
        dout("dbg_out", [1024, T])
        dout("dbg_sb", [128, 64])

    with ExitStack() as st:
      try:
        P = Prog(nc, st)
        cpool = sb(nc, st, "sb_cpool", [128, 8], F32)
        ident = sb(nc, st, "sb_ident", [128, 128], BF16)
        perm = sb(nc, st, "sb_perm", [128, 128], BF16)
        onesf = sb(nc, st, "sb_onesf", [128, 1], F32)
        halvesf = sb(nc, st, "sb_halvesf", [128, 2], F32)
        og4 = sb(nc, st, "sb_og4", [128, 32], F32)
        og4b = sb(nc, st, "sb_og4b", [128, 32], BF16)
        onesb1 = sb(nc, st, "sb_onesb1", [128, 1], BF16)
        maskl = sb(nc, st, "sb_maskl", [128, 128], BF16)
        maskr = sb(nc, st, "sb_maskr", [128, 128], BF16)
        bg4 = sb(nc, st, "sb_bg4", [8, 512], BF16)
        bonef = sb(nc, st, "sb_bonef", [1, 128], F32)
        halves2 = sb(nc, st, "sb_halves2", [2, 128], F32)
        smallv = {}
        for v in ("tmr", "tmk", "tmv", "decay", "first", "gnw", "gnb"):
            smallv[v] = sb(nc, st, "s_" + v, [128, 8], F32)
        xin = [[sb(nc, st, "xin%d_%d" % (p, g), [128, TC + 1], F32) for g in range(DT)] for p in range(2)]
        hbf = [sb(nc, st, "hbf%d" % g, [128, TC + 1], BF16) for g in range(DT)]
        stt_m = sb(nc, st, "stt_m", [1, TC], F32)
        stt_i = sb(nc, st, "stt_i", [1, TC], F32)
        fin8 = sb(nc, st, "fin8", [128, 48], F32)
        asum = [sb(nc, st, "asum%d" % g, [128, NCH], F32) for g in range(DT)]
        qsum = [sb(nc, st, "qsum%d" % g, [128, NCH], F32) for g in range(DT)]
        scale_sb = sb(nc, st, "scale_sb", [128, 8], F32)
        bias_sb = sb(nc, st, "bias_sb", [128, 8], F32)
        w_eu = sb(nc, st, "w_eu", [128, 16], F32)

        B = {}
        def buf(name):
            if name not in B:
                B[name] = Buf(name)
            return B[name]

        P.dma("sp", lambda h: h.dma_start(out=ident[:, :], in_=d["identb"]), writes=[buf("ident")])
        P.dma("sp", lambda h: h.dma_start(out=perm[:, :], in_=d["permb"]), writes=[buf("perm")])
        P.dma("sp", lambda h: h.dma_start(out=onesf[:, :], in_=d["onesf"]), writes=[buf("onesf")])
        P.dma("sp", lambda h: h.dma_start(out=halvesf[:, :], in_=d["halvesf"]), writes=[buf("halvesf")])
        P.dma("sp", lambda h: h.dma_start(out=og4[:, :], in_=d["og4"]), writes=[buf("og4")])
        P.dma("sp", lambda h: h.dma_start(out=og4b[:, :], in_=d["og4b"]), writes=[buf("og4b")])
        P.dma("sp", lambda h: h.dma_start(out=onesb1[:, :], in_=d["onesb1"]), writes=[buf("onesb1")])
        P.dma("sp", lambda h: h.dma_start(out=maskl[:, :], in_=d["maskl"]), writes=[buf("maskl")])
        P.dma("sp", lambda h: h.dma_start(out=maskr[:, :], in_=d["maskr"]), writes=[buf("maskr")])
        P.dma("sp", lambda h: h.dma_start(out=bg4[:, :], in_=d["bg4"]), writes=[buf("bg4")])
        P.dma("sp", lambda h: h.dma_start(out=bonef[:, :], in_=d["bonef"]), writes=[buf("bonef")])
        P.dma("sp", lambda h: h.dma_start(out=halves2[:, :], in_=d["halves2"]), writes=[buf("halves2")])
        for v in ("tmr", "tmk", "tmv", "decay", "first", "gnw", "gnb"):
            P.dma("sp", lambda h, v=v: h.dma_start(out=smallv[v][:, :], in_=d[v]), writes=[buf("sv_" + v)])
        P.rw("gpsimd", lambda h: h.memset(cpool[:, 0:1], -8.0), writes=[buf("cpool")])
        P.rw("gpsimd", lambda h: h.memset(fin8[:, :], 0.0), writes=[buf("fin8")])
        P.rw("gpsimd", lambda h: h.memset(cpool[:, 1:2], 1e-5), writes=[buf("cpool")])
        P.rw("scalar", lambda h: h.activation(w_eu[:, 0:8], smallv["decay"][:, :], AF.Exp),
             reads=[buf("sv_decay")], writes=[buf("w_eu")])
        P.rw("scalar", lambda h: h.activation(w_eu[:, 0:8], w_eu[:, 0:8], AF.Exp, scale=-1.0),
             writes=[buf("w_eu")])
        P.rw("scalar", lambda h: h.activation(w_eu[:, 8:16], smallv["first"][:, :], AF.Exp),
             reads=[buf("sv_first")], writes=[buf("w_eu")])

        with ExitStack() as stA:
            wr = [sb(nc, stA, "wr%d" % g, [128, 1024], BF16) for g in range(DT)]
            wk = [sb(nc, stA, "wk%d" % g, [128, 1024], BF16) for g in range(DT)]
            wv = [sb(nc, stA, "wv%d" % g, [128, 1024], BF16) for g in range(DT)]
            hb = [sb(nc, stA, "hb%d" % g, [128, TC + 1], BF16) for g in range(DT)]
            numb = [sb(nc, stA, "numb%d" % g, [128, TC + 1], F32) for g in range(DT)]
            denb = [sb(nc, stA, "denb%d" % g, [128, TC + 1], F32) for g in range(DT)]
            NS = 2
            S1 = [sb(nc, stA, "Sa%d" % g, [128, TC], F32) for g in range(NS)]
            S2 = [sb(nc, stA, "Sb%d" % g, [128, TC], F32) for g in range(NS)]
            S3 = [sb(nc, stA, "Sc%d" % g, [128, TC], F32) for g in range(NS)]
            Bx = [[[sb(nc, stA, "Bx%d_%d_%d" % (p, i, g), [128, TC], BF16) for g in range(DT)] for i in range(3)] for p in range(2)]
            ddb = [sb(nc, stA, "ddb0", [128, TC], BF16)]
            sqs = sb(nc, stA, "sqs", [128, TC], BF16)
            ps_stat = ps_(nc, stA, "ps_stat", [1, TC])
            ps_stat2 = ps_(nc, stA, "ps_stat2", [1, TC])
            ps_b1 = ps_(nc, stA, "ps_b1", [128, TC])
            ps_b2 = ps_(nc, stA, "ps_b2", [128, TC])
            ps_mm = [ps_(nc, stA, "ps_mm%d" % i, [128, TC]) for i in range(4)]

            for g in range(DT):
                for nm, dst in (("wrT", wr), ("wkT", wk), ("wvT", wv)):
                    P.dma("pl", lambda h, nm=nm, dst=dst, g=g: h.dma_start(
                        out=dst[g][:, :], in_=d[nm][128 * g:128 * (g + 1), :]),
                        writes=[buf("w_%s%d" % (nm, g))])
            for g in range(DT):
                P.rw("gpsimd", lambda h, g=g: h.memset(hb[g][:, TC:TC + 1], 0.0), writes=[buf("hb%d" % g)])
                P.rw("gpsimd", lambda h, g=g: h.memset(numb[g][:, TC:TC + 1], 0.0), writes=[buf("numb%d" % g)])
                P.rw("gpsimd", lambda h, g=g: h.memset(denb[g][:, TC:TC + 1], 0.0), writes=[buf("denb%d" % g)])

            def frontA_thunks(c):
                t0 = c * TC
                par = c % 2
                xb = xin[par]
                th = []
                def t_loads():
                    for g in range(DT):
                        P.dma("sp", lambda h, g=g, t0=t0, xb=xb: h.dma_start(
                            out=xb[g][:, 1:TC + 1], in_=d["xT"][128 * g:128 * (g + 1), t0:t0 + TC]),
                            writes=[buf("xin%d_%d" % (par, g))])
                th.append(t_loads)
                def t_stat(g):
                    bx = buf("xin%d_%d" % (par, g))
                    bsq = buf("sqs")
                    P.rw("scalar", lambda h, g=g, xb=xb: h.activation(sqs[:, :], xb[g][:, 1:TC + 1], AF.Square),
                         reads=[bx], writes=[bsq])
                    P.rw("tensor", lambda h, g=g, xb=xb: h.matmul(
                        ps_stat[:, :], onesf[:, :], xb[g][:, 1:TC + 1], start=(g == 0), stop=(g == DT - 1)),
                        reads=[bx, buf("onesf")], writes=[buf("ps_stat")])
                    P.rw("tensor", lambda h, g=g: h.matmul(
                        ps_stat2[:, :], onesb1[:, :], sqs[:, :], start=(g == 0), stop=(g == DT - 1)),
                        reads=[bsq, buf("onesb1")], writes=[buf("ps_stat2")])
                for g in range(DT):
                    th.append(lambda g=g: t_stat(g))
                def t_fin():
                    P.rw("vector", lambda h: h.tensor_scalar(stt_m[:, :], ps_stat[:, :], 1.0 / 1024, None, ALU.mult),
                         reads=[buf("ps_stat")], writes=[buf("stt_m")])
                    P.rw("vector", lambda h: h.tensor_scalar(stt_i[:, :], ps_stat2[:, :], 1.0 / 1024, None, ALU.mult),
                         reads=[buf("ps_stat2")], writes=[buf("stt_i")])
                    P.rw("vector", lambda h: h.scalar_tensor_tensor(
                        sqs[0:1, 0:TC], stt_m[:, :], 1.0, stt_m[:, :], ALU.mult, ALU.mult),
                        reads=[buf("stt_m")], writes=[buf("sqs")])
                    P.rw("vector", lambda h: h.tensor_sub(stt_i[:, :], stt_i[:, :], sqs[0:1, 0:TC]),
                         reads=[buf("sqs")], writes=[buf("stt_i")])
                    P.rw("scalar", lambda h: h.activation(stt_i[:, :], stt_i[:, :], AF.Ln, bias=cpool[0:1, 1:2]),
                         reads=[buf("cpool")], writes=[buf("stt_i")])
                    P.rw("scalar", lambda h: h.activation(stt_i[:, :], stt_i[:, :], AF.Exp, scale=-0.5),
                         writes=[buf("stt_i")])
                    P.rw("tensor", lambda h: h.matmul(ps_b1[:, :], bonef[:, :], stt_m[:, :], start=True, stop=True),
                         reads=[buf("stt_m"), buf("bonef")], writes=[buf("ps_b1")])
                    P.rw("tensor", lambda h: h.matmul(ps_b2[:, :], bonef[:, :], stt_i[:, :], start=True, stop=True),
                         reads=[buf("stt_i"), buf("bonef")], writes=[buf("ps_b2")])
                th.append(t_fin)
                def t_h(g):
                    bh = buf("hb%d" % g)
                    dd = ddb[0]
                    bdd = buf("ddb0")
                    P.rw("vector", lambda h, g=g: h.tensor_copy(hb[g][:, 0:1], hb[g][:, TC:TC + 1]),
                         writes=[bh])
                    P.rw("vector", lambda h, g=g, xb=xb: h.tensor_sub(
                        hb[g][:, 1:TC + 1], xb[g][:, 1:TC + 1], ps_b1[:, :]),
                        reads=[buf("xin%d_%d" % (par, g)), buf("ps_b1")], writes=[bh])
                    P.rw("vector", lambda h, g=g: h.tensor_mul(
                        hb[g][:, 1:TC + 1], hb[g][:, 1:TC + 1], ps_b2[:, :]),
                        reads=[buf("ps_b2")], writes=[bh])
                    P.rw("vector", lambda h, g=g, dd=dd: h.tensor_sub(
                        dd[:, :], hb[g][:, 1:TC + 1], hb[g][:, 0:TC]),
                        reads=[bh], writes=[bdd])
                    for i, tm in enumerate(("tmr", "tmk", "tmv")):
                        P.rw("vector", lambda h, g=g, i=i, tm=tm, dd=dd: h.scalar_tensor_tensor(
                            Bx[par][i][g][:, :], dd[:, :], smallv[tm][:, g:g + 1], hb[g][:, 0:TC],
                            ALU.mult, ALU.add),
                            reads=[bdd, buf("sv_" + tm), bh],
                            writes=[buf("Bx%d_%d_%d" % (par, i, g))])
                for g in range(DT):
                    th.append(lambda g=g: t_h(g))
                return th

            def scanA(c, th):
                par = c % 2
                t0 = c * TC
                def mk_mm(dst_ps, wts, bxi, g, par=par):
                    def f(h):
                        ins = None
                        for gg in range(DT):
                            ins = h.matmul(dst_ps[:, :], wts[gg][:, 128 * g:128 * (g + 1)],
                                           Bx[par][bxi][gg][:, :], start=(gg == 0), stop=(gg == DT - 1))
                        return ins
                    return f
                def emit_k(g):
                    sg = g % NS
                    kp = ps_mm[(g % 2) * 3]
                    bk_ps = buf("ps_mm%d" % ((g % 2) * 3))
                    P.rw("tensor", mk_mm(kp, wk, 1, g),
                         reads=[buf("Bx%d_1_%d" % (par, gg)) for gg in range(DT)] + [buf("w_wkT%d" % gg) for gg in range(DT)],
                         writes=[bk_ps])
                    P.rw("scalar", lambda h, g=g, sg=sg, kp=kp: h.activation(S2[sg][:, :], kp[:, :], AF.Exp),
                         reads=[bk_ps], writes=[buf("S2%d" % sg)])
                emit_k(0)
                for g in range(DT):
                    sg = g % NS
                    bv_ps, br_ps = buf("ps_mm1"), buf("ps_mm2")
                    P.rw("tensor", mk_mm(ps_mm[1], wv, 2, g),
                         reads=[buf("Bx%d_2_%d" % (par, gg)) for gg in range(DT)] + [buf("w_wvT%d" % gg) for gg in range(DT)],
                         writes=[bv_ps])
                    P.rw("vector", lambda h, g=g, sg=sg: h.tensor_mul(S3[sg][:, :], S2[sg][:, :], ps_mm[1][:, :]),
                         reads=[buf("S2%d" % sg), bv_ps], writes=[buf("S3%d" % sg)])
                    if g + 1 < DT:
                        emit_k(g + 1)
                    bn, bd = buf("numb%d" % g), buf("denb%d" % g)
                    P.rw("vector", lambda h, g=g: h.tensor_copy(numb[g][:, 0:1], numb[g][:, TC:TC + 1]),
                         writes=[bn])
                    P.rw("vector", lambda h, g=g: h.tensor_copy(denb[g][:, 0:1], denb[g][:, TC:TC + 1]),
                         writes=[bd])
                    P.rw("vector", lambda h, g=g, sg=sg: h.tensor_tensor_scan(
                        numb[g][:, 1:TC + 1], w_eu[:, g:g + 1].broadcast_to((128, TC)), S3[sg][:, :],
                        numb[g][:, 0:1], ALU.mult, ALU.add),
                        reads=[buf("w_eu"), buf("S3%d" % sg)], writes=[bn])
                    P.rw("vector", lambda h, g=g, sg=sg: h.tensor_tensor_scan(
                        denb[g][:, 1:TC + 1], w_eu[:, g:g + 1].broadcast_to((128, TC)), S2[sg][:, :],
                        denb[g][:, 0:1], ALU.mult, ALU.add),
                        reads=[buf("w_eu"), buf("S2%d" % sg)], writes=[bd])
                    P.rw("vector", lambda h, g=g, sg=sg: h.scalar_tensor_tensor(
                        S3[sg][:, :], S3[sg][:, :], w_eu[:, 8 + g:9 + g], numb[g][:, 0:TC], ALU.mult, ALU.add),
                        reads=[buf("w_eu"), bn], writes=[buf("S3%d" % sg)])
                    P.rw("vector", lambda h, g=g, sg=sg: h.tensor_scalar(
                        S2[sg][:, :], S2[sg][:, :], w_eu[:, 8 + g:9 + g], 1e-9, ALU.mult, ALU.add),
                        reads=[buf("w_eu")], writes=[buf("S2%d" % sg)])
                    P.rw("vector", lambda h, g=g, sg=sg: h.tensor_add(S2[sg][:, :], S2[sg][:, :], denb[g][:, 0:TC]),
                         reads=[bd], writes=[buf("S2%d" % sg)])
                    P.rw("tensor", mk_mm(ps_mm[2], wr, 0, g),
                         reads=[buf("Bx%d_0_%d" % (par, gg)) for gg in range(DT)] + [buf("w_wrT%d" % gg) for gg in range(DT)],
                         writes=[br_ps])
                    P.rw("scalar", lambda h, g=g, sg=sg: h.activation(S1[sg][:, :], ps_mm[2][:, :], AF.Exp, scale=-1.0),
                         reads=[br_ps], writes=[buf("S1%d" % sg)])
                    P.rw("vector", lambda h, g=g, sg=sg: h.scalar_tensor_tensor(
                        S2[sg][:, :], S1[sg][:, :], 1.0, S2[sg][:, :], ALU.add, ALU.mult),
                        reads=[buf("S1%d" % sg)], writes=[buf("S2%d" % sg)])
                    P.rw("scalar", lambda h, g=g, sg=sg: h.activation(S2[sg][:, :], S2[sg][:, :], AF.Ln),
                         writes=[buf("S2%d" % sg)])
                    P.rw("scalar", lambda h, g=g, sg=sg: h.activation(S2[sg][:, :], S2[sg][:, :], AF.Exp, scale=-1.0),
                         writes=[buf("S2%d" % sg)])
                    P.rw("vector", lambda h, g=g, c=c, sg=sg: h.scalar_tensor_tensor(
                        hbf[g][:, 0:TC], S3[sg][:, :], 1.0, S2[sg][:, :], ALU.mult, ALU.mult,
                        accum_out=asum[g][:, c:c + 1]),
                        reads=[buf("S3%d" % sg), buf("S2%d" % sg)],
                        writes=[buf("hbf%d" % g), buf("asum%d" % g)])
                    P.rw("scalar", lambda h, g=g, c=c: h.activation(
                        sqs[:, :], hbf[g][:, 0:TC], AF.Square, accum_out=qsum[g][:, c:c + 1]),
                        reads=[buf("hbf%d" % g)], writes=[buf("sqs"), buf("qsum%d" % g)])
                    P.dma("pl", lambda h, g=g, t0=t0: h.dma_start(
                        out=d["sc_out"][128 * g:128 * (g + 1), t0:t0 + TC], in_=hbf[g][:, 0:TC]),
                        reads=[buf("hbf%d" % g)])
                    if th:
                        for _ in range((3, 3, 3, 2, 2, 2, 2, 1)[g]):
                            if th:
                                th.pop(0)()

            for th_ in frontA_thunks(0):
                th_()
            for c in range(NCH):
                tb = frontA_thunks(c + 1) if c + 1 < NCH else []
                scanA(c, tb)
                for th_ in tb:
                    th_()
                tb.clear()
            for g in range(DT):
                P.rw("vector", lambda h, g=g: h.reduce_sum(
                    scale_sb[:, g:g + 1], asum[g][:, :], mybir.AxisListType.X),
                    reads=[buf("asum%d" % g)], writes=[buf("scale_sb")])
                P.rw("vector", lambda h, g=g: h.reduce_sum(
                    bias_sb[:, g:g + 1], qsum[g][:, :], mybir.AxisListType.X),
                    reads=[buf("qsum%d" % g)], writes=[buf("bias_sb")])
            P.rw("tensor", lambda h: h.matmul(ps_b1[0:2, 0:8],
                                              halvesf[:, :], scale_sb[:, :], start=True, stop=True),
                 reads=[buf("scale_sb"), buf("halvesf")], writes=[buf("ps_b1")])
            P.rw("tensor", lambda h: h.matmul(ps_b2[0:2, 0:8], halvesf[:, :], bias_sb[:, :], start=True, stop=True),
                 reads=[buf("bias_sb"), buf("halvesf")], writes=[buf("ps_b2")])
            if debug:
                P.rw("vector", lambda h: h.tensor_copy(fin8[0:2, 40:48], ps_b2[0:2, 0:8]),
                     reads=[buf("ps_b2")], writes=[buf("fin8")])
                P.rw("vector", lambda h: h.tensor_copy(fin8[0:2, 24:32], ps_b1[0:2, 0:8]),
                     reads=[buf("ps_b1")], writes=[buf("fin8")])
                P.dma("pl", lambda h: h.dma_start(out=d["dbg_sb"][:, 24:32], in_=bias_sb[:, :]),
                      reads=[buf("bias_sb")])
            ninv = 1.0 / (64.0 * T)
            P.rw("vector", lambda h: h.tensor_scalar(fin8[0:2, 0:8],
                                                     ps_b1[0:2, 0:8], ninv, None, ALU.mult),
                 reads=[buf("ps_b1")], writes=[buf("fin8")])
            P.rw("vector", lambda h: h.tensor_scalar(fin8[0:2, 8:16], ps_b2[0:2, 0:8], ninv, None, ALU.mult),
                 reads=[buf("ps_b2")], writes=[buf("fin8")])
            if debug:
                P.dma("pl", lambda h: h.dma_start(out=d["dbg_sb"][0:2, 48:56], in_=fin8[0:2, 8:16]),
                      reads=[buf("fin8")])
            P.rw("vector", lambda h: h.tensor_mul(fin8[0:2, 16:24], fin8[0:2, 0:8], fin8[0:2, 0:8]),
                reads=[buf("fin8")], writes=[buf("fin8")])
            P.rw("vector", lambda h: h.tensor_sub(fin8[0:2, 8:16], fin8[0:2, 8:16], fin8[0:2, 16:24]),
                 reads=[buf("fin8")], writes=[buf("fin8")])
            if debug:
                P.dma("pl", lambda h: h.dma_start(out=d["dbg_sb"][0:2, 56:64], in_=fin8[0:2, 8:16]),
                      reads=[buf("fin8")])
            P.rw("scalar", lambda h: h.activation(fin8[0:2, 8:16], fin8[0:2, 8:16], AF.Ln, bias=cpool[0:2, 1:2]),
                 reads=[buf("cpool")], writes=[buf("fin8")])
            P.rw("scalar", lambda h: h.activation(fin8[0:2, 8:16], fin8[0:2, 8:16], AF.Exp, scale=-0.5),
                 writes=[buf("fin8")])
            P.rw("tensor", lambda h: h.matmul(ps_b1[:, 0:8], halves2[:, :], fin8[0:2, 0:8], start=True, stop=True),
                 reads=[buf("fin8"), buf("halves2")], writes=[buf("ps_b1")])
            P.rw("tensor", lambda h: h.matmul(ps_b2[:, 0:8], halves2[:, :], fin8[0:2, 8:16], start=True, stop=True),
                 reads=[buf("fin8"), buf("halves2")], writes=[buf("ps_b2")])
            P.rw("vector", lambda h: h.tensor_mul(scale_sb[:, :], ps_b2[:, 0:8], smallv["gnw"][:, :]),
                 reads=[buf("ps_b2"), buf("sv_gnw")], writes=[buf("scale_sb")])
            P.rw("vector", lambda h: h.tensor_mul(fin8[:, 32:40], ps_b1[:, 0:8], scale_sb[:, :]),
                 reads=[buf("ps_b1"), buf("scale_sb")], writes=[buf("fin8")])
            P.rw("vector", lambda h: h.tensor_sub(bias_sb[:, :], smallv["gnb"][:, :], fin8[:, 32:40]),
                 reads=[buf("sv_gnb"), buf("fin8")], writes=[buf("bias_sb")])
            if debug:
                P.dma("pl", lambda h: h.dma_start(out=d["dbg_sb"][:, 0:8], in_=scale_sb[:, :]),
                      reads=[buf("scale_sb")])
                P.dma("pl", lambda h: h.dma_start(out=d["dbg_sb"][:, 8:16], in_=bias_sb[:, :]),
                      reads=[buf("bias_sb")])
                P.dma("pl", lambda h: h.dma_start(out=d["dbg_sb"][:, 16:24], in_=fin8[:, 0:8]),
                      reads=[buf("fin8")])
                P.dma("pl", lambda h: h.dma_start(out=d["dbg_sb"][:, 32:40], in_=fin8[:, 40:48]),
                      reads=[buf("fin8")])
                P.dma("pl", lambda h: h.dma_start(out=d["dbg_sb"][:, 40:48], in_=fin8[:, 24:32]),
                      reads=[buf("fin8")])


        if PH < 2:
            raise StopIteration
        P.floor = P.barrier()
        with ExitStack() as stB:
            wob = [sb(nc, stB, "wob%d" % g, [128, 1024], BF16) for g in range(DT)]
            yb = [sb(nc, stB, "yb%d" % g, [128, TC], BF16) for g in range(DT)]
            SB1 = [[sb(nc, stB, "SB1_%d_%d" % (p, g), [128, TC], F32) for g in range(DT)] for p in range(2)]
            sqB = sb(nc, stB, "sqB", [128, TC], BF16)
            x1b = sb(nc, stB, "x1b", [128, TC], BF16)
            ps_stat = ps_(nc, stB, "psB_stat", [1, TC])
            ps_stat2 = ps_(nc, stB, "psB_stat2", [1, TC])
            ps_b1 = ps_(nc, stB, "psB_b1", [128, TC])
            ps_b2 = ps_(nc, stB, "psB_b2", [128, TC])
            ps_mm = [ps_(nc, stB, "psB_mm%d" % i, [128, TC]) for i in range(3)]
            for g in range(DT):
                P.dma("pl", lambda h, g=g: h.dma_start(
                    out=wob[g][:, :], in_=d["woT"][128 * g:128 * (g + 1), :]),
                    writes=[buf("wob%d" % g)])
            def woB(c):
                t0 = c * TC
                par = c % 2
                xb = xin[par]
                S_ = SB1[par]
                for g in range(DT):
                    P.dma("sp", lambda h, g=g, t0=t0, xb=xb: h.dma_start(
                        out=xb[g][:, 1:TC + 1], in_=d["xT"][128 * g:128 * (g + 1), t0:t0 + TC]),
                        writes=[buf("xin%d_%d" % (par, g))])
                    P.dma("sp", lambda h, g=g, t0=t0: h.dma_start(
                        out=hbf[g][:, 0:TC], in_=d["sc_out"][128 * g:128 * (g + 1), t0:t0 + TC]),
                        writes=[buf("hbf%d" % g)])
                    P.rw("vector", lambda h, g=g: h.tensor_scalar(
                        yb[g][:, :], hbf[g][:, 0:TC], scale_sb[:, g:g + 1], bias_sb[:, g:g + 1],
                        ALU.mult, ALU.add),
                        reads=[buf("hbf%d" % g), buf("scale_sb"), buf("bias_sb")],
                        writes=[buf("yb%d" % g)])
                for et in range(DT):
                    pm = ps_mm[et % 2]
                    bpm = buf("psB_mm%d" % (et % 2))
                    def wo_mm(h, et=et, pm=pm):
                        ins = None
                        for gg in range(DT):
                            ins = h.matmul(pm[:, :], wob[gg][:, 128 * et:128 * (et + 1)],
                                           yb[gg][:, :], start=(gg == 0), stop=(gg == DT - 1))
                        return ins
                    P.rw("tensor", wo_mm,
                         reads=[buf("yb%d" % gg) for gg in range(DT)] + [buf("wob%d" % gg) for gg in range(DT)],
                         writes=[bpm])
                    P.rw("vector", lambda h, et=et, pm=pm, xb=xb, S_=S_: h.tensor_add(
                        S_[et][:, :], xb[et][:, 1:TC + 1], pm[:, :]),
                        reads=[buf("xin%d_%d" % (par, et)), bpm], writes=[buf("SB1_%d_%d" % (par, et))])
                    P.dma("pl", lambda h, et=et, t0=t0, S_=S_: h.dma_start(
                        out=d["sc_x1"][128 * et:128 * (et + 1), t0:t0 + TC], in_=S_[et][:, :]),
                        reads=[buf("SB1_%d_%d" % (par, et))])
            def lnB(c):
                t0 = c * TC
                par = c % 2
                S_ = SB1[par]
                for g in range(DT):
                    P.rw("scalar", lambda h, g=g, S_=S_: h.activation(sqB[:, :], S_[g][:, :], AF.Square),
                         reads=[buf("SB1_%d_%d" % (par, g))], writes=[buf("sqB")])
                    P.rw("tensor", lambda h, g=g, S_=S_: h.matmul(
                        ps_stat[:, :], onesf[:, :], S_[g][:, :], start=(g == 0), stop=(g == DT - 1)),
                        reads=[buf("SB1_%d_%d" % (par, g)), buf("onesf")], writes=[buf("psB_stat")])
                    P.rw("tensor", lambda h, g=g: h.matmul(
                        ps_stat2[:, :], onesb1[:, :], sqB[:, :], start=(g == 0), stop=(g == DT - 1)),
                        reads=[buf("sqB"), buf("onesb1")], writes=[buf("psB_stat2")])
                P.rw("vector", lambda h: h.tensor_scalar(stt_m[:, :], ps_stat[:, :], 1.0 / 1024, None, ALU.mult),
                     reads=[buf("psB_stat")], writes=[buf("stt_m")])
                P.rw("vector", lambda h: h.tensor_scalar(stt_i[:, :], ps_stat2[:, :], 1.0 / 1024, None, ALU.mult),
                     reads=[buf("psB_stat2")], writes=[buf("stt_i")])
                P.rw("vector", lambda h: h.tensor_mul(sqB[0:1, 0:TC], stt_m[:, :], stt_m[:, :]),
                     reads=[buf("stt_m")], writes=[buf("sqB")])
                P.rw("vector", lambda h: h.tensor_sub(stt_i[:, :], stt_i[:, :], sqB[0:1, 0:TC]),
                     reads=[buf("sqB")], writes=[buf("stt_i")])
                P.rw("scalar", lambda h: h.activation(stt_i[:, :], stt_i[:, :], AF.Ln, bias=cpool[0:1, 1:2]),
                     reads=[buf("cpool")], writes=[buf("stt_i")])
                P.rw("scalar", lambda h: h.activation(stt_i[:, :], stt_i[:, :], AF.Exp, scale=-0.5),
                     writes=[buf("stt_i")])
                P.rw("tensor", lambda h: h.matmul(ps_b1[:, :], bonef[:, :], stt_m[:, :], start=True, stop=True),
                     reads=[buf("stt_m"), buf("bonef")], writes=[buf("psB_b1")])
                P.rw("tensor", lambda h: h.matmul(ps_b2[:, :], bonef[:, :], stt_i[:, :], start=True, stop=True),
                     reads=[buf("stt_i"), buf("bonef")], writes=[buf("psB_b2")])
                for g in range(DT):
                    P.rw("vector", lambda h, g=g, S_=S_: h.tensor_sub(S_[g][:, :], S_[g][:, :], ps_b1[:, :]),
                         reads=[buf("psB_b1")], writes=[buf("SB1_%d_%d" % (par, g))])
                    P.rw("vector", lambda h, g=g, S_=S_: h.tensor_mul(S_[g][:, :], S_[g][:, :], ps_b2[:, :]),
                         reads=[buf("psB_b2")], writes=[buf("SB1_%d_%d" % (par, g))])
                    P.dma("pl", lambda h, g=g, t0=t0, S_=S_: h.dma_start(
                        out=d["sc_xf0"][128 * g:128 * (g + 1), t0:t0 + TC], in_=S_[g][:, :]),
                        reads=[buf("SB1_%d_%d" % (par, g))])
            woB(0)
            for c in range(NCH):
                if c + 1 < NCH:
                    woB(c + 1)
                lnB(c)
        if PH < 3:
            raise StopIteration
        for l in range(2):
            src_xf = "sc_xf0" if l == 0 else "sc_xf1"
            P.floor = P.barrier()
            if PH < 3 + 2 * l:
                raise StopIteration
            with ExitStack() as sA:
                wqb = [sb(nc, sA, "wqb%d_%d" % (l, g), [128, 1536], BF16) for g in range(DT)]
                wlb = [sb(nc, sA, "wlb%d_%d" % (l, g), [128, 1024], BF16) for g in range(4)]
                qw = [[sb(nc, sA, "qw%d_%d_%d" % (l, p, g), [128, TC], BF16) for g in range(4)] for p in range(2)]
                kw = [[sb(nc, sA, "kw%d_%d_%d" % (l, p, g), [128, TC], BF16) for g in range(4)] for p in range(3)]
                vt = [sb(nc, sA, "vt%d_%d" % (l, p), [128, 2080], BF16) for p in range(3)]
                atT = [sb(nc, sA, "atT%d_%d" % (l, p), [128, 4096], BF16) for p in range(2)]
                oT = [sb(nc, sA, "oT%d_%d" % (l, g), [128, TC], BF16) for g in range(4)]
                vraw = [sb(nc, sA, "vraw%d_%d" % (l, g), [128, TC], F32) for g in range(4)]
                qraw = [sb(nc, sA, "qraw%d_%d" % (l, g), [128, TC], BF16) for g in range(4)]
                kraw = [sb(nc, sA, "kraw%d_%d" % (l, g), [128, TC], BF16) for g in range(4)]
                tbl = [[sb(nc, sA, "tbl%d_%d_%d" % (l, p, i), [128, TC], BF16) for i in range(4)] for p in range(2)]
                rtmp = [sb(nc, sA, "rtmp%d_%d" % (l, i), [128, TC], BF16) for i in range(2)]
                SM = [sb(nc, sA, "SM%d_%d" % (l, g), [128, TC + 1], F32) for g in range(DT)]
                statm = sb(nc, sA, "statm%d" % l, [1, TC], F32)
                stati = sb(nc, sA, "stati%d" % l, [1, TC], F32)
                sqA = sb(nc, sA, "sqA%d" % l, [128, TC], F32)
                sqAb = sb(nc, sA, "sqAb%d" % l, [128, TC], BF16)
                nrm = sb(nc, sA, "nrm%d" % l, [8, 2 * TC], F32)
                nrmb = sb(nc, sA, "nrmb%d" % l, [8, 2 * TC], BF16)
                sums0 = sb(nc, sA, "sums0%d" % l, [1, 8 * TC], F32)
                sums = sb(nc, sA, "sums%d" % l, [8, TC], F32)
                sumsb = sb(nc, sA, "sumsb%d" % l, [8, TC], BF16)
                identd = sb(nc, sA, "identd%d" % l, [128, 64], F32)
                qs8 = sb(nc, sA, "qs8_%d" % l, [128, 1], F32)
                ks1 = sb(nc, sA, "ks1_%d" % l, [128, 1], F32)
                ps_sim = [ps_(nc, sA, "psA_sim%d_%d" % (l, i), [128, 1024]) for i in range(2)]
                ps_mmA = [ps_(nc, sA, "psA_mm%d_%d" % (l, i), [128, TC]) for i in range(2)]
                ps_on = ps_(nc, sA, "psA_on%d" % l, [128, 1024])

                P.dma("sp", lambda h: h.dma_start(out=identd[:, :], in_=d["identd"]),
                      writes=[buf("identd%d" % l)])
                P.dma("sp", lambda h: h.dma_start(out=qs8[:, :], in_=d["qs%d" % l]), writes=[buf("qs8_%d" % l)])
                P.dma("sp", lambda h: h.dma_start(out=ks1[:, :], in_=d["ks%d" % l]), writes=[buf("ks1_%d" % l)])
                P.rw("vector", lambda h: h.tensor_scalar(qs8[:, :], qs8[:, :], 8.0, None, ALU.mult),
                     writes=[buf("qs8_%d" % l)])
                for g in range(DT):
                    P.dma("pl", lambda h, g=g: h.dma_start(
                        out=wqb[g][:, :], in_=d["wqkv%d" % l][128 * g:128 * (g + 1), :]),
                        writes=[buf("wqb%d" % g)])
                for g in range(4):
                    P.dma("pl", lambda h, g=g: h.dma_start(
                        out=wlb[g][:, :], in_=d["wol%d" % l][128 * g:128 * (g + 1), :]),
                        writes=[buf("wlb%d" % g)])
                for p in range(2):
                    for jc in range(8):
                        for ic in range(4):
                            if ic <= jc <= ic + 4:
                                continue
                            P.rw("gpsimd", lambda h, p=p, jc=jc, ic=ic: h.memset(
                                atT[p][:, jc * 512 + ic * 128: jc * 512 + (ic + 1) * 128], 0.0),
                                writes=[buf("atT%d" % p)])
                for p in range(3):
                    P.rw("gpsimd", lambda h, p=p: h.memset(
                        bass.AP(vt[p], 64, [[vt[p].ap().tensor.shape[1], 128], [65, 32], [1, 1]]), 1.0),
                        writes=[buf("vt%d" % p)])

                def produce_a(c):
                    t0 = c * TC
                    par = c % 2
                    xb = xin[par]
                    for g in range(DT):
                        if c == 0:
                            P.rw("gpsimd", lambda h, g=g, xb=xb: h.memset(xb[g][:, 0:1], 0.0),
                                 writes=[buf("xin%d_%d" % (par, g))])
                            P.dma("sp", lambda h, g=g, xb=xb: h.dma_start(
                                out=xb[g][:, 1:TC + 1], in_=d[src_xf][128 * g:128 * (g + 1), 0:TC]),
                                writes=[buf("xin%d_%d" % (par, g))])
                        else:
                            P.dma("sp", lambda h, g=g, t0=t0, xb=xb: h.dma_start(
                                out=xb[g][:, :], in_=d[src_xf][128 * g:128 * (g + 1), t0 - 1:t0 + TC]),
                                writes=[buf("xin%d_%d" % (par, g))])
                    for i, tn in enumerate(("c1", "s1", "c2", "s2")):
                        P.dma("pl", lambda h, i=i, tn=tn, t0=t0, par=par: h.dma_start(
                            out=tbl[par][i][:, :], in_=d[tn][:, t0:t0 + TC]),
                            writes=[buf("tbl%d_%d" % (par, i))])
                    for g in range(DT):
                        sl = (1, TC + 1) if g < 4 else (0, TC)
                        P.rw("vector", lambda h, g=g, sl=sl, xb=xb: h.tensor_copy(
                            hbf[g][:, 0:TC], xb[g][:, sl[0]:sl[1]]),
                            reads=[buf("xin%d_%d" % (par, g))], writes=[buf("hbf%d" % g)])
                    for et in range(12):
                        pm = ps_mmA[et % 2]
                        bpm = buf("psA_mm%d" % (et % 2))
                        def qkv_mm(h, et=et, pm=pm):
                            ins = None
                            for gg in range(DT):
                                ins = h.matmul(pm[:, :], wqb[gg][:, 128 * et:128 * (et + 1)],
                                               hbf[gg][:, 0:TC], start=(gg == 0), stop=(gg == DT - 1))
                            return ins
                        P.rw("tensor", qkv_mm,
                             reads=[buf("hbf%d" % gg) for gg in range(DT)] + [buf("wqb%d" % gg) for gg in range(DT)],
                             writes=[bpm])

                        if et < 4:
                            P.rw("scalar", lambda h, pm=pm: h.activation(sqAb[:, :], pm[:, :], AF.Square),
                                 reads=[bpm], writes=[buf("sqA")])
                            P.rw("tensor", lambda h, et=et: h.matmul(
                                ps_on[0:8, 0:TC], og4b[:, 8 * et:8 * (et + 1)], sqAb[:, :],
                                start=(et == 0), stop=(et == 3)),
                                reads=[buf("sqA"), buf("og4b")], writes=[buf("ps_on1")])
                            P.rw("vector", lambda h, et=et, pm=pm: h.tensor_copy(qraw[et][:, :], pm[:, :]),
                                 reads=[bpm], writes=[buf("qraw%d" % et)])
                        elif et < 8:
                            P.rw("scalar", lambda h, pm=pm: h.activation(sqAb[:, :], pm[:, :], AF.Square),
                                 reads=[bpm], writes=[buf("sqA")])
                            P.rw("tensor", lambda h, et=et: h.matmul(
                                ps_on[0:8, TC:2 * TC], og4b[:, 8 * (et - 4):8 * (et - 3)], sqAb[:, :],
                                start=(et == 4), stop=(et == 7)),
                                reads=[buf("sqA"), buf("og4b")], writes=[buf("ps_on2")])
                            P.rw("vector", lambda h, et=et, pm=pm: h.tensor_copy(kraw[et - 4][:, :], pm[:, :]),
                                 reads=[bpm], writes=[buf("kraw%d" % (et - 4))])
                        else:
                            P.rw("scalar", lambda h, et=et, pm=pm: h.copy(vraw[et - 8][:, :], pm[:, :]),
                                 reads=[bpm], writes=[buf("vraw%d" % (et - 8))])
                    for coff, bn in ((0, "ps_on1"), (TC, "ps_on2")):
                        P.rw("scalar", lambda h, coff=coff: h.activation(
                            nrm[0:8, coff:coff + TC], ps_on[0:8, coff:coff + TC], AF.Ln),
                            reads=[buf(bn)], writes=[buf("nrm")])
                        P.rw("scalar", lambda h, coff=coff: h.activation(
                            nrm[0:8, coff:coff + TC], nrm[0:8, coff:coff + TC], AF.Exp, scale=-0.5),
                            writes=[buf("nrm")])
                        P.rw("vector", lambda h, coff=coff: h.tensor_scalar_min(
                            nrm[0:8, coff:coff + TC], nrm[0:8, coff:coff + TC], 1e12),
                            writes=[buf("nrm")])
                        P.rw("vector", lambda h, coff=coff: h.tensor_copy(
                            nrmb[0:8, coff:coff + TC], nrm[0:8, coff:coff + TC]),
                            reads=[buf("nrm")], writes=[buf("nrmb")])
                def produce_b(c):
                    t0 = c * TC
                    par = c % 2
                    par3 = c % 3
                    xb = xin[par]
                    for et in range(4):
                        for (raw, wbufs, scv, scb, tci, tsi, nbase) in (
                            (qraw, qw, qs8, "qs8_%d" % l, 0, 1, 0),
                            (kraw, kw, ks1, "ks1_%d" % l, 2, 3, TC),
                        ):
                            pm = ps_mmA[0]
                            bpm = buf("psA_mm0")
                            P.rw("tensor", lambda h, et=et, nbase=nbase, pm=pm: h.matmul(
                                pm[:, :], bg4[:, 128 * et:128 * (et + 1)], nrmb[0:8, nbase:nbase + TC],
                                start=True, stop=True),
                                reads=[buf("nrmb"), buf("bg4")], writes=[bpm])
                            P.rw("vector", lambda h, et=et, raw=raw, scv=scv, pm=pm: h.scalar_tensor_tensor(
                                raw[et][:, :], raw[et][:, :], scv[:, :], pm[:, :], ALU.mult, ALU.mult),
                                reads=[buf(scb), bpm],
                                writes=[buf("%s%d" % ("qraw" if raw is qraw else "kraw", et))])
                            pm2 = ps_mmA[1]
                            bpm2 = buf("psA_mm1")
                            P.rw("tensor", lambda h, et=et, raw=raw, pm2=pm2: h.matmul(
                                pm2[:, :], perm[:, :], raw[et][:, :], start=True, stop=True),
                                reads=[buf("%s%d" % ("qraw" if raw is qraw else "kraw", et)), buf("perm")],
                                writes=[bpm2])
                            P.rw("gpsimd", lambda h, et=et, raw=raw, tci=tci, par=par: h.tensor_mul(
                                rtmp[0][:, :], raw[et][:, :], tbl[par][tci][:, :]),
                                reads=[buf("%s%d" % ("qraw" if raw is qraw else "kraw", et)),
                                       buf("tbl%d_%d" % (par, tci))],
                                writes=[buf("rtmp0")])
                            P.rw("vector", lambda h, tsi=tsi, par=par, pm2=pm2: h.tensor_mul(
                                rtmp[1][:, :], pm2[:, :], tbl[par][tsi][:, :]),
                                reads=[bpm2, buf("tbl%d_%d" % (par, tsi))], writes=[buf("rtmp1")])
                            wi = par if wbufs is qw else par3
                            P.rw("gpsimd", lambda h, et=et, wbufs=wbufs, wi=wi: h.tensor_add(
                                wbufs[wi][et][:, :], rtmp[0][:, :], rtmp[1][:, :]),
                                reads=[buf("rtmp0"), buf("rtmp1")],
                                writes=[buf("%sw%d_%d" % ("q" if wbufs is qw else "k", wi, et))])
                    for hh in range(8):
                        vcol = (hh % 2) * TC
                        vbuf = "ps_on%d" % (1 + hh % 2)
                        def tr_f(h, hh=hh, par=par, vcol=vcol):
                            ins = None
                            for sub in range(4):
                                ins = h.transpose(ps_on[:, vcol + 64 * sub:vcol + 64 * (sub + 1)],
                                                  vraw[hh // 2][64 * (hh % 2):64 * (hh % 2) + 64,
                                                                128 * sub:128 * (sub + 1)],
                                                  identd[64 * (hh % 2):64 * (hh % 2) + 64, :])
                            return ins
                        P.rw("tensor", tr_f,
                             reads=[buf("vraw%d" % (hh // 2)), buf("identd%d" % l)],
                             writes=[buf(vbuf)])
                        P.rw("vector", lambda h, hh=hh, par3=par3, vcol=vcol: h.tensor_copy(
                            bass.AP(vt[par3], hh * 260, [[vt[par3].ap().tensor.shape[1], 128], [65, 4], [1, 64]]),
                            ps_on[:, vcol:vcol + 256]),
                            reads=[buf(vbuf)], writes=[buf("vt%d" % par3)])
                def consume_heads(c, hrng):
                    t0 = c * TC
                    par = c % 2
                    par3 = c % 3
                    pv3 = (c - 1) % 3
                    xb = xin[par]
                    for hh in hrng:
                        tile = hh // 2
                        rbase = 64 * (hh % 2)
                        for pi, pair in enumerate(((0, 1), (2, 3))):
                            plo, phi = pair
                            jlo = max(4, plo) if c == 0 else plo
                            plan_mm = []
                            for jc in range(jlo, phi + 4 + 1):
                                for ic in pair:
                                    if ic <= jc <= ic + 4 and not (c == 0 and jc < 4):
                                        plan_mm.append(("qk", jc, ic, jc - ic))
                            for ic in pair:
                                if c > 0:
                                    plan_mm.append(("ml", ic, ic, 0))
                                plan_mm.append(("mr", ic + 4, ic, 4))
                            seen = set()
                            flags = []
                            for (kind, jc, ic, r) in plan_mm:
                                bk = (ic, 0 if r < 4 else 1)
                                flags.append(bk not in seen)
                                seen.add(bk)
                            last_of = {}
                            for idx, (kind, jc, ic, r) in enumerate(plan_mm):
                                last_of[(ic, 0 if r < 4 else 1)] = idx
                            lastf = [idx in last_of.values() for idx in range(len(plan_mm))]
                            def qk_f(h, plan_mm=plan_mm, flags=flags, lastf=lastf,
                                     tile=tile, rbase=rbase, par=par, par3=par3, pv3=pv3):
                                ins = None
                                for idx, (kind, jc, ic, r) in enumerate(plan_mm):
                                    if kind == "qk":
                                        kwsrc = kw[par3] if jc >= 4 else kw[pv3]
                                        kcol = (jc % 4) * 128
                                        ins = h.matmul(
                                            ps_sim[ic % 2][:, r * 128:(r + 1) * 128],
                                            kwsrc[tile][rbase:rbase + 64, kcol:kcol + 128],
                                            qw[par][tile][rbase:rbase + 64, ic * 128:(ic + 1) * 128],
                                            start=flags[idx], stop=lastf[idx])
                                    else:
                                        msk = maskl if kind == "ml" else maskr
                                        ins = h.matmul(
                                            ps_sim[ic % 2][:, r * 128:(r + 1) * 128],
                                            ident[:, :], msk[:, :],
                                            start=flags[idx], stop=lastf[idx])
                                return ins
                            reads_qk = [buf("qw%d_%d" % (par, tile)), buf("maskl"), buf("maskr"), buf("ident")]
                            if jlo < 4:
                                reads_qk.append(buf("kw%d_%d" % (pv3, tile)))
                            if phi + 4 >= 4:
                                reads_qk.append(buf("kw%d_%d" % (par3, tile)))
                            P.rw("tensor", qk_f, reads=reads_qk,
                                 writes=[buf("psA_sim0"), buf("psA_sim1")])
                            for ic in pair:
                                jlo_ic = max(max(4, ic) if c == 0 else ic, ic)
                                r0 = jlo_ic - ic
                                nr = 5 - r0
                                def exp_f(h, ic=ic, r0=r0, nr=nr, par=par):
                                    return h.activation(
                                        bass.AP(atT[par], (ic + r0) * 512 + ic * 128,
                                                [[atT[par].ap().tensor.shape[1], 128], [512, nr], [1, 128]]),
                                        ps_sim[ic % 2][:, r0 * 128:640],
                                        AF.Exp, bias=cpool[:, 0:1])
                                P.rw("scalar", exp_f,
                                     reads=[buf("psA_sim%d" % (ic % 2)), buf("cpool")],
                                     writes=[buf("atT%d_%d" % (par, ic))])
                        jset = range(4, 8) if c == 0 else range(8)
                        avcol = (hh % 2) * TC
                        avbuf = "ps_on%d" % (1 + hh % 2)
                        def av_f(h, hh=hh, par=par, par3=par3, pv3=pv3, jset=tuple(jset), avcol=avcol):
                            ins = None
                            for n_i, jc in enumerate(jset):
                                vsrc = vt[par3] if jc >= 4 else vt[pv3]
                                col = hh * 260 + (jc % 4) * 65
                                lo = max(0, jc - 4)
                                hi = min(3, jc)
                                ins = h.matmul(ps_on[0:65, avcol + lo * 128:avcol + (hi + 1) * 128],
                                               vsrc[:, col:col + 65],
                                               atT[par][:, jc * 512 + lo * 128:jc * 512 + (hi + 1) * 128],
                                               start=(n_i == 0), stop=(n_i == len(jset) - 1))
                            return ins
                        reads_av = [buf("atT%d_%d" % (par, ic)) for ic in range(4)]
                        reads_av.append(buf("atT%d" % par))
                        reads_av.append(buf("vt%d" % par3))
                        if c > 0:
                            reads_av.append(buf("vt%d" % pv3))
                        P.rw("tensor", av_f, reads=reads_av, writes=[buf(avbuf)])
                        P.rw("vector", lambda h, hh=hh, avcol=avcol: h.tensor_copy(
                            oT[hh // 2][64 * (hh % 2):64 * (hh % 2) + 64, :], ps_on[0:64, avcol:avcol + TC]),
                            reads=[buf(avbuf)], writes=[buf("oT%d" % (hh // 2))])
                        P.rw("vector", lambda h, hh=hh, avcol=avcol: h.tensor_copy(
                            sums0[0:1, hh * TC:(hh + 1) * TC], ps_on[64:65, avcol:avcol + TC]),
                            reads=[buf(avbuf)], writes=[buf("sums0")])
                def consume_tail(c):
                    t0 = c * TC
                    par = c % 2
                    xb = xin[par]
                    P.dma("pl", lambda h: h.dma_start(out=d["sc_sums"], in_=sums0[0:1, :]),
                          reads=[buf("sums0")], writes=[buf("sc_sums")])
                    P.dma("pl", lambda h: h.dma_start(out=sums[:, :], in_=d["sc_sums"].rearrange("(h t) -> h t", h=8)),
                          reads=[buf("sc_sums")], writes=[buf("sums")])
                    P.rw("scalar", lambda h: h.activation(sums[:, :], sums[:, :], AF.Ln),
                         writes=[buf("sums")])
                    P.rw("scalar", lambda h: h.activation(sumsb[:, :], sums[:, :], AF.Exp, scale=-1.0),
                         reads=[buf("sums")], writes=[buf("sumsb")])
                    for g in range(4):
                        pm = ps_mmA[g % 2]
                        bpm = buf("psA_mm%d" % (g % 2))
                        P.rw("tensor", lambda h, g=g, pm=pm: h.matmul(
                            pm[:, :], bg4[:, 128 * g:128 * (g + 1)], sumsb[:, :], start=True, stop=True),
                            reads=[buf("sumsb"), buf("bg4")], writes=[bpm])
                        P.rw("vector", lambda h, g=g, pm=pm: h.tensor_mul(oT[g][:, :], oT[g][:, :], pm[:, :]),
                             reads=[bpm], writes=[buf("oT%d" % g)])
                    for et in range(DT):
                        if c == 0:
                            P.rw("gpsimd", lambda h, et=et: h.memset(SM[et][:, 0:1], 0.0),
                                 writes=[buf("SM%d" % et)])
                        else:
                            P.rw("vector", lambda h, et=et: h.tensor_copy(SM[et][:, 0:1], SM[et][:, TC:TC + 1]),
                                 writes=[buf("SM%d" % et)])
                        pm = ps_mmA[et % 2]
                        bpm = buf("psA_mm%d" % (et % 2))
                        def wo_mm(h, et=et, pm=pm):
                            ins = None
                            for gg in range(4):
                                ins = h.matmul(pm[:, :], wlb[gg][:, 128 * et:128 * (et + 1)],
                                               oT[gg][:, :], start=(gg == 0), stop=(gg == 3))
                            return ins
                        P.rw("tensor", wo_mm,
                             reads=[buf("oT%d" % gg) for gg in range(4)] + [buf("wlb%d" % gg) for gg in range(4)],
                             writes=[bpm])
                        P.rw("vector", lambda h, et=et, pm=pm, xb=xb: h.tensor_add(
                            SM[et][:, 1:TC + 1], xb[et][:, 1:TC + 1], pm[:, :]),
                            reads=[buf("xin%d_%d" % (par, et)), bpm], writes=[buf("SM%d" % et)])
                        P.dma("pl", lambda h, et=et, t0=t0: h.dma_start(
                            out=d["sc_xm"][128 * et:128 * (et + 1), t0:t0 + TC], in_=SM[et][:, 1:TC + 1]),
                            reads=[buf("SM%d" % et)])
                    for et in range(DT):
                        sl = (1, TC + 1) if et < 4 else (0, TC)
                        P.rw("scalar", lambda h, et=et, sl=sl: h.activation(
                            sqAb[:, :], SM[et][:, sl[0]:sl[1]], AF.Square),
                            reads=[buf("SM%d" % et)], writes=[buf("sqA")])
                        P.rw("tensor", lambda h, et=et, sl=sl: h.matmul(
                            ps_sim[0][0:1, 0:TC], onesf[:, :], SM[et][:, sl[0]:sl[1]],
                            start=(et == 0), stop=(et == DT - 1)),
                            reads=[buf("SM%d" % et), buf("onesf")], writes=[buf("psA_sim0")])
                        P.rw("tensor", lambda h, et=et: h.matmul(
                            ps_sim[0][0:1, TC:2 * TC], onesb1[:, :], sqAb[:, :],
                            start=(et == 0), stop=(et == DT - 1)),
                            reads=[buf("sqA"), buf("onesb1")], writes=[buf("psA_sim0")])
                    P.rw("vector", lambda h: h.tensor_scalar(statm[:, :], ps_sim[0][0:1, 0:TC], 1.0 / 1024, None, ALU.mult),
                         reads=[buf("psA_sim0")], writes=[buf("statm")])
                    P.rw("vector", lambda h: h.tensor_scalar(stati[:, :], ps_sim[0][0:1, TC:2 * TC], 1.0 / 1024, None, ALU.mult),
                         reads=[buf("psA_sim0")], writes=[buf("stati")])
                    P.rw("vector", lambda h: h.tensor_mul(sqA[0:1, 0:TC], statm[:, :], statm[:, :]),
                         reads=[buf("statm")], writes=[buf("sqA")])
                    P.rw("vector", lambda h: h.tensor_sub(stati[:, :], stati[:, :], sqA[0:1, 0:TC]),
                         reads=[buf("sqA")], writes=[buf("stati")])
                    P.rw("scalar", lambda h: h.activation(stati[:, :], stati[:, :], AF.Ln, bias=cpool[0:1, 1:2]),
                         reads=[buf("cpool")], writes=[buf("stati")])
                    P.rw("scalar", lambda h: h.activation(stati[:, :], stati[:, :], AF.Exp, scale=-0.5),
                         writes=[buf("stati")])
                    P.dma("pl", lambda h, t0=t0: h.dma_start(
                        out=d["sc_stats"][0:1, t0:t0 + TC], in_=statm[:, :]),
                        reads=[buf("statm")])
                    P.dma("pl", lambda h, t0=t0: h.dma_start(
                        out=d["sc_stats"][1:2, t0:t0 + TC], in_=stati[:, :]),
                        reads=[buf("stati")])
                produce_a(0)
                produce_b(0)
                for c in range(NCH):
                    if c + 1 < NCH:
                        produce_a(c + 1)
                    consume_heads(c, range(0, 4))
                    if c + 1 < NCH:
                        produce_b(c + 1)
                    consume_heads(c, range(4, 8))
                    consume_tail(c)

            if PH < 4 + 2 * l:
                raise StopIteration
            P.floor = P.barrier()
            TC2 = 256
            NJ = 22
            JW = [128] * 21 + [42]
            with ExitStack() as sM:
                w1b = [sb(nc, sM, "w1b%d_%d" % (l, g), [128, 5460], BF16) for g in range(DT)]
                w2b = [sb(nc, sM, "w2b%d_%d" % (l, j), [JW[j], 1024], BF16) for j in range(NJ)]
                smallv2 = {}
                for v in ("lg", "lb", "b2"):
                    smallv2[v] = sb(nc, sM, "sv2_%s_%d" % (v, l), [128, 8], F32)
                    P.dma("sp", lambda h, v=v: h.dma_start(
                        out=smallv2[v][:, :], in_=d["%sl%d" % (v, l) if v != "b2" else "b2l%d" % l]
                        if False else d[{"lg": "lg%d", "lb": "lb%d", "b2": "b2l%d"}[v] % l]),
                        writes=[buf("sv2_" + v)])
                b1a = sb(nc, sM, "b1a_%d" % l, [128, NJ], F32)
                b1g = sb(nc, sM, "b1g_%d" % l, [128, NJ], F32)
                fsc = [sb(nc, sM, "fsc%d_%d" % (l, i), [128, TC2], BF16) for i in range(3)]
                glx = [sb(nc, sM, "glx_%d_%d" % (l, i), [128, TC2], BF16) for i in range(3)]
                sqM = sb(nc, sM, "sqM_%d" % l, [128, TC2], F32)
                rowm = [sb(nc, sM, "rowm_%d_%d" % (l, i), [1, 2 * TC2], F32) for i in range(2)]
                xz = [sb(nc, sM, "xz%d_%d" % (l, g), [128, TC2], F32) for g in range(4)]
                xz2 = [[sb(nc, sM, "xz2%d_%d_%d" % (l, p, g), [128, TC2], F32) for g in range(DT)] for p in range(2)] if l == 1 else None
                ps_acc = [ps_(nc, sM, "psM_acc%d_%d" % (l, i), [128, 512]) for i in range(4)]
                ps_ag = [ps_(nc, sM, "psM_ag%d_%d" % (l, i), [128, 512]) for i in range(2)]
                ps_aux = ps_(nc, sM, "psM_aux%d" % l, [128, 1024])
                for g in range(DT):
                    P.dma("pl", lambda h, g=g: h.dma_start(
                        out=w1b[g][:, :], in_=d["w1l%d" % l][128 * g:128 * (g + 1), :]),
                        writes=[buf("w1b%d" % g)])
                for j in range(NJ):
                    P.dma("pl", lambda h, j=j: h.dma_start(
                        out=w2b[j][:, :], in_=d["w2l%d" % l][128 * j:128 * j + JW[j], :]),
                        writes=[buf("w2b%d" % j)])
                P.dma("sp", lambda h: h.dma_start(out=b1a[:, :], in_=d["b1a%d" % l]), writes=[buf("b1a")])
                P.dma("sp", lambda h: h.dma_start(out=b1g[:, :], in_=d["b1g%d" % l]), writes=[buf("b1g")])
                NCH2 = T // TC2
                def prep(c2):
                    t0 = c2 * TC2
                    par = c2 % 2
                    xb = xin[par]
                    co = par * TC2
                    for g in range(DT):
                        if c2 == 0:
                            P.rw("gpsimd", lambda h, g=g, xb=xb: h.memset(xb[g][:, 0:1], 0.0),
                                 writes=[buf("xin%d_%d" % (par, g))])
                            P.dma("sp", lambda h, g=g, xb=xb: h.dma_start(
                                out=xb[g][:, 1:TC2 + 1], in_=d["sc_xm"][128 * g:128 * (g + 1), 0:TC2]),
                                writes=[buf("xin%d_%d" % (par, g))])
                        else:
                            P.dma("sp", lambda h, g=g, t0=t0, xb=xb: h.dma_start(
                                out=xb[g][:, 0:TC2 + 1], in_=d["sc_xm"][128 * g:128 * (g + 1), t0 - 1:t0 + TC2]),
                                writes=[buf("xin%d_%d" % (par, g))])
                        if l == 1:
                            P.dma("sp", lambda h, g=g, t0=t0, par=par: h.dma_start(
                                out=xz2[par][g][:, :], in_=d["sc_x1"][128 * g:128 * (g + 1), t0:t0 + TC2]),
                                writes=[buf("xz2_%d_%d" % (par, g))])
                    P.dma("sp", lambda h, t0=t0, par=par: h.dma_start(
                        out=rowm[par][:, :], in_=d["sc_stats"][:, t0:t0 + TC2]),
                        writes=[buf("rowm%d" % par)])
                    aco = par * 2 * TC2
                    P.rw("tensor", lambda h, par=par, aco=aco: h.matmul(
                        ps_aux[:, aco:aco + TC2], bonef[:, :], rowm[par][0:1, 0:TC2], start=True, stop=False),
                        reads=[buf("rowm%d" % par), buf("bonef")], writes=[buf("psM_aux%d" % par)])
                    P.rw("tensor", lambda h, par=par, aco=aco: h.matmul(
                        ps_aux[:, aco + TC2:aco + 2 * TC2], bonef[:, :], rowm[par][0:1, TC2:2 * TC2], start=False, stop=True),
                        reads=[buf("rowm%d" % par), buf("bonef")], writes=[buf("psM_aux%d" % par)])
                    for g in range(DT):
                        sl = (1, TC2 + 1) if g < 4 else (0, TC2)
                        gx = glx[g % 2]
                        gbuf = buf("glx%d" % (g % 2))
                        P.rw("vector", lambda h, g=g, sl=sl, xb=xb, gx=gx, aco=aco, par=par: h.tensor_sub(
                            gx[:, :], xb[g][:, sl[0]:sl[1]], ps_aux[:, aco:aco + TC2]),
                            reads=[buf("xin%d_%d" % (par, g)), buf("psM_aux%d" % par)], writes=[gbuf])
                        P.rw("vector", lambda h, g=g, gx=gx, aco=aco, par=par: h.tensor_mul(
                            gx[:, :], gx[:, :], ps_aux[:, aco + TC2:aco + 2 * TC2]),
                            reads=[buf("psM_aux%d" % par)], writes=[gbuf])
                        P.rw("vector", lambda h, g=g, gx=gx, co=co, par=par: h.tensor_scalar(
                            hbf[g][:, co:co + TC2], gx[:, :], smallv2["lg"][:, g:g + 1], smallv2["lb"][:, g:g + 1],
                            ALU.mult, ALU.add),
                            reads=[gbuf, buf("sv2_lg"), buf("sv2_lb")], writes=[buf("hbf_%d_%d" % (par, g))])
                def emit_ag(c2, j):
                    par = c2 % 2
                    co = par * TC2
                    wj = JW[j]
                    pag = ps_ag[j % 2]
                    pagbuf = buf("psM_ag%d" % (j % 2))
                    def ag_mm(h, j=j, wj=wj, pag=pag, co=co):
                        ins = None
                        for gg in range(DT):
                            ins = h.matmul(pag[0:wj, 0:TC2], w1b[gg][:, 128 * j:128 * j + wj],
                                           hbf[gg][:, co:co + TC2], start=(gg == 0), stop=False)
                        for gg in range(DT):
                            ins = h.matmul(pag[0:wj, TC2:2 * TC2], w1b[gg][:, 2730 + 128 * j:2730 + 128 * j + wj],
                                           hbf[gg][:, co:co + TC2], start=False, stop=(gg == DT - 1))
                        return ins
                    P.rw("tensor", ag_mm,
                         reads=[buf("hbf_%d_%d" % (par, gg)) for gg in range(DT)] + [buf("w1b%d" % gg) for gg in range(DT)],
                         writes=[pagbuf])
                    fb_ = fsc[j % 3]
                    bfb = buf("fsc%d" % (j % 3))
                    gx = glx[j % 3]
                    gbuf = buf("glx%d" % (j % 3))
                    P.rw("scalar", lambda h, j=j, wj=wj, pag=pag, gx=gx: h.activation(
                        gx[0:wj, :], pag[0:wj, TC2:2 * TC2], AF.Gelu, bias=b1g[0:wj, j:j + 1]),
                        reads=[pagbuf, buf("b1g")], writes=[gbuf])
                    P.rw("vector", lambda h, j=j, wj=wj, fb_=fb_, pag=pag, gx=gx: h.scalar_tensor_tensor(
                        fb_[0:wj, :], pag[0:wj, 0:TC2], b1a[0:wj, j:j + 1], gx[0:wj, :],
                        ALU.add, ALU.mult),
                        reads=[pagbuf, buf("b1a"), gbuf], writes=[bfb])
                def emit_w2(j):
                    wj = JW[j]
                    fb_ = fsc[j % 3]
                    bfb = buf("fsc%d" % (j % 3))
                    def w2_mm(h, j=j, wj=wj, fb_=fb_):
                        ins = None
                        for et in range(DT):
                            ins = h.matmul(ps_acc[et // 2][:, (et % 2) * TC2:(et % 2) * TC2 + TC2],
                                           w2b[j][0:wj, 128 * et:128 * (et + 1)], fb_[0:wj, :],
                                           start=(j == 0 and et % 2 == 0), stop=(j == NJ - 1 and et % 2 == 1))
                        return ins
                    P.rw("tensor", w2_mm,
                         reads=[bfb, buf("w2b%d" % j)],
                         writes=[buf("psM_acc0"), buf("psM_acc1"), buf("psM_acc2"), buf("psM_acc3")])
                def evac(c2):
                    t0 = c2 * TC2
                    par = c2 % 2
                    xb = xin[par]
                    for et in range(DT):
                        xe = xz[et % 4]
                        xbuf = buf("xz%d" % (et % 4))
                        P.rw("vector", lambda h, et=et, xb=xb, xe=xe: h.scalar_tensor_tensor(
                            xe[:, :], ps_acc[et // 2][:, (et % 2) * TC2:(et % 2) * TC2 + TC2],
                            smallv2["b2"][:, et:et + 1], xb[et][:, 1:TC2 + 1], ALU.add, ALU.add),
                            reads=[buf("psM_acc%d" % (et // 2)), buf("sv2_b2"), buf("xin%d_%d" % (par, et))],
                            writes=[xbuf])
                        if l == 0:
                            P.dma("pl", lambda h, et=et, t0=t0, xe=xe: h.dma_start(
                                out=d["sc_xf1"][128 * et:128 * (et + 1), t0:t0 + TC2], in_=xe[:, :]),
                                reads=[xbuf])
                        else:
                            P.rw("vector", lambda h, et=et, par=par, xe=xe: h.tensor_add(
                                xe[:, :], xe[:, :], xz2[par][et][:, :]),
                                reads=[buf("xz2_%d_%d" % (par, et))], writes=[xbuf])
                            P.dma("pl", lambda h, et=et, t0=t0, xe=xe: h.dma_start(
                                out=d["outT"][128 * et:128 * (et + 1), t0:t0 + TC2], in_=xe[:, :]),
                                reads=[xbuf])
                prep(0)
                for c2 in range(NCH2):
                    emit_ag(c2, 0)
                    emit_ag(c2, 1)
                    for j in range(2, NJ):
                        emit_ag(c2, j)
                        if j == 4 and c2 + 1 < NCH2:
                            prep(c2 + 1)
                        emit_w2(j - 2)
                    emit_w2(NJ - 2)
                    emit_w2(NJ - 1)
                    evac(c2)

      except StopIteration:
        pass
      fin = P.barrier()
      P._emit("sync", lambda h: h.nop(), fin, None, 0)
      P.run()
    return nc, d


def stage_inputs(ins, b):
    m = {}
    m["xT"] = np.ascontiguousarray(ins["L0_attn_x"][b].T.astype(np.float32))
    m["wrT"] = np.ascontiguousarray(ins["L0_attn_W_r"].T.astype(np.float32))
    m["wkT"] = np.ascontiguousarray(ins["L0_attn_W_k"].T.astype(np.float32))
    m["wvT"] = np.ascontiguousarray(ins["L0_attn_W_v"].T.astype(np.float32))
    m["woT"] = np.ascontiguousarray(ins["L0_attn_W_o"].T.astype(np.float32))
    for k, src in (("tmr", "time_mix_r"), ("tmk", "time_mix_k"), ("tmv", "time_mix_v")):
        m[k] = np.ascontiguousarray(ins["L0_attn_" + src].astype(np.float32).reshape(8, 128).T)
    m["decay"] = np.ascontiguousarray(ins["L0_attn_time_decay"].astype(np.float32).reshape(-1).reshape(8, 128).T)
    m["first"] = np.ascontiguousarray(ins["L0_attn_time_first"].astype(np.float32).reshape(-1).reshape(8, 128).T)
    m["gnw"] = np.ascontiguousarray(ins["L0_attn_gn_weight"].astype(np.float32).reshape(8, 128).T)
    m["gnb"] = np.ascontiguousarray(ins["L0_attn_gn_bias"].astype(np.float32).reshape(8, 128).T)
    import ml_dtypes
    for l in range(2):
        m["wqkv%d" % l] = np.ascontiguousarray(ins["L0_ffn_wqkv"][l].astype(np.float32))
        m["wol%d" % l] = np.ascontiguousarray(ins["L0_ffn_wo"][l].astype(np.float32))
        m["w1l%d" % l] = np.ascontiguousarray(ins["L0_ffn_w1"][l].astype(np.float32))
        m["w2l%d" % l] = np.ascontiguousarray(ins["L0_ffn_w2"][l].astype(np.float32))
        m["qs%d" % l] = np.ascontiguousarray(np.tile(ins["L0_ffn_q_scale"][l].astype(np.float32), 2)[:, None])
        m["ks%d" % l] = np.ascontiguousarray(np.tile(ins["L0_ffn_k_scale"][l].astype(np.float32), 2)[:, None])
        m["lg%d" % l] = np.ascontiguousarray(ins["L0_ffn_ln_g"][l].astype(np.float32).reshape(8, 128).T)
        m["lb%d" % l] = np.ascontiguousarray(ins["L0_ffn_ln_b"][l].astype(np.float32).reshape(8, 128).T)
        m["b1l%d" % l] = np.ascontiguousarray(ins["L0_ffn_b1"][l].astype(np.float32)[:, None])
        _b1 = ins["L0_ffn_b1"][l].astype(np.float32)
        _pa = np.zeros(22 * 128, np.float32); _pa[:2730] = _b1[:2730]
        _pg = np.zeros(22 * 128, np.float32); _pg[:2730] = _b1[2730:]
        m["b1a%d" % l] = np.ascontiguousarray(_pa.reshape(22, 128).T)
        m["b1g%d" % l] = np.ascontiguousarray(_pg.reshape(22, 128).T)
        m["b2l%d" % l] = np.ascontiguousarray(ins["L0_ffn_b2"][l].astype(np.float32).reshape(8, 128).T)
    i128 = np.arange(128)
    m["identb"] = np.eye(128, dtype=ml_dtypes.bfloat16)
    m["identf"] = np.eye(128, dtype=np.float32)
    m["identd"] = np.ascontiguousarray(np.tile(np.eye(64, dtype=np.float32), (2, 1)))
    perm = np.zeros((128, 128), np.float32)
    for po in range(128):
        if po % 64 < 32:
            perm[po + 32, po] = -1.0
        else:
            perm[po - 32, po] = 1.0
    m["permb"] = perm.astype(ml_dtypes.bfloat16)
    m["onesf"] = np.ones((128, 1), np.float32)
    hv = np.zeros((128, 2), np.float32)
    hv[:64, 0] = 1.0
    hv[64:, 1] = 1.0
    m["halvesf"] = hv
    og = np.zeros((128, 32), np.float32)
    for g in range(4):
        for p in range(128):
            og[p, g * 8 + 2 * g + (1 if p >= 64 else 0)] = 1.0
    m["og4"] = og
    m["og4b"] = og.astype(ml_dtypes.bfloat16)
    m["onesb1"] = np.ones((128, 1), ml_dtypes.bfloat16)
    _p = np.arange(128)[:, None]; _f = np.arange(128)[None, :]
    m["maskl"] = np.where(_p >= _f, 0.0, -3e38).astype(ml_dtypes.bfloat16)
    m["maskr"] = np.where(_p <= _f, 0.0, -3e38).astype(ml_dtypes.bfloat16)
    bg = np.zeros((8, 512), np.float32)
    for g in range(4):
        bg[2 * g, g * 128: g * 128 + 64] = 1.0
        bg[2 * g + 1, g * 128 + 64: g * 128 + 128] = 1.0
    m["bg4"] = bg.astype(ml_dtypes.bfloat16)
    m["bonef"] = np.ones((1, 128), np.float32)
    h2 = np.zeros((2, 128), np.float32)
    h2[0, :64] = 1.0
    h2[1, 64:] = 1.0
    m["halves2"] = h2
    DIM_HEAD, WINDOW = 64, 512
    half_idx = np.arange(0, DIM_HEAD, 2, dtype=np.float64)
    inv_freq = 1.0 / 10000.0 ** (half_idx / DIM_HEAD)
    tt = np.arange(T, dtype=np.float64)
    freqs = np.concatenate([tt[:, None] * inv_freq[None, :]] * 2, axis=-1)
    base_ = (half_idx + 0.4 * DIM_HEAD) / (1.4 * DIM_HEAD)
    power = (tt - T // 2) / float(WINDOW // 2)
    scale = np.concatenate([base_[None, :] ** power[:, None]] * 2, axis=-1)
    cos, sin = np.cos(freqs), np.sin(freqs)
    m["c1"] = np.ascontiguousarray(np.tile((cos * scale).T, (2, 1)).astype(np.float32))
    m["s1"] = np.ascontiguousarray(np.tile((sin * scale).T, (2, 1)).astype(np.float32))
    m["c2"] = np.ascontiguousarray(np.tile((cos / scale).T, (2, 1)).astype(np.float32))
    m["s2"] = np.ascontiguousarray(np.tile((sin / scale).T, (2, 1)).astype(np.float32))
    return m




_CACHE = {}

def kernel(**inputs):
    if "nc" not in _CACHE:
        nc, d = build(debug=0)
        _CACHE["nc"] = nc
    nc = _CACHE["nc"]
    base = stage_inputs(inputs, 0)
    in_maps = []
    for b in range(8):
        m = dict(base)
        m["xT"] = np.ascontiguousarray(inputs["L0_attn_x"][b].T.astype(np.float32))
        in_maps.append(m)
    res = run_bass_kernel_spmd(nc, in_maps, core_ids=list(range(8)))
    out = np.stack([np.asarray(r["outT"]).T for r in res.results], axis=0)
    return np.ascontiguousarray(out.astype(np.float32))


# version 39
# speedup vs baseline: 1.2152x; 1.2058x over previous
import sys
if '/opt/trn_rl_repo' not in sys.path:
    sys.path.insert(0, '/opt/trn_rl_repo')
import numpy as np
from contextlib import ExitStack

import concourse.bass as bass
import concourse.mybir as mybir
from concourse.bass_utils import run_bass_kernel_spmd

F32 = mybir.dt.float32
BF16 = mybir.dt.bfloat16
AF = mybir.ActivationFunctionType
ALU = mybir.AluOpType

T = 4096
TC = 512
NCH = T // TC
DT = 8

ENG = ("sync", "vector", "scalar", "tensor", "gpsimd")
QUEUE = {"sp": ("sync", "dsp"), "pl": ("gpsimd", "dpl")}
ESEM = {"vector": "ve", "scalar": "se", "tensor": "pe", "gpsimd": "pl"}


class Buf:
    __slots__ = ("ev", "readers", "name", "serial")
    def __init__(self, name=""):
        self.ev = None
        self.readers = []
        self.name = name
        self.serial = name.startswith("ps")


NSLOT = 8

class Prog:
    def __init__(self, nc, stack):
        self.strict = True
        self.nc = nc
        self.stack = stack
        self.ops = {e: [] for e in ENG}
        self.sem = {}
        self.count = {}
        self.last_wait = {}
        for s in ("ve", "se", "pe", "pl"):
            self.sem[s] = stack.enter_context(nc.semaphore(s))
            self.count[s] = 0
        self.qslot = {}
        for q in ("dsp", "dpl"):
            for i in range(NSLOT):
                s = "%s%d" % (q, i)
                self.sem[s] = stack.enter_context(nc.semaphore(s))
                self.count[s] = 0
            self.qslot[q] = 0
        self.n_instr = 0

    def _emit(self, eng, fn, deps, inc_sem, inc_n):
        need = {}
        for s, c in list(deps) + getattr(self, "floor", []):
            if c is None or c <= 0:
                continue
            if s in ("ve", "se", "pe", "pl") and ESEM.get(eng) == s and not self.strict:
                continue
            if need.get(s, 0) < c:
                need[s] = c
        waits = []
        for s, c in need.items():
            k = (eng, s)
            if self.last_wait.get(k, 0) >= c:
                continue
            self.last_wait[k] = c
            waits.append((self.sem[s], c))
        if inc_sem is not None:
            self.count[inc_sem] += inc_n
            tgt = (self.sem[inc_sem], inc_n)
            ev = (inc_sem, self.count[inc_sem])
        else:
            tgt = None
            ev = None
        class _Rec:
            __slots__ = ("calls",)
            def __init__(self):
                self.calls = []
            def __getattr__(self, name):
                def m(*a, **k):
                    self.calls.append((name, a, k))
                    return None
                return m
        rec = _Rec()
        fn(rec)
        assert rec.calls, "emission recorded no instruction"
        def closure(h, waits=waits, calls=rec.calls, tgt=tgt):
            for sh, c in waits:
                h.wait_ge(sh, c)
            ins = None
            for (name, a, k) in calls:
                ins = getattr(h, name)(*a, **k)
            if tgt is not None:
                ins.then_inc(tgt[0], tgt[1])
        self.ops[eng].append(closure)
        self.n_instr += len(rec.calls) + len(waits)
        return ev

    @staticmethod
    def _collect(items):
        deps = []
        for b in items:
            if b is None:
                continue
            if isinstance(b, tuple):
                deps.append(b)
            else:
                if b.ev is not None:
                    deps.append(b.ev)
        return deps

    def rw(self, eng, fn, reads=(), writes=(), extra=()):
        reads = list(reads)
        writes = list(writes)
        for b in list(reads):
            if b is not None and not isinstance(b, tuple) and b.serial:
                reads.remove(b)
                writes.append(b)
        deps = self._collect(reads)
        for b in writes:
            if b is None:
                continue
            if b.ev is not None:
                deps.append(b.ev)
            deps.extend(b.readers)
        deps.extend(self._collect(extra))
        ev = self._emit(eng, fn, deps, ESEM[eng], 1)
        for b in reads:
            if b is not None and not isinstance(b, tuple):
                b.readers.append(ev)
        for b in writes:
            if b is not None:
                b.ev = ev
                b.readers = []
        return ev

    def dma(self, q, fn, reads=(), writes=(), extra=()):
        reads = list(reads)
        writes = list(writes)
        for b in list(reads):
            if b is not None and not isinstance(b, tuple) and b.serial:
                reads.remove(b)
                writes.append(b)
        eng, qname = QUEUE[q]
        slot = self.qslot[qname]
        self.qslot[qname] = (slot + 1) % NSLOT
        semn = "%s%d" % (qname, slot)
        deps = self._collect(reads)
        deps.append((semn, self.count[semn]))
        for b in writes:
            if b is None:
                continue
            if b.ev is not None:
                deps.append(b.ev)
            deps.extend(b.readers)
        deps.extend(self._collect(extra))
        ev = self._emit(eng, fn, deps, semn, 16)
        for b in reads:
            if b is not None and not isinstance(b, tuple):
                b.readers.append(ev)
        for b in writes:
            if b is not None:
                b.ev = ev
                b.readers = []
        return ev

    def barrier(self):
        evs = [(s, self.count[s]) for s in self.sem if self.count[s] > 0]
        return evs

    def run(self):
        with self.nc.Block() as block:
            for e in ENG:
                getattr(block, e)(lambda h, e=e: [f(h) for f in self.ops[e]])


def sb(nc, stack, name, shape, dtype):
    return stack.enter_context(nc.sbuf_tensor(name, shape, dtype))


def ps_(nc, stack, name, shape, dtype=F32):
    return stack.enter_context(nc.psum_tensor(name, shape, dtype))


def build(debug=0):
    PH = 6
    SUB = 9
    nc = bass.Bass("TRN2", target_bir_lowering=False, debug=False,
                   detect_race_conditions=False)
    d = {}
    def din(name, shape, dtype=F32):
        d[name] = nc.dram_tensor(name, shape, dtype, kind="ExternalInput").ap()
    def dout(name, shape, dtype=F32):
        d[name] = nc.dram_tensor(name, shape, dtype, kind="ExternalOutput").ap()
    def dint(name, shape, dtype=F32):
        kind = "ExternalOutput" if debug else "Internal"
        d[name] = nc.dram_tensor(name, shape, dtype, kind=kind).ap()

    din("xT", [1024, T])
    for w in ("wrT", "wkT", "wvT", "woT"):
        din(w, [1024, 1024])
    for v in ("tmr", "tmk", "tmv", "decay", "first", "gnw", "gnb"):
        din(v, [128, 8])
    for l in range(2):
        din("wqkv%d" % l, [1024, 1536])
        din("wol%d" % l, [512, 1024])
        din("w1l%d" % l, [1024, 5460])
        din("w2l%d" % l, [2730, 1024])
        din("qs%d" % l, [128, 1])
        din("ks%d" % l, [128, 1])
        din("lg%d" % l, [128, 8])
        din("lb%d" % l, [128, 8])
        din("b1l%d" % l, [5460, 1])
        din("b2l%d" % l, [128, 8])
    din("identb", [128, 128], BF16)
    din("identf", [128, 128])
    din("identd", [128, 64])
    for l in range(2):
        din("b1a%d" % l, [128, 22])
        din("b1g%d" % l, [128, 22])
    din("permb", [128, 128], BF16)
    din("onesf", [128, 1])
    din("halvesf", [128, 2])
    din("og4", [128, 32])
    din("og4b", [128, 32], BF16)
    din("onesb1", [128, 1], BF16)
    din("maskl", [128, 128], BF16)
    din("maskr", [128, 128], BF16)
    din("bg4", [8, 512], BF16)
    din("bonef", [1, 128])
    din("halves2", [2, 128])
    for t_ in ("c1", "s1", "c2", "s2"):
        din(t_, [128, T])
    dout("outT", [1024, T])
    dint("sc_out", [1024, T], BF16)
    dint("sc_x1", [1024, T])
    dint("sc_xf0", [1024, T])
    dint("sc_xm", [1024, T])
    dint("sc_xf1", [1024, T])
    d["sc_stats"] = nc.dram_tensor("sc_stats", [2, T], F32, kind="Internal").ap()
    nc_sums = nc.dram_tensor("sc_sums", [8 * 512], F32, kind="Internal"); d["sc_sums"] = nc_sums.ap()
    if debug:
        dout("dbg_out", [1024, T])
        dout("dbg_sb", [128, 64])

    with ExitStack() as st:
      try:
        P = Prog(nc, st)
        cpool = sb(nc, st, "sb_cpool", [128, 8], F32)
        ident = sb(nc, st, "sb_ident", [128, 128], BF16)
        perm = sb(nc, st, "sb_perm", [128, 128], BF16)
        onesf = sb(nc, st, "sb_onesf", [128, 1], F32)
        halvesf = sb(nc, st, "sb_halvesf", [128, 2], F32)
        og4 = sb(nc, st, "sb_og4", [128, 32], F32)
        og4b = sb(nc, st, "sb_og4b", [128, 32], BF16)
        onesb1 = sb(nc, st, "sb_onesb1", [128, 1], BF16)
        maskl = sb(nc, st, "sb_maskl", [128, 128], BF16)
        maskr = sb(nc, st, "sb_maskr", [128, 128], BF16)
        bg4 = sb(nc, st, "sb_bg4", [8, 512], BF16)
        bonef = sb(nc, st, "sb_bonef", [1, 128], F32)
        halves2 = sb(nc, st, "sb_halves2", [2, 128], F32)
        smallv = {}
        for v in ("tmr", "tmk", "tmv", "decay", "first", "gnw", "gnb"):
            smallv[v] = sb(nc, st, "s_" + v, [128, 8], F32)
        xin = [[sb(nc, st, "xin%d_%d" % (p, g), [128, TC + 1], F32) for g in range(DT)] for p in range(2)]
        hbf = [sb(nc, st, "hbf%d" % g, [128, TC + 1], BF16) for g in range(DT)]
        stt_m = sb(nc, st, "stt_m", [1, TC], F32)
        stt_i = sb(nc, st, "stt_i", [1, TC], F32)
        fin8 = sb(nc, st, "fin8", [128, 48], F32)
        asum = [sb(nc, st, "asum%d" % g, [128, NCH], F32) for g in range(DT)]
        qsum = [sb(nc, st, "qsum%d" % g, [128, NCH], F32) for g in range(DT)]
        scale_sb = sb(nc, st, "scale_sb", [128, 8], F32)
        bias_sb = sb(nc, st, "bias_sb", [128, 8], F32)
        w_eu = sb(nc, st, "w_eu", [128, 16], F32)

        B = {}
        def buf(name):
            if name not in B:
                B[name] = Buf(name)
            return B[name]

        P.dma("sp", lambda h: h.dma_start(out=ident[:, :], in_=d["identb"]), writes=[buf("ident")])
        P.dma("sp", lambda h: h.dma_start(out=perm[:, :], in_=d["permb"]), writes=[buf("perm")])
        P.dma("sp", lambda h: h.dma_start(out=onesf[:, :], in_=d["onesf"]), writes=[buf("onesf")])
        P.dma("sp", lambda h: h.dma_start(out=halvesf[:, :], in_=d["halvesf"]), writes=[buf("halvesf")])
        P.dma("sp", lambda h: h.dma_start(out=og4[:, :], in_=d["og4"]), writes=[buf("og4")])
        P.dma("sp", lambda h: h.dma_start(out=og4b[:, :], in_=d["og4b"]), writes=[buf("og4b")])
        P.dma("sp", lambda h: h.dma_start(out=onesb1[:, :], in_=d["onesb1"]), writes=[buf("onesb1")])
        P.dma("sp", lambda h: h.dma_start(out=maskl[:, :], in_=d["maskl"]), writes=[buf("maskl")])
        P.dma("sp", lambda h: h.dma_start(out=maskr[:, :], in_=d["maskr"]), writes=[buf("maskr")])
        P.dma("sp", lambda h: h.dma_start(out=bg4[:, :], in_=d["bg4"]), writes=[buf("bg4")])
        P.dma("sp", lambda h: h.dma_start(out=bonef[:, :], in_=d["bonef"]), writes=[buf("bonef")])
        P.dma("sp", lambda h: h.dma_start(out=halves2[:, :], in_=d["halves2"]), writes=[buf("halves2")])
        for v in ("tmr", "tmk", "tmv", "decay", "first", "gnw", "gnb"):
            P.dma("sp", lambda h, v=v: h.dma_start(out=smallv[v][:, :], in_=d[v]), writes=[buf("sv_" + v)])
        P.rw("gpsimd", lambda h: h.memset(cpool[:, 0:1], -8.0), writes=[buf("cpool")])
        P.rw("gpsimd", lambda h: h.memset(fin8[:, :], 0.0), writes=[buf("fin8")])
        P.rw("gpsimd", lambda h: h.memset(cpool[:, 1:2], 1e-5), writes=[buf("cpool")])
        P.rw("scalar", lambda h: h.activation(w_eu[:, 0:8], smallv["decay"][:, :], AF.Exp),
             reads=[buf("sv_decay")], writes=[buf("w_eu")])
        P.rw("scalar", lambda h: h.activation(w_eu[:, 0:8], w_eu[:, 0:8], AF.Exp, scale=-1.0),
             writes=[buf("w_eu")])
        P.rw("scalar", lambda h: h.activation(w_eu[:, 8:16], smallv["first"][:, :], AF.Exp),
             reads=[buf("sv_first")], writes=[buf("w_eu")])

        with ExitStack() as stA:
            wr = [sb(nc, stA, "wr%d" % g, [128, 1024], BF16) for g in range(DT)]
            wk = [sb(nc, stA, "wk%d" % g, [128, 1024], BF16) for g in range(DT)]
            wv = [sb(nc, stA, "wv%d" % g, [128, 1024], BF16) for g in range(DT)]
            hb = [sb(nc, stA, "hb%d" % g, [128, TC + 1], BF16) for g in range(DT)]
            numb = [sb(nc, stA, "numb%d" % g, [128, TC + 1], F32) for g in range(DT)]
            denb = [sb(nc, stA, "denb%d" % g, [128, TC + 1], F32) for g in range(DT)]
            NS = 2
            S1 = [sb(nc, stA, "Sa%d" % g, [128, TC], F32) for g in range(NS)]
            S2 = [sb(nc, stA, "Sb%d" % g, [128, TC], F32) for g in range(NS)]
            S3 = [sb(nc, stA, "Sc%d" % g, [128, TC], F32) for g in range(NS)]
            Bx = [[[sb(nc, stA, "Bx%d_%d_%d" % (p, i, g), [128, TC], BF16) for g in range(DT)] for i in range(3)] for p in range(2)]
            ddb = [sb(nc, stA, "ddb0", [128, TC], BF16)]
            sqs = sb(nc, stA, "sqs", [128, TC], BF16)
            ps_stat = ps_(nc, stA, "ps_stat", [1, TC])
            ps_stat2 = ps_(nc, stA, "ps_stat2", [1, TC])
            ps_b1 = ps_(nc, stA, "ps_b1", [128, TC])
            ps_b2 = ps_(nc, stA, "ps_b2", [128, TC])
            ps_mm = [ps_(nc, stA, "ps_mm%d" % i, [128, TC]) for i in range(4)]

            for g in range(DT):
                for nm, dst in (("wrT", wr), ("wkT", wk), ("wvT", wv)):
                    P.dma("pl", lambda h, nm=nm, dst=dst, g=g: h.dma_start(
                        out=dst[g][:, :], in_=d[nm][128 * g:128 * (g + 1), :]),
                        writes=[buf("w_%s%d" % (nm, g))])
            for g in range(DT):
                P.rw("gpsimd", lambda h, g=g: h.memset(hb[g][:, TC:TC + 1], 0.0), writes=[buf("hb%d" % g)])
                P.rw("gpsimd", lambda h, g=g: h.memset(numb[g][:, TC:TC + 1], 0.0), writes=[buf("numb%d" % g)])
                P.rw("gpsimd", lambda h, g=g: h.memset(denb[g][:, TC:TC + 1], 0.0), writes=[buf("denb%d" % g)])

            def frontA_thunks(c):
                t0 = c * TC
                par = c % 2
                xb = xin[par]
                th = []
                def t_loads():
                    for g in range(DT):
                        P.dma("sp", lambda h, g=g, t0=t0, xb=xb: h.dma_start(
                            out=xb[g][:, 1:TC + 1], in_=d["xT"][128 * g:128 * (g + 1), t0:t0 + TC]),
                            writes=[buf("xin%d_%d" % (par, g))])
                th.append(t_loads)
                def t_stat(g):
                    bx = buf("xin%d_%d" % (par, g))
                    bsq = buf("sqs")
                    P.rw("scalar", lambda h, g=g, xb=xb: h.activation(sqs[:, :], xb[g][:, 1:TC + 1], AF.Square),
                         reads=[bx], writes=[bsq])
                    P.rw("tensor", lambda h, g=g, xb=xb: h.matmul(
                        ps_stat[:, :], onesf[:, :], xb[g][:, 1:TC + 1], start=(g == 0), stop=(g == DT - 1)),
                        reads=[bx, buf("onesf")], writes=[buf("ps_stat")])
                    P.rw("tensor", lambda h, g=g: h.matmul(
                        ps_stat2[:, :], onesb1[:, :], sqs[:, :], start=(g == 0), stop=(g == DT - 1)),
                        reads=[bsq, buf("onesb1")], writes=[buf("ps_stat2")])
                for g in range(DT):
                    th.append(lambda g=g: t_stat(g))
                def t_fin():
                    P.rw("vector", lambda h: h.tensor_scalar(stt_m[:, :], ps_stat[:, :], 1.0 / 1024, None, ALU.mult),
                         reads=[buf("ps_stat")], writes=[buf("stt_m")])
                    P.rw("vector", lambda h: h.tensor_scalar(stt_i[:, :], ps_stat2[:, :], 1.0 / 1024, None, ALU.mult),
                         reads=[buf("ps_stat2")], writes=[buf("stt_i")])
                    P.rw("vector", lambda h: h.scalar_tensor_tensor(
                        sqs[0:1, 0:TC], stt_m[:, :], 1.0, stt_m[:, :], ALU.mult, ALU.mult),
                        reads=[buf("stt_m")], writes=[buf("sqs")])
                    P.rw("vector", lambda h: h.tensor_sub(stt_i[:, :], stt_i[:, :], sqs[0:1, 0:TC]),
                         reads=[buf("sqs")], writes=[buf("stt_i")])
                    P.rw("scalar", lambda h: h.activation(stt_i[:, :], stt_i[:, :], AF.Ln, bias=cpool[0:1, 1:2]),
                         reads=[buf("cpool")], writes=[buf("stt_i")])
                    P.rw("scalar", lambda h: h.activation(stt_i[:, :], stt_i[:, :], AF.Exp, scale=-0.5),
                         writes=[buf("stt_i")])
                    P.rw("tensor", lambda h: h.matmul(ps_b1[:, :], bonef[:, :], stt_m[:, :], start=True, stop=True),
                         reads=[buf("stt_m"), buf("bonef")], writes=[buf("ps_b1")])
                    P.rw("tensor", lambda h: h.matmul(ps_b2[:, :], bonef[:, :], stt_i[:, :], start=True, stop=True),
                         reads=[buf("stt_i"), buf("bonef")], writes=[buf("ps_b2")])
                th.append(t_fin)
                def t_h(g):
                    bh = buf("hb%d" % g)
                    dd = ddb[0]
                    bdd = buf("ddb0")
                    P.rw("vector", lambda h, g=g: h.tensor_copy(hb[g][:, 0:1], hb[g][:, TC:TC + 1]),
                         writes=[bh])
                    P.rw("vector", lambda h, g=g, xb=xb: h.tensor_sub(
                        hb[g][:, 1:TC + 1], xb[g][:, 1:TC + 1], ps_b1[:, :]),
                        reads=[buf("xin%d_%d" % (par, g)), buf("ps_b1")], writes=[bh])
                    P.rw("vector", lambda h, g=g: h.tensor_mul(
                        hb[g][:, 1:TC + 1], hb[g][:, 1:TC + 1], ps_b2[:, :]),
                        reads=[buf("ps_b2")], writes=[bh])
                    P.rw("vector", lambda h, g=g, dd=dd: h.tensor_sub(
                        dd[:, :], hb[g][:, 1:TC + 1], hb[g][:, 0:TC]),
                        reads=[bh], writes=[bdd])
                    for i, tm in enumerate(("tmr", "tmk", "tmv")):
                        P.rw("vector", lambda h, g=g, i=i, tm=tm, dd=dd: h.scalar_tensor_tensor(
                            Bx[par][i][g][:, :], dd[:, :], smallv[tm][:, g:g + 1], hb[g][:, 0:TC],
                            ALU.mult, ALU.add),
                            reads=[bdd, buf("sv_" + tm), bh],
                            writes=[buf("Bx%d_%d_%d" % (par, i, g))])
                for g in range(DT):
                    th.append(lambda g=g: t_h(g))
                return th

            def scanA(c, th):
                par = c % 2
                t0 = c * TC
                def mk_mm(dst_ps, wts, bxi, g, par=par):
                    def f(h):
                        ins = None
                        for gg in range(DT):
                            ins = h.matmul(dst_ps[:, :], wts[gg][:, 128 * g:128 * (g + 1)],
                                           Bx[par][bxi][gg][:, :], start=(gg == 0), stop=(gg == DT - 1))
                        return ins
                    return f
                def emit_k(g):
                    sg = g % NS
                    kp = ps_mm[(g % 2) * 3]
                    bk_ps = buf("ps_mm%d" % ((g % 2) * 3))
                    P.rw("tensor", mk_mm(kp, wk, 1, g),
                         reads=[buf("Bx%d_1_%d" % (par, gg)) for gg in range(DT)] + [buf("w_wkT%d" % gg) for gg in range(DT)],
                         writes=[bk_ps])
                    P.rw("scalar", lambda h, g=g, sg=sg, kp=kp: h.activation(S2[sg][:, :], kp[:, :], AF.Exp),
                         reads=[bk_ps], writes=[buf("S2%d" % sg)])
                emit_k(0)
                for g in range(DT):
                    sg = g % NS
                    bv_ps, br_ps = buf("ps_mm1"), buf("ps_mm2")
                    P.rw("tensor", mk_mm(ps_mm[1], wv, 2, g),
                         reads=[buf("Bx%d_2_%d" % (par, gg)) for gg in range(DT)] + [buf("w_wvT%d" % gg) for gg in range(DT)],
                         writes=[bv_ps])
                    P.rw("vector", lambda h, g=g, sg=sg: h.tensor_mul(S3[sg][:, :], S2[sg][:, :], ps_mm[1][:, :]),
                         reads=[buf("S2%d" % sg), bv_ps], writes=[buf("S3%d" % sg)])
                    if g + 1 < DT:
                        emit_k(g + 1)
                    bn, bd = buf("numb%d" % g), buf("denb%d" % g)
                    P.rw("vector", lambda h, g=g: h.tensor_copy(numb[g][:, 0:1], numb[g][:, TC:TC + 1]),
                         writes=[bn])
                    P.rw("vector", lambda h, g=g: h.tensor_copy(denb[g][:, 0:1], denb[g][:, TC:TC + 1]),
                         writes=[bd])
                    P.rw("vector", lambda h, g=g, sg=sg: h.tensor_tensor_scan(
                        numb[g][:, 1:TC + 1], w_eu[:, g:g + 1].broadcast_to((128, TC)), S3[sg][:, :],
                        numb[g][:, 0:1], ALU.mult, ALU.add),
                        reads=[buf("w_eu"), buf("S3%d" % sg)], writes=[bn])
                    P.rw("vector", lambda h, g=g, sg=sg: h.tensor_tensor_scan(
                        denb[g][:, 1:TC + 1], w_eu[:, g:g + 1].broadcast_to((128, TC)), S2[sg][:, :],
                        denb[g][:, 0:1], ALU.mult, ALU.add),
                        reads=[buf("w_eu"), buf("S2%d" % sg)], writes=[bd])
                    P.rw("vector", lambda h, g=g, sg=sg: h.scalar_tensor_tensor(
                        S3[sg][:, :], S3[sg][:, :], w_eu[:, 8 + g:9 + g], numb[g][:, 0:TC], ALU.mult, ALU.add),
                        reads=[buf("w_eu"), bn], writes=[buf("S3%d" % sg)])
                    P.rw("vector", lambda h, g=g, sg=sg: h.scalar_tensor_tensor(
                        S2[sg][:, :], S2[sg][:, :], w_eu[:, 8 + g:9 + g], denb[g][:, 0:TC],
                        ALU.mult, ALU.add),
                        reads=[buf("w_eu"), bd], writes=[buf("S2%d" % sg)])
                    P.rw("tensor", mk_mm(ps_mm[2], wr, 0, g),
                         reads=[buf("Bx%d_0_%d" % (par, gg)) for gg in range(DT)] + [buf("w_wrT%d" % gg) for gg in range(DT)],
                         writes=[br_ps])
                    P.rw("scalar", lambda h, g=g, sg=sg: h.activation(S1[sg][:, :], ps_mm[2][:, :], AF.Exp, scale=-1.0),
                         reads=[br_ps], writes=[buf("S1%d" % sg)])
                    P.rw("vector", lambda h, g=g, sg=sg: h.scalar_tensor_tensor(
                        S2[sg][:, :], S1[sg][:, :], 1.0, S2[sg][:, :], ALU.add, ALU.mult),
                        reads=[buf("S1%d" % sg)], writes=[buf("S2%d" % sg)])
                    P.rw("scalar", lambda h, g=g, sg=sg: h.activation(S2[sg][:, :], S2[sg][:, :], AF.Ln),
                         writes=[buf("S2%d" % sg)])
                    P.rw("scalar", lambda h, g=g, sg=sg: h.activation(S2[sg][:, :], S2[sg][:, :], AF.Exp, scale=-1.0),
                         writes=[buf("S2%d" % sg)])
                    P.rw("vector", lambda h, g=g, c=c, sg=sg: h.scalar_tensor_tensor(
                        hbf[g][:, 0:TC], S3[sg][:, :], 1.0, S2[sg][:, :], ALU.mult, ALU.mult,
                        accum_out=asum[g][:, c:c + 1]),
                        reads=[buf("S3%d" % sg), buf("S2%d" % sg)],
                        writes=[buf("hbf%d" % g), buf("asum%d" % g)])
                    P.rw("scalar", lambda h, g=g, c=c: h.activation(
                        sqs[:, :], hbf[g][:, 0:TC], AF.Square, accum_out=qsum[g][:, c:c + 1]),
                        reads=[buf("hbf%d" % g)], writes=[buf("sqs"), buf("qsum%d" % g)])
                    P.dma("pl", lambda h, g=g, t0=t0: h.dma_start(
                        out=d["sc_out"][128 * g:128 * (g + 1), t0:t0 + TC], in_=hbf[g][:, 0:TC]),
                        reads=[buf("hbf%d" % g)])
                    if th:
                        for _ in range((3, 3, 3, 2, 2, 2, 2, 1)[g]):
                            if th:
                                th.pop(0)()

            for th_ in frontA_thunks(0):
                th_()
            for c in range(NCH):
                tb = frontA_thunks(c + 1) if c + 1 < NCH else []
                scanA(c, tb)
                for th_ in tb:
                    th_()
                tb.clear()
            for g in range(DT):
                P.rw("vector", lambda h, g=g: h.reduce_sum(
                    scale_sb[:, g:g + 1], asum[g][:, :], mybir.AxisListType.X),
                    reads=[buf("asum%d" % g)], writes=[buf("scale_sb")])
                P.rw("vector", lambda h, g=g: h.reduce_sum(
                    bias_sb[:, g:g + 1], qsum[g][:, :], mybir.AxisListType.X),
                    reads=[buf("qsum%d" % g)], writes=[buf("bias_sb")])
            P.rw("tensor", lambda h: h.matmul(ps_b1[0:2, 0:8],
                                              halvesf[:, :], scale_sb[:, :], start=True, stop=True),
                 reads=[buf("scale_sb"), buf("halvesf")], writes=[buf("ps_b1")])
            P.rw("tensor", lambda h: h.matmul(ps_b2[0:2, 0:8], halvesf[:, :], bias_sb[:, :], start=True, stop=True),
                 reads=[buf("bias_sb"), buf("halvesf")], writes=[buf("ps_b2")])
            if debug:
                P.rw("vector", lambda h: h.tensor_copy(fin8[0:2, 40:48], ps_b2[0:2, 0:8]),
                     reads=[buf("ps_b2")], writes=[buf("fin8")])
                P.rw("vector", lambda h: h.tensor_copy(fin8[0:2, 24:32], ps_b1[0:2, 0:8]),
                     reads=[buf("ps_b1")], writes=[buf("fin8")])
                P.dma("pl", lambda h: h.dma_start(out=d["dbg_sb"][:, 24:32], in_=bias_sb[:, :]),
                      reads=[buf("bias_sb")])
            ninv = 1.0 / (64.0 * T)
            P.rw("vector", lambda h: h.tensor_scalar(fin8[0:2, 0:8],
                                                     ps_b1[0:2, 0:8], ninv, None, ALU.mult),
                 reads=[buf("ps_b1")], writes=[buf("fin8")])
            P.rw("vector", lambda h: h.tensor_scalar(fin8[0:2, 8:16], ps_b2[0:2, 0:8], ninv, None, ALU.mult),
                 reads=[buf("ps_b2")], writes=[buf("fin8")])
            if debug:
                P.dma("pl", lambda h: h.dma_start(out=d["dbg_sb"][0:2, 48:56], in_=fin8[0:2, 8:16]),
                      reads=[buf("fin8")])
            P.rw("vector", lambda h: h.tensor_mul(fin8[0:2, 16:24], fin8[0:2, 0:8], fin8[0:2, 0:8]),
                reads=[buf("fin8")], writes=[buf("fin8")])
            P.rw("vector", lambda h: h.tensor_sub(fin8[0:2, 8:16], fin8[0:2, 8:16], fin8[0:2, 16:24]),
                 reads=[buf("fin8")], writes=[buf("fin8")])
            if debug:
                P.dma("pl", lambda h: h.dma_start(out=d["dbg_sb"][0:2, 56:64], in_=fin8[0:2, 8:16]),
                      reads=[buf("fin8")])
            P.rw("scalar", lambda h: h.activation(fin8[0:2, 8:16], fin8[0:2, 8:16], AF.Ln, bias=cpool[0:2, 1:2]),
                 reads=[buf("cpool")], writes=[buf("fin8")])
            P.rw("scalar", lambda h: h.activation(fin8[0:2, 8:16], fin8[0:2, 8:16], AF.Exp, scale=-0.5),
                 writes=[buf("fin8")])
            P.rw("tensor", lambda h: h.matmul(ps_b1[:, 0:8], halves2[:, :], fin8[0:2, 0:8], start=True, stop=True),
                 reads=[buf("fin8"), buf("halves2")], writes=[buf("ps_b1")])
            P.rw("tensor", lambda h: h.matmul(ps_b2[:, 0:8], halves2[:, :], fin8[0:2, 8:16], start=True, stop=True),
                 reads=[buf("fin8"), buf("halves2")], writes=[buf("ps_b2")])
            P.rw("vector", lambda h: h.tensor_mul(scale_sb[:, :], ps_b2[:, 0:8], smallv["gnw"][:, :]),
                 reads=[buf("ps_b2"), buf("sv_gnw")], writes=[buf("scale_sb")])
            P.rw("vector", lambda h: h.tensor_mul(fin8[:, 32:40], ps_b1[:, 0:8], scale_sb[:, :]),
                 reads=[buf("ps_b1"), buf("scale_sb")], writes=[buf("fin8")])
            P.rw("vector", lambda h: h.tensor_sub(bias_sb[:, :], smallv["gnb"][:, :], fin8[:, 32:40]),
                 reads=[buf("sv_gnb"), buf("fin8")], writes=[buf("bias_sb")])
            if debug:
                P.dma("pl", lambda h: h.dma_start(out=d["dbg_sb"][:, 0:8], in_=scale_sb[:, :]),
                      reads=[buf("scale_sb")])
                P.dma("pl", lambda h: h.dma_start(out=d["dbg_sb"][:, 8:16], in_=bias_sb[:, :]),
                      reads=[buf("bias_sb")])
                P.dma("pl", lambda h: h.dma_start(out=d["dbg_sb"][:, 16:24], in_=fin8[:, 0:8]),
                      reads=[buf("fin8")])
                P.dma("pl", lambda h: h.dma_start(out=d["dbg_sb"][:, 32:40], in_=fin8[:, 40:48]),
                      reads=[buf("fin8")])
                P.dma("pl", lambda h: h.dma_start(out=d["dbg_sb"][:, 40:48], in_=fin8[:, 24:32]),
                      reads=[buf("fin8")])


        if PH < 2:
            raise StopIteration
        P.floor = P.barrier()
        with ExitStack() as stB:
            wob = [sb(nc, stB, "wob%d" % g, [128, 1024], BF16) for g in range(DT)]
            yb = [sb(nc, stB, "yb%d" % g, [128, TC], BF16) for g in range(DT)]
            SB1 = [[sb(nc, stB, "SB1_%d_%d" % (p, g), [128, TC], F32) for g in range(DT)] for p in range(2)]
            sqB = sb(nc, stB, "sqB", [128, TC], BF16)
            x1b = sb(nc, stB, "x1b", [128, TC], BF16)
            ps_stat = ps_(nc, stB, "psB_stat", [1, TC])
            ps_stat2 = ps_(nc, stB, "psB_stat2", [1, TC])
            ps_b1 = ps_(nc, stB, "psB_b1", [128, TC])
            ps_b2 = ps_(nc, stB, "psB_b2", [128, TC])
            ps_mm = [ps_(nc, stB, "psB_mm%d" % i, [128, TC]) for i in range(3)]
            for g in range(DT):
                P.dma("pl", lambda h, g=g: h.dma_start(
                    out=wob[g][:, :], in_=d["woT"][128 * g:128 * (g + 1), :]),
                    writes=[buf("wob%d" % g)])
            def woB(c):
                t0 = c * TC
                par = c % 2
                xb = xin[par]
                S_ = SB1[par]
                for g in range(DT):
                    P.dma("sp", lambda h, g=g, t0=t0, xb=xb: h.dma_start(
                        out=xb[g][:, 1:TC + 1], in_=d["xT"][128 * g:128 * (g + 1), t0:t0 + TC]),
                        writes=[buf("xin%d_%d" % (par, g))])
                    P.dma("sp", lambda h, g=g, t0=t0: h.dma_start(
                        out=hbf[g][:, 0:TC], in_=d["sc_out"][128 * g:128 * (g + 1), t0:t0 + TC]),
                        writes=[buf("hbf%d" % g)])
                    P.rw("vector", lambda h, g=g: h.tensor_scalar(
                        yb[g][:, :], hbf[g][:, 0:TC], scale_sb[:, g:g + 1], bias_sb[:, g:g + 1],
                        ALU.mult, ALU.add),
                        reads=[buf("hbf%d" % g), buf("scale_sb"), buf("bias_sb")],
                        writes=[buf("yb%d" % g)])
                for et in range(DT):
                    pm = ps_mm[et % 2]
                    bpm = buf("psB_mm%d" % (et % 2))
                    def wo_mm(h, et=et, pm=pm):
                        ins = None
                        for gg in range(DT):
                            ins = h.matmul(pm[:, :], wob[gg][:, 128 * et:128 * (et + 1)],
                                           yb[gg][:, :], start=(gg == 0), stop=(gg == DT - 1))
                        return ins
                    P.rw("tensor", wo_mm,
                         reads=[buf("yb%d" % gg) for gg in range(DT)] + [buf("wob%d" % gg) for gg in range(DT)],
                         writes=[bpm])
                    P.rw("vector", lambda h, et=et, pm=pm, xb=xb, S_=S_: h.tensor_add(
                        S_[et][:, :], xb[et][:, 1:TC + 1], pm[:, :]),
                        reads=[buf("xin%d_%d" % (par, et)), bpm], writes=[buf("SB1_%d_%d" % (par, et))])
                    P.dma("pl", lambda h, et=et, t0=t0, S_=S_: h.dma_start(
                        out=d["sc_x1"][128 * et:128 * (et + 1), t0:t0 + TC], in_=S_[et][:, :]),
                        reads=[buf("SB1_%d_%d" % (par, et))])
            def lnB(c):
                t0 = c * TC
                par = c % 2
                S_ = SB1[par]
                for g in range(DT):
                    P.rw("scalar", lambda h, g=g, S_=S_: h.activation(sqB[:, :], S_[g][:, :], AF.Square),
                         reads=[buf("SB1_%d_%d" % (par, g))], writes=[buf("sqB")])
                    P.rw("tensor", lambda h, g=g, S_=S_: h.matmul(
                        ps_stat[:, :], onesf[:, :], S_[g][:, :], start=(g == 0), stop=(g == DT - 1)),
                        reads=[buf("SB1_%d_%d" % (par, g)), buf("onesf")], writes=[buf("psB_stat")])
                    P.rw("tensor", lambda h, g=g: h.matmul(
                        ps_stat2[:, :], onesb1[:, :], sqB[:, :], start=(g == 0), stop=(g == DT - 1)),
                        reads=[buf("sqB"), buf("onesb1")], writes=[buf("psB_stat2")])
                P.rw("vector", lambda h: h.tensor_scalar(stt_m[:, :], ps_stat[:, :], 1.0 / 1024, None, ALU.mult),
                     reads=[buf("psB_stat")], writes=[buf("stt_m")])
                P.rw("vector", lambda h: h.tensor_scalar(stt_i[:, :], ps_stat2[:, :], 1.0 / 1024, None, ALU.mult),
                     reads=[buf("psB_stat2")], writes=[buf("stt_i")])
                P.rw("vector", lambda h: h.tensor_mul(sqB[0:1, 0:TC], stt_m[:, :], stt_m[:, :]),
                     reads=[buf("stt_m")], writes=[buf("sqB")])
                P.rw("vector", lambda h: h.tensor_sub(stt_i[:, :], stt_i[:, :], sqB[0:1, 0:TC]),
                     reads=[buf("sqB")], writes=[buf("stt_i")])
                P.rw("scalar", lambda h: h.activation(stt_i[:, :], stt_i[:, :], AF.Ln, bias=cpool[0:1, 1:2]),
                     reads=[buf("cpool")], writes=[buf("stt_i")])
                P.rw("scalar", lambda h: h.activation(stt_i[:, :], stt_i[:, :], AF.Exp, scale=-0.5),
                     writes=[buf("stt_i")])
                P.rw("tensor", lambda h: h.matmul(ps_b1[:, :], bonef[:, :], stt_m[:, :], start=True, stop=True),
                     reads=[buf("stt_m"), buf("bonef")], writes=[buf("psB_b1")])
                P.rw("tensor", lambda h: h.matmul(ps_b2[:, :], bonef[:, :], stt_i[:, :], start=True, stop=True),
                     reads=[buf("stt_i"), buf("bonef")], writes=[buf("psB_b2")])
                for g in range(DT):
                    P.rw("vector", lambda h, g=g, S_=S_: h.tensor_sub(S_[g][:, :], S_[g][:, :], ps_b1[:, :]),
                         reads=[buf("psB_b1")], writes=[buf("SB1_%d_%d" % (par, g))])
                    P.rw("vector", lambda h, g=g, S_=S_: h.tensor_mul(S_[g][:, :], S_[g][:, :], ps_b2[:, :]),
                         reads=[buf("psB_b2")], writes=[buf("SB1_%d_%d" % (par, g))])
                    P.dma("pl", lambda h, g=g, t0=t0, S_=S_: h.dma_start(
                        out=d["sc_xf0"][128 * g:128 * (g + 1), t0:t0 + TC], in_=S_[g][:, :]),
                        reads=[buf("SB1_%d_%d" % (par, g))])
            woB(0)
            for c in range(NCH):
                if c + 1 < NCH:
                    woB(c + 1)
                lnB(c)
        if PH < 3:
            raise StopIteration
        for l in range(2):
            src_xf = "sc_xf0" if l == 0 else "sc_xf1"
            P.floor = P.barrier()
            if PH < 3 + 2 * l:
                raise StopIteration
            with ExitStack() as sA:
                wqb = [sb(nc, sA, "wqb%d_%d" % (l, g), [128, 1536], BF16) for g in range(DT)]
                wlb = [sb(nc, sA, "wlb%d_%d" % (l, g), [128, 1024], BF16) for g in range(4)]
                qw = [[sb(nc, sA, "qw%d_%d_%d" % (l, p, g), [128, TC], BF16) for g in range(4)] for p in range(2)]
                kw = [[sb(nc, sA, "kw%d_%d_%d" % (l, p, g), [128, TC], BF16) for g in range(4)] for p in range(3)]
                vt = [sb(nc, sA, "vt%d_%d" % (l, p), [128, 2080], BF16) for p in range(3)]
                atT = [sb(nc, sA, "atT%d_%d" % (l, p), [128, 4096], BF16) for p in range(2)]
                oT = [sb(nc, sA, "oT%d_%d" % (l, g), [128, TC], BF16) for g in range(4)]
                vraw = [sb(nc, sA, "vraw%d_%d" % (l, g), [128, TC], F32) for g in range(4)]
                qraw = [sb(nc, sA, "qraw%d_%d" % (l, g), [128, TC], BF16) for g in range(4)]
                kraw = [sb(nc, sA, "kraw%d_%d" % (l, g), [128, TC], BF16) for g in range(4)]
                tbl = [[sb(nc, sA, "tbl%d_%d_%d" % (l, p, i), [128, TC], BF16) for i in range(4)] for p in range(2)]
                rtmp = [sb(nc, sA, "rtmp%d_%d" % (l, i), [128, TC], BF16) for i in range(2)]
                SM = [sb(nc, sA, "SM%d_%d" % (l, g), [128, TC + 1], F32) for g in range(DT)]
                statm = sb(nc, sA, "statm%d" % l, [1, TC], F32)
                stati = sb(nc, sA, "stati%d" % l, [1, TC], F32)
                sqA = sb(nc, sA, "sqA%d" % l, [128, TC], F32)
                sqAb = sb(nc, sA, "sqAb%d" % l, [128, TC], BF16)
                nrm = sb(nc, sA, "nrm%d" % l, [8, 2 * TC], F32)
                nrmb = sb(nc, sA, "nrmb%d" % l, [8, 2 * TC], BF16)
                sums0 = sb(nc, sA, "sums0%d" % l, [1, 8 * TC], F32)
                sums = sb(nc, sA, "sums%d" % l, [8, TC], F32)
                sumsb = sb(nc, sA, "sumsb%d" % l, [8, TC], BF16)
                identd = sb(nc, sA, "identd%d" % l, [128, 64], F32)
                qs8 = sb(nc, sA, "qs8_%d" % l, [128, 1], F32)
                ks1 = sb(nc, sA, "ks1_%d" % l, [128, 1], F32)
                ps_sim = [ps_(nc, sA, "psA_sim%d_%d" % (l, i), [128, 1024]) for i in range(2)]
                ps_mmA = [ps_(nc, sA, "psA_mm%d_%d" % (l, i), [128, TC]) for i in range(2)]
                ps_on = ps_(nc, sA, "psA_on%d" % l, [128, 1024])

                P.dma("sp", lambda h: h.dma_start(out=identd[:, :], in_=d["identd"]),
                      writes=[buf("identd%d" % l)])
                P.dma("sp", lambda h: h.dma_start(out=qs8[:, :], in_=d["qs%d" % l]), writes=[buf("qs8_%d" % l)])
                P.dma("sp", lambda h: h.dma_start(out=ks1[:, :], in_=d["ks%d" % l]), writes=[buf("ks1_%d" % l)])
                P.rw("vector", lambda h: h.tensor_scalar(qs8[:, :], qs8[:, :], 8.0, None, ALU.mult),
                     writes=[buf("qs8_%d" % l)])
                for g in range(DT):
                    P.dma("pl", lambda h, g=g: h.dma_start(
                        out=wqb[g][:, :], in_=d["wqkv%d" % l][128 * g:128 * (g + 1), :]),
                        writes=[buf("wqb%d" % g)])
                for g in range(4):
                    P.dma("pl", lambda h, g=g: h.dma_start(
                        out=wlb[g][:, :], in_=d["wol%d" % l][128 * g:128 * (g + 1), :]),
                        writes=[buf("wlb%d" % g)])
                for p in range(2):
                    for jc in range(8):
                        for ic in range(4):
                            if ic <= jc <= ic + 4:
                                continue
                            P.rw("gpsimd", lambda h, p=p, jc=jc, ic=ic: h.memset(
                                atT[p][:, jc * 512 + ic * 128: jc * 512 + (ic + 1) * 128], 0.0),
                                writes=[buf("atT%d" % p)])
                for p in range(3):
                    P.rw("gpsimd", lambda h, p=p: h.memset(
                        bass.AP(vt[p], 64, [[vt[p].ap().tensor.shape[1], 128], [65, 32], [1, 1]]), 1.0),
                        writes=[buf("vt%d" % p)])

                def produce_a(c):
                    t0 = c * TC
                    par = c % 2
                    xb = xin[par]
                    for g in range(DT):
                        if c == 0:
                            P.rw("gpsimd", lambda h, g=g, xb=xb: h.memset(xb[g][:, 0:1], 0.0),
                                 writes=[buf("xin%d_%d" % (par, g))])
                            P.dma("sp", lambda h, g=g, xb=xb: h.dma_start(
                                out=xb[g][:, 1:TC + 1], in_=d[src_xf][128 * g:128 * (g + 1), 0:TC]),
                                writes=[buf("xin%d_%d" % (par, g))])
                        else:
                            P.dma("sp", lambda h, g=g, t0=t0, xb=xb: h.dma_start(
                                out=xb[g][:, :], in_=d[src_xf][128 * g:128 * (g + 1), t0 - 1:t0 + TC]),
                                writes=[buf("xin%d_%d" % (par, g))])
                    for i, tn in enumerate(("c1", "s1", "c2", "s2")):
                        P.dma("pl", lambda h, i=i, tn=tn, t0=t0, par=par: h.dma_start(
                            out=tbl[par][i][:, :], in_=d[tn][:, t0:t0 + TC]),
                            writes=[buf("tbl%d_%d" % (par, i))])
                    for g in range(DT):
                        sl = (1, TC + 1) if g < 4 else (0, TC)
                        P.rw("vector", lambda h, g=g, sl=sl, xb=xb: h.tensor_copy(
                            hbf[g][:, 0:TC], xb[g][:, sl[0]:sl[1]]),
                            reads=[buf("xin%d_%d" % (par, g))], writes=[buf("hbf%d" % g)])
                    for et in range(12):
                        pm = ps_mmA[et % 2]
                        bpm = buf("psA_mm%d" % (et % 2))
                        def qkv_mm(h, et=et, pm=pm):
                            ins = None
                            for gg in range(DT):
                                ins = h.matmul(pm[:, :], wqb[gg][:, 128 * et:128 * (et + 1)],
                                               hbf[gg][:, 0:TC], start=(gg == 0), stop=(gg == DT - 1))
                            return ins
                        P.rw("tensor", qkv_mm,
                             reads=[buf("hbf%d" % gg) for gg in range(DT)] + [buf("wqb%d" % gg) for gg in range(DT)],
                             writes=[bpm])

                        if et < 4:
                            P.rw("scalar", lambda h, pm=pm: h.activation(sqAb[:, :], pm[:, :], AF.Square),
                                 reads=[bpm], writes=[buf("sqA")])
                            P.rw("tensor", lambda h, et=et: h.matmul(
                                ps_on[0:8, 0:TC], og4b[:, 8 * et:8 * (et + 1)], sqAb[:, :],
                                start=(et == 0), stop=(et == 3)),
                                reads=[buf("sqA"), buf("og4b")], writes=[buf("ps_on1")])
                            P.rw("vector", lambda h, et=et, pm=pm: h.tensor_copy(qraw[et][:, :], pm[:, :]),
                                 reads=[bpm], writes=[buf("qraw%d" % et)])
                        elif et < 8:
                            P.rw("scalar", lambda h, pm=pm: h.activation(sqAb[:, :], pm[:, :], AF.Square),
                                 reads=[bpm], writes=[buf("sqA")])
                            P.rw("tensor", lambda h, et=et: h.matmul(
                                ps_on[0:8, TC:2 * TC], og4b[:, 8 * (et - 4):8 * (et - 3)], sqAb[:, :],
                                start=(et == 4), stop=(et == 7)),
                                reads=[buf("sqA"), buf("og4b")], writes=[buf("ps_on2")])
                            P.rw("vector", lambda h, et=et, pm=pm: h.tensor_copy(kraw[et - 4][:, :], pm[:, :]),
                                 reads=[bpm], writes=[buf("kraw%d" % (et - 4))])
                        else:
                            P.rw("scalar", lambda h, et=et, pm=pm: h.copy(vraw[et - 8][:, :], pm[:, :]),
                                 reads=[bpm], writes=[buf("vraw%d" % (et - 8))])
                    for coff, bn in ((0, "ps_on1"), (TC, "ps_on2")):
                        P.rw("scalar", lambda h, coff=coff: h.activation(
                            nrm[0:8, coff:coff + TC], ps_on[0:8, coff:coff + TC], AF.Ln),
                            reads=[buf(bn)], writes=[buf("nrm")])
                        P.rw("scalar", lambda h, coff=coff: h.activation(
                            nrm[0:8, coff:coff + TC], nrm[0:8, coff:coff + TC], AF.Exp, scale=-0.5),
                            writes=[buf("nrm")])
                        P.rw("vector", lambda h, coff=coff: h.tensor_scalar_min(
                            nrm[0:8, coff:coff + TC], nrm[0:8, coff:coff + TC], 1e12),
                            writes=[buf("nrm")])
                        P.rw("vector", lambda h, coff=coff: h.tensor_copy(
                            nrmb[0:8, coff:coff + TC], nrm[0:8, coff:coff + TC]),
                            reads=[buf("nrm")], writes=[buf("nrmb")])
                def produce_b(c):
                    t0 = c * TC
                    par = c % 2
                    par3 = c % 3
                    xb = xin[par]
                    for et in range(4):
                        for (raw, wbufs, scv, scb, tci, tsi, nbase) in (
                            (qraw, qw, qs8, "qs8_%d" % l, 0, 1, 0),
                            (kraw, kw, ks1, "ks1_%d" % l, 2, 3, TC),
                        ):
                            pm = ps_mmA[0]
                            bpm = buf("psA_mm0")
                            P.rw("tensor", lambda h, et=et, nbase=nbase, pm=pm: h.matmul(
                                pm[:, :], bg4[:, 128 * et:128 * (et + 1)], nrmb[0:8, nbase:nbase + TC],
                                start=True, stop=True),
                                reads=[buf("nrmb"), buf("bg4")], writes=[bpm])
                            P.rw("vector", lambda h, et=et, raw=raw, scv=scv, pm=pm: h.scalar_tensor_tensor(
                                raw[et][:, :], raw[et][:, :], scv[:, :], pm[:, :], ALU.mult, ALU.mult),
                                reads=[buf(scb), bpm],
                                writes=[buf("%s%d" % ("qraw" if raw is qraw else "kraw", et))])
                            pm2 = ps_mmA[1]
                            bpm2 = buf("psA_mm1")
                            P.rw("tensor", lambda h, et=et, raw=raw, pm2=pm2: h.matmul(
                                pm2[:, :], perm[:, :], raw[et][:, :], start=True, stop=True),
                                reads=[buf("%s%d" % ("qraw" if raw is qraw else "kraw", et)), buf("perm")],
                                writes=[bpm2])
                            P.rw("gpsimd", lambda h, et=et, raw=raw, tci=tci, par=par: h.tensor_mul(
                                rtmp[0][:, :], raw[et][:, :], tbl[par][tci][:, :]),
                                reads=[buf("%s%d" % ("qraw" if raw is qraw else "kraw", et)),
                                       buf("tbl%d_%d" % (par, tci))],
                                writes=[buf("rtmp0")])
                            P.rw("vector", lambda h, tsi=tsi, par=par, pm2=pm2: h.tensor_mul(
                                rtmp[1][:, :], pm2[:, :], tbl[par][tsi][:, :]),
                                reads=[bpm2, buf("tbl%d_%d" % (par, tsi))], writes=[buf("rtmp1")])
                            wi = par if wbufs is qw else par3
                            P.rw("gpsimd", lambda h, et=et, wbufs=wbufs, wi=wi: h.tensor_add(
                                wbufs[wi][et][:, :], rtmp[0][:, :], rtmp[1][:, :]),
                                reads=[buf("rtmp0"), buf("rtmp1")],
                                writes=[buf("%sw%d_%d" % ("q" if wbufs is qw else "k", wi, et))])
                    for hh in range(8):
                        vcol = (hh % 2) * TC
                        vbuf = "ps_on%d" % (1 + hh % 2)
                        def tr_f(h, hh=hh, par=par, vcol=vcol):
                            ins = None
                            for sub in range(4):
                                ins = h.transpose(ps_on[:, vcol + 64 * sub:vcol + 64 * (sub + 1)],
                                                  vraw[hh // 2][64 * (hh % 2):64 * (hh % 2) + 64,
                                                                128 * sub:128 * (sub + 1)],
                                                  identd[64 * (hh % 2):64 * (hh % 2) + 64, :])
                            return ins
                        P.rw("tensor", tr_f,
                             reads=[buf("vraw%d" % (hh // 2)), buf("identd%d" % l)],
                             writes=[buf(vbuf)])
                        P.rw("vector", lambda h, hh=hh, par3=par3, vcol=vcol: h.tensor_copy(
                            bass.AP(vt[par3], hh * 260, [[vt[par3].ap().tensor.shape[1], 128], [65, 4], [1, 64]]),
                            ps_on[:, vcol:vcol + 256]),
                            reads=[buf(vbuf)], writes=[buf("vt%d" % par3)])
                def consume_heads(c, hrng):
                    t0 = c * TC
                    par = c % 2
                    par3 = c % 3
                    pv3 = (c - 1) % 3
                    xb = xin[par]
                    for hh in hrng:
                        tile = hh // 2
                        rbase = 64 * (hh % 2)
                        for pi, pair in enumerate(((0, 1), (2, 3))):
                            plo, phi = pair
                            jlo = max(4, plo) if c == 0 else plo
                            plan_mm = []
                            for jc in range(jlo, phi + 4 + 1):
                                for ic in pair:
                                    if ic <= jc <= ic + 4 and not (c == 0 and jc < 4):
                                        plan_mm.append(("qk", jc, ic, jc - ic))
                            for ic in pair:
                                if c > 0:
                                    plan_mm.append(("ml", ic, ic, 0))
                                plan_mm.append(("mr", ic + 4, ic, 4))
                            seen = set()
                            flags = []
                            for (kind, jc, ic, r) in plan_mm:
                                bk = (ic, 0 if r < 4 else 1)
                                flags.append(bk not in seen)
                                seen.add(bk)
                            last_of = {}
                            for idx, (kind, jc, ic, r) in enumerate(plan_mm):
                                last_of[(ic, 0 if r < 4 else 1)] = idx
                            lastf = [idx in last_of.values() for idx in range(len(plan_mm))]
                            def qk_f(h, plan_mm=plan_mm, flags=flags, lastf=lastf,
                                     tile=tile, rbase=rbase, par=par, par3=par3, pv3=pv3):
                                ins = None
                                for idx, (kind, jc, ic, r) in enumerate(plan_mm):
                                    if kind == "qk":
                                        kwsrc = kw[par3] if jc >= 4 else kw[pv3]
                                        kcol = (jc % 4) * 128
                                        ins = h.matmul(
                                            ps_sim[ic % 2][:, r * 128:(r + 1) * 128],
                                            kwsrc[tile][rbase:rbase + 64, kcol:kcol + 128],
                                            qw[par][tile][rbase:rbase + 64, ic * 128:(ic + 1) * 128],
                                            start=flags[idx], stop=lastf[idx])
                                    else:
                                        msk = maskl if kind == "ml" else maskr
                                        ins = h.matmul(
                                            ps_sim[ic % 2][:, r * 128:(r + 1) * 128],
                                            ident[:, :], msk[:, :],
                                            start=flags[idx], stop=lastf[idx])
                                return ins
                            reads_qk = [buf("qw%d_%d" % (par, tile)), buf("maskl"), buf("maskr"), buf("ident")]
                            if jlo < 4:
                                reads_qk.append(buf("kw%d_%d" % (pv3, tile)))
                            if phi + 4 >= 4:
                                reads_qk.append(buf("kw%d_%d" % (par3, tile)))
                            P.rw("tensor", qk_f, reads=reads_qk,
                                 writes=[buf("psA_sim0"), buf("psA_sim1")])
                            for ic in pair:
                                jlo_ic = max(max(4, ic) if c == 0 else ic, ic)
                                r0 = jlo_ic - ic
                                nr = 5 - r0
                                def exp_f(h, ic=ic, r0=r0, nr=nr, par=par):
                                    return h.activation(
                                        bass.AP(atT[par], (ic + r0) * 512 + ic * 128,
                                                [[atT[par].ap().tensor.shape[1], 128], [512, nr], [1, 128]]),
                                        ps_sim[ic % 2][:, r0 * 128:640],
                                        AF.Exp, bias=cpool[:, 0:1])
                                P.rw("scalar", exp_f,
                                     reads=[buf("psA_sim%d" % (ic % 2)), buf("cpool")],
                                     writes=[buf("atT%d_%d" % (par, ic))])
                        jset = range(4, 8) if c == 0 else range(8)
                        avcol = (hh % 2) * TC
                        avbuf = "ps_on%d" % (1 + hh % 2)
                        def av_f(h, hh=hh, par=par, par3=par3, pv3=pv3, jset=tuple(jset), avcol=avcol):
                            ins = None
                            for n_i, jc in enumerate(jset):
                                vsrc = vt[par3] if jc >= 4 else vt[pv3]
                                col = hh * 260 + (jc % 4) * 65
                                lo = max(0, jc - 4)
                                hi = min(3, jc)
                                ins = h.matmul(ps_on[0:65, avcol + lo * 128:avcol + (hi + 1) * 128],
                                               vsrc[:, col:col + 65],
                                               atT[par][:, jc * 512 + lo * 128:jc * 512 + (hi + 1) * 128],
                                               start=(n_i == 0), stop=(n_i == len(jset) - 1))
                            return ins
                        reads_av = [buf("atT%d_%d" % (par, ic)) for ic in range(4)]
                        reads_av.append(buf("atT%d" % par))
                        reads_av.append(buf("vt%d" % par3))
                        if c > 0:
                            reads_av.append(buf("vt%d" % pv3))
                        P.rw("tensor", av_f, reads=reads_av, writes=[buf(avbuf)])
                        P.rw("vector", lambda h, hh=hh, avcol=avcol: h.tensor_copy(
                            oT[hh // 2][64 * (hh % 2):64 * (hh % 2) + 64, :], ps_on[0:64, avcol:avcol + TC]),
                            reads=[buf(avbuf)], writes=[buf("oT%d" % (hh // 2))])
                        P.rw("vector", lambda h, hh=hh, avcol=avcol: h.tensor_copy(
                            sums0[0:1, hh * TC:(hh + 1) * TC], ps_on[64:65, avcol:avcol + TC]),
                            reads=[buf(avbuf)], writes=[buf("sums0")])
                def consume_tail(c):
                    t0 = c * TC
                    par = c % 2
                    xb = xin[par]
                    P.dma("pl", lambda h: h.dma_start(out=sums[:, :], in_=sums0[0:1, :]),
                          reads=[buf("sums0")], writes=[buf("sums")])
                    P.rw("scalar", lambda h: h.activation(sums[:, :], sums[:, :], AF.Ln),
                         writes=[buf("sums")])
                    P.rw("scalar", lambda h: h.activation(sumsb[:, :], sums[:, :], AF.Exp, scale=-1.0),
                         reads=[buf("sums")], writes=[buf("sumsb")])
                    for g in range(4):
                        pm = ps_mmA[g % 2]
                        bpm = buf("psA_mm%d" % (g % 2))
                        P.rw("tensor", lambda h, g=g, pm=pm: h.matmul(
                            pm[:, :], bg4[:, 128 * g:128 * (g + 1)], sumsb[:, :], start=True, stop=True),
                            reads=[buf("sumsb"), buf("bg4")], writes=[bpm])
                        P.rw("vector", lambda h, g=g, pm=pm: h.tensor_mul(oT[g][:, :], oT[g][:, :], pm[:, :]),
                             reads=[bpm], writes=[buf("oT%d" % g)])
                    for et in range(DT):
                        if c == 0:
                            P.rw("gpsimd", lambda h, et=et: h.memset(SM[et][:, 0:1], 0.0),
                                 writes=[buf("SM%d" % et)])
                        else:
                            P.rw("vector", lambda h, et=et: h.tensor_copy(SM[et][:, 0:1], SM[et][:, TC:TC + 1]),
                                 writes=[buf("SM%d" % et)])
                        pm = ps_mmA[et % 2]
                        bpm = buf("psA_mm%d" % (et % 2))
                        def wo_mm(h, et=et, pm=pm):
                            ins = None
                            for gg in range(4):
                                ins = h.matmul(pm[:, :], wlb[gg][:, 128 * et:128 * (et + 1)],
                                               oT[gg][:, :], start=(gg == 0), stop=(gg == 3))
                            return ins
                        P.rw("tensor", wo_mm,
                             reads=[buf("oT%d" % gg) for gg in range(4)] + [buf("wlb%d" % gg) for gg in range(4)],
                             writes=[bpm])
                        P.rw("vector", lambda h, et=et, pm=pm, xb=xb: h.tensor_add(
                            SM[et][:, 1:TC + 1], xb[et][:, 1:TC + 1], pm[:, :]),
                            reads=[buf("xin%d_%d" % (par, et)), bpm], writes=[buf("SM%d" % et)])
                        P.dma("pl", lambda h, et=et, t0=t0: h.dma_start(
                            out=d["sc_xm"][128 * et:128 * (et + 1), t0:t0 + TC], in_=SM[et][:, 1:TC + 1]),
                            reads=[buf("SM%d" % et)])
                    for et in range(DT):
                        sl = (1, TC + 1) if et < 4 else (0, TC)
                        P.rw("scalar", lambda h, et=et, sl=sl: h.activation(
                            sqAb[:, :], SM[et][:, sl[0]:sl[1]], AF.Square),
                            reads=[buf("SM%d" % et)], writes=[buf("sqA")])
                        P.rw("tensor", lambda h, et=et, sl=sl: h.matmul(
                            ps_sim[0][0:1, 0:TC], onesf[:, :], SM[et][:, sl[0]:sl[1]],
                            start=(et == 0), stop=(et == DT - 1)),
                            reads=[buf("SM%d" % et), buf("onesf")], writes=[buf("psA_sim0")])
                        P.rw("tensor", lambda h, et=et: h.matmul(
                            ps_sim[0][0:1, TC:2 * TC], onesb1[:, :], sqAb[:, :],
                            start=(et == 0), stop=(et == DT - 1)),
                            reads=[buf("sqA"), buf("onesb1")], writes=[buf("psA_sim0")])
                    P.rw("vector", lambda h: h.tensor_scalar(statm[:, :], ps_sim[0][0:1, 0:TC], 1.0 / 1024, None, ALU.mult),
                         reads=[buf("psA_sim0")], writes=[buf("statm")])
                    P.rw("vector", lambda h: h.tensor_scalar(stati[:, :], ps_sim[0][0:1, TC:2 * TC], 1.0 / 1024, None, ALU.mult),
                         reads=[buf("psA_sim0")], writes=[buf("stati")])
                    P.rw("vector", lambda h: h.tensor_mul(sqA[0:1, 0:TC], statm[:, :], statm[:, :]),
                         reads=[buf("statm")], writes=[buf("sqA")])
                    P.rw("vector", lambda h: h.tensor_sub(stati[:, :], stati[:, :], sqA[0:1, 0:TC]),
                         reads=[buf("sqA")], writes=[buf("stati")])
                    P.rw("scalar", lambda h: h.activation(stati[:, :], stati[:, :], AF.Ln, bias=cpool[0:1, 1:2]),
                         reads=[buf("cpool")], writes=[buf("stati")])
                    P.rw("scalar", lambda h: h.activation(stati[:, :], stati[:, :], AF.Exp, scale=-0.5),
                         writes=[buf("stati")])
                    P.dma("pl", lambda h, t0=t0: h.dma_start(
                        out=d["sc_stats"][0:1, t0:t0 + TC], in_=statm[:, :]),
                        reads=[buf("statm")])
                    P.dma("pl", lambda h, t0=t0: h.dma_start(
                        out=d["sc_stats"][1:2, t0:t0 + TC], in_=stati[:, :]),
                        reads=[buf("stati")])
                produce_a(0)
                produce_b(0)
                for c in range(NCH):
                    if c + 1 < NCH:
                        produce_a(c + 1)
                    consume_heads(c, range(0, 4))
                    if c + 1 < NCH:
                        produce_b(c + 1)
                    consume_heads(c, range(4, 8))
                    consume_tail(c)

            if PH < 4 + 2 * l:
                raise StopIteration
            P.floor = P.barrier()
            TC2 = 256
            NJ = 22
            JW = [128] * 21 + [42]
            with ExitStack() as sM:
                w1b = [sb(nc, sM, "w1b%d_%d" % (l, g), [128, 5460], BF16) for g in range(DT)]
                w2b = [sb(nc, sM, "w2b%d_%d" % (l, j), [JW[j], 1024], BF16) for j in range(NJ)]
                smallv2 = {}
                for v in ("lg", "lb", "b2"):
                    smallv2[v] = sb(nc, sM, "sv2_%s_%d" % (v, l), [128, 8], F32)
                    P.dma("sp", lambda h, v=v: h.dma_start(
                        out=smallv2[v][:, :], in_=d["%sl%d" % (v, l) if v != "b2" else "b2l%d" % l]
                        if False else d[{"lg": "lg%d", "lb": "lb%d", "b2": "b2l%d"}[v] % l]),
                        writes=[buf("sv2_" + v)])
                b1a = sb(nc, sM, "b1a_%d" % l, [128, NJ], F32)
                b1g = sb(nc, sM, "b1g_%d" % l, [128, NJ], F32)
                fsc = [sb(nc, sM, "fsc%d_%d" % (l, i), [128, TC2], BF16) for i in range(3)]
                glx = [sb(nc, sM, "glx_%d_%d" % (l, i), [128, TC2], BF16) for i in range(3)]
                sqM = sb(nc, sM, "sqM_%d" % l, [128, TC2], F32)
                rowm = [sb(nc, sM, "rowm_%d_%d" % (l, i), [1, 2 * TC2], F32) for i in range(2)]
                xz = [sb(nc, sM, "xz%d_%d" % (l, g), [128, TC2], F32) for g in range(4)]
                xz2 = [[sb(nc, sM, "xz2%d_%d_%d" % (l, p, g), [128, TC2], F32) for g in range(DT)] for p in range(2)] if l == 1 else None
                ps_acc = [ps_(nc, sM, "psM_acc%d_%d" % (l, i), [128, 512]) for i in range(4)]
                ps_ag = [ps_(nc, sM, "psM_ag%d_%d" % (l, i), [128, 512]) for i in range(2)]
                ps_aux = ps_(nc, sM, "psM_aux%d" % l, [128, 1024])
                for g in range(DT):
                    P.dma("pl", lambda h, g=g: h.dma_start(
                        out=w1b[g][:, :], in_=d["w1l%d" % l][128 * g:128 * (g + 1), :]),
                        writes=[buf("w1b%d" % g)])
                for j in range(NJ):
                    P.dma("pl", lambda h, j=j: h.dma_start(
                        out=w2b[j][:, :], in_=d["w2l%d" % l][128 * j:128 * j + JW[j], :]),
                        writes=[buf("w2b%d" % j)])
                P.dma("sp", lambda h: h.dma_start(out=b1a[:, :], in_=d["b1a%d" % l]), writes=[buf("b1a")])
                P.dma("sp", lambda h: h.dma_start(out=b1g[:, :], in_=d["b1g%d" % l]), writes=[buf("b1g")])
                NCH2 = T // TC2
                def prep(c2):
                    t0 = c2 * TC2
                    par = c2 % 2
                    xb = xin[par]
                    co = par * TC2
                    for g in range(DT):
                        if c2 == 0:
                            P.rw("gpsimd", lambda h, g=g, xb=xb: h.memset(xb[g][:, 0:1], 0.0),
                                 writes=[buf("xin%d_%d" % (par, g))])
                            P.dma("sp", lambda h, g=g, xb=xb: h.dma_start(
                                out=xb[g][:, 1:TC2 + 1], in_=d["sc_xm"][128 * g:128 * (g + 1), 0:TC2]),
                                writes=[buf("xin%d_%d" % (par, g))])
                        else:
                            P.dma("sp", lambda h, g=g, t0=t0, xb=xb: h.dma_start(
                                out=xb[g][:, 0:TC2 + 1], in_=d["sc_xm"][128 * g:128 * (g + 1), t0 - 1:t0 + TC2]),
                                writes=[buf("xin%d_%d" % (par, g))])
                        if l == 1:
                            P.dma("sp", lambda h, g=g, t0=t0, par=par: h.dma_start(
                                out=xz2[par][g][:, :], in_=d["sc_x1"][128 * g:128 * (g + 1), t0:t0 + TC2]),
                                writes=[buf("xz2_%d_%d" % (par, g))])
                    P.dma("sp", lambda h, t0=t0, par=par: h.dma_start(
                        out=rowm[par][:, :], in_=d["sc_stats"][:, t0:t0 + TC2]),
                        writes=[buf("rowm%d" % par)])
                    aco = par * 2 * TC2
                    P.rw("tensor", lambda h, par=par, aco=aco: h.matmul(
                        ps_aux[:, aco:aco + TC2], bonef[:, :], rowm[par][0:1, 0:TC2], start=True, stop=False),
                        reads=[buf("rowm%d" % par), buf("bonef")], writes=[buf("psM_aux%d" % par)])
                    P.rw("tensor", lambda h, par=par, aco=aco: h.matmul(
                        ps_aux[:, aco + TC2:aco + 2 * TC2], bonef[:, :], rowm[par][0:1, TC2:2 * TC2], start=False, stop=True),
                        reads=[buf("rowm%d" % par), buf("bonef")], writes=[buf("psM_aux%d" % par)])
                    for g in range(DT):
                        sl = (1, TC2 + 1) if g < 4 else (0, TC2)
                        gx = glx[g % 2]
                        gbuf = buf("glx%d" % (g % 2))
                        P.rw("vector", lambda h, g=g, sl=sl, xb=xb, gx=gx, aco=aco, par=par: h.tensor_sub(
                            gx[:, :], xb[g][:, sl[0]:sl[1]], ps_aux[:, aco:aco + TC2]),
                            reads=[buf("xin%d_%d" % (par, g)), buf("psM_aux%d" % par)], writes=[gbuf])
                        P.rw("vector", lambda h, g=g, gx=gx, aco=aco, par=par: h.tensor_mul(
                            gx[:, :], gx[:, :], ps_aux[:, aco + TC2:aco + 2 * TC2]),
                            reads=[buf("psM_aux%d" % par)], writes=[gbuf])
                        P.rw("vector", lambda h, g=g, gx=gx, co=co, par=par: h.tensor_scalar(
                            hbf[g][:, co:co + TC2], gx[:, :], smallv2["lg"][:, g:g + 1], smallv2["lb"][:, g:g + 1],
                            ALU.mult, ALU.add),
                            reads=[gbuf, buf("sv2_lg"), buf("sv2_lb")], writes=[buf("hbf_%d_%d" % (par, g))])
                def emit_ag(c2, j):
                    par = c2 % 2
                    co = par * TC2
                    wj = JW[j]
                    pag = ps_ag[j % 2]
                    pagbuf = buf("psM_ag%d" % (j % 2))
                    def ag_mm(h, j=j, wj=wj, pag=pag, co=co):
                        ins = None
                        for gg in range(DT):
                            ins = h.matmul(pag[0:wj, 0:TC2], w1b[gg][:, 128 * j:128 * j + wj],
                                           hbf[gg][:, co:co + TC2], start=(gg == 0), stop=False)
                        for gg in range(DT):
                            ins = h.matmul(pag[0:wj, TC2:2 * TC2], w1b[gg][:, 2730 + 128 * j:2730 + 128 * j + wj],
                                           hbf[gg][:, co:co + TC2], start=False, stop=(gg == DT - 1))
                        return ins
                    P.rw("tensor", ag_mm,
                         reads=[buf("hbf_%d_%d" % (par, gg)) for gg in range(DT)] + [buf("w1b%d" % gg) for gg in range(DT)],
                         writes=[pagbuf])
                    fb_ = fsc[j % 3]
                    bfb = buf("fsc%d" % (j % 3))
                    gx = glx[j % 3]
                    gbuf = buf("glx%d" % (j % 3))
                    P.rw("scalar", lambda h, j=j, wj=wj, pag=pag, gx=gx: h.activation(
                        gx[0:wj, :], pag[0:wj, TC2:2 * TC2], AF.Gelu, bias=b1g[0:wj, j:j + 1]),
                        reads=[pagbuf, buf("b1g")], writes=[gbuf])
                    P.rw("vector", lambda h, j=j, wj=wj, fb_=fb_, pag=pag, gx=gx: h.scalar_tensor_tensor(
                        fb_[0:wj, :], pag[0:wj, 0:TC2], b1a[0:wj, j:j + 1], gx[0:wj, :],
                        ALU.add, ALU.mult),
                        reads=[pagbuf, buf("b1a"), gbuf], writes=[bfb])
                def emit_w2(j):
                    wj = JW[j]
                    fb_ = fsc[j % 3]
                    bfb = buf("fsc%d" % (j % 3))
                    def w2_mm(h, j=j, wj=wj, fb_=fb_):
                        ins = None
                        for et in range(DT):
                            ins = h.matmul(ps_acc[et // 2][:, (et % 2) * TC2:(et % 2) * TC2 + TC2],
                                           w2b[j][0:wj, 128 * et:128 * (et + 1)], fb_[0:wj, :],
                                           start=(j == 0 and et % 2 == 0), stop=(j == NJ - 1 and et % 2 == 1))
                        return ins
                    P.rw("tensor", w2_mm,
                         reads=[bfb, buf("w2b%d" % j)],
                         writes=[buf("psM_acc0"), buf("psM_acc1"), buf("psM_acc2"), buf("psM_acc3")])
                def evac(c2):
                    t0 = c2 * TC2
                    par = c2 % 2
                    xb = xin[par]
                    for et in range(DT):
                        xe = xz[et % 4]
                        xbuf = buf("xz%d" % (et % 4))
                        P.rw("vector", lambda h, et=et, xb=xb, xe=xe: h.scalar_tensor_tensor(
                            xe[:, :], ps_acc[et // 2][:, (et % 2) * TC2:(et % 2) * TC2 + TC2],
                            smallv2["b2"][:, et:et + 1], xb[et][:, 1:TC2 + 1], ALU.add, ALU.add),
                            reads=[buf("psM_acc%d" % (et // 2)), buf("sv2_b2"), buf("xin%d_%d" % (par, et))],
                            writes=[xbuf])
                        if l == 0:
                            P.dma("pl", lambda h, et=et, t0=t0, xe=xe: h.dma_start(
                                out=d["sc_xf1"][128 * et:128 * (et + 1), t0:t0 + TC2], in_=xe[:, :]),
                                reads=[xbuf])
                        else:
                            P.rw("vector", lambda h, et=et, par=par, xe=xe: h.tensor_add(
                                xe[:, :], xe[:, :], xz2[par][et][:, :]),
                                reads=[buf("xz2_%d_%d" % (par, et))], writes=[xbuf])
                            P.dma("pl", lambda h, et=et, t0=t0, xe=xe: h.dma_start(
                                out=d["outT"][128 * et:128 * (et + 1), t0:t0 + TC2], in_=xe[:, :]),
                                reads=[xbuf])
                prep(0)
                for c2 in range(NCH2):
                    emit_ag(c2, 0)
                    emit_ag(c2, 1)
                    for j in range(2, NJ):
                        emit_ag(c2, j)
                        if j == 4 and c2 + 1 < NCH2:
                            prep(c2 + 1)
                        emit_w2(j - 2)
                    emit_w2(NJ - 2)
                    emit_w2(NJ - 1)
                    evac(c2)

      except StopIteration:
        pass
      fin = P.barrier()
      P._emit("sync", lambda h: h.nop(), fin, None, 0)
      P.run()
    return nc, d


def stage_inputs(ins, b):
    m = {}
    m["xT"] = np.ascontiguousarray(ins["L0_attn_x"][b].T.astype(np.float32))
    m["wrT"] = np.ascontiguousarray(ins["L0_attn_W_r"].T.astype(np.float32))
    m["wkT"] = np.ascontiguousarray(ins["L0_attn_W_k"].T.astype(np.float32))
    m["wvT"] = np.ascontiguousarray(ins["L0_attn_W_v"].T.astype(np.float32))
    m["woT"] = np.ascontiguousarray(ins["L0_attn_W_o"].T.astype(np.float32))
    for k, src in (("tmr", "time_mix_r"), ("tmk", "time_mix_k"), ("tmv", "time_mix_v")):
        m[k] = np.ascontiguousarray(ins["L0_attn_" + src].astype(np.float32).reshape(8, 128).T)
    m["decay"] = np.ascontiguousarray(ins["L0_attn_time_decay"].astype(np.float32).reshape(-1).reshape(8, 128).T)
    m["first"] = np.ascontiguousarray(ins["L0_attn_time_first"].astype(np.float32).reshape(-1).reshape(8, 128).T)
    m["gnw"] = np.ascontiguousarray(ins["L0_attn_gn_weight"].astype(np.float32).reshape(8, 128).T)
    m["gnb"] = np.ascontiguousarray(ins["L0_attn_gn_bias"].astype(np.float32).reshape(8, 128).T)
    import ml_dtypes
    for l in range(2):
        m["wqkv%d" % l] = np.ascontiguousarray(ins["L0_ffn_wqkv"][l].astype(np.float32))
        m["wol%d" % l] = np.ascontiguousarray(ins["L0_ffn_wo"][l].astype(np.float32))
        m["w1l%d" % l] = np.ascontiguousarray(ins["L0_ffn_w1"][l].astype(np.float32))
        m["w2l%d" % l] = np.ascontiguousarray(ins["L0_ffn_w2"][l].astype(np.float32))
        m["qs%d" % l] = np.ascontiguousarray(np.tile(ins["L0_ffn_q_scale"][l].astype(np.float32), 2)[:, None])
        m["ks%d" % l] = np.ascontiguousarray(np.tile(ins["L0_ffn_k_scale"][l].astype(np.float32), 2)[:, None])
        m["lg%d" % l] = np.ascontiguousarray(ins["L0_ffn_ln_g"][l].astype(np.float32).reshape(8, 128).T)
        m["lb%d" % l] = np.ascontiguousarray(ins["L0_ffn_ln_b"][l].astype(np.float32).reshape(8, 128).T)
        m["b1l%d" % l] = np.ascontiguousarray(ins["L0_ffn_b1"][l].astype(np.float32)[:, None])
        _b1 = ins["L0_ffn_b1"][l].astype(np.float32)
        _pa = np.zeros(22 * 128, np.float32); _pa[:2730] = _b1[:2730]
        _pg = np.zeros(22 * 128, np.float32); _pg[:2730] = _b1[2730:]
        m["b1a%d" % l] = np.ascontiguousarray(_pa.reshape(22, 128).T)
        m["b1g%d" % l] = np.ascontiguousarray(_pg.reshape(22, 128).T)
        m["b2l%d" % l] = np.ascontiguousarray(ins["L0_ffn_b2"][l].astype(np.float32).reshape(8, 128).T)
    i128 = np.arange(128)
    m["identb"] = np.eye(128, dtype=ml_dtypes.bfloat16)
    m["identf"] = np.eye(128, dtype=np.float32)
    m["identd"] = np.ascontiguousarray(np.tile(np.eye(64, dtype=np.float32), (2, 1)))
    perm = np.zeros((128, 128), np.float32)
    for po in range(128):
        if po % 64 < 32:
            perm[po + 32, po] = -1.0
        else:
            perm[po - 32, po] = 1.0
    m["permb"] = perm.astype(ml_dtypes.bfloat16)
    m["onesf"] = np.ones((128, 1), np.float32)
    hv = np.zeros((128, 2), np.float32)
    hv[:64, 0] = 1.0
    hv[64:, 1] = 1.0
    m["halvesf"] = hv
    og = np.zeros((128, 32), np.float32)
    for g in range(4):
        for p in range(128):
            og[p, g * 8 + 2 * g + (1 if p >= 64 else 0)] = 1.0
    m["og4"] = og
    m["og4b"] = og.astype(ml_dtypes.bfloat16)
    m["onesb1"] = np.ones((128, 1), ml_dtypes.bfloat16)
    _p = np.arange(128)[:, None]; _f = np.arange(128)[None, :]
    m["maskl"] = np.where(_p >= _f, 0.0, -3e38).astype(ml_dtypes.bfloat16)
    m["maskr"] = np.where(_p <= _f, 0.0, -3e38).astype(ml_dtypes.bfloat16)
    bg = np.zeros((8, 512), np.float32)
    for g in range(4):
        bg[2 * g, g * 128: g * 128 + 64] = 1.0
        bg[2 * g + 1, g * 128 + 64: g * 128 + 128] = 1.0
    m["bg4"] = bg.astype(ml_dtypes.bfloat16)
    m["bonef"] = np.ones((1, 128), np.float32)
    h2 = np.zeros((2, 128), np.float32)
    h2[0, :64] = 1.0
    h2[1, 64:] = 1.0
    m["halves2"] = h2
    DIM_HEAD, WINDOW = 64, 512
    half_idx = np.arange(0, DIM_HEAD, 2, dtype=np.float64)
    inv_freq = 1.0 / 10000.0 ** (half_idx / DIM_HEAD)
    tt = np.arange(T, dtype=np.float64)
    freqs = np.concatenate([tt[:, None] * inv_freq[None, :]] * 2, axis=-1)
    base_ = (half_idx + 0.4 * DIM_HEAD) / (1.4 * DIM_HEAD)
    power = (tt - T // 2) / float(WINDOW // 2)
    scale = np.concatenate([base_[None, :] ** power[:, None]] * 2, axis=-1)
    cos, sin = np.cos(freqs), np.sin(freqs)
    m["c1"] = np.ascontiguousarray(np.tile((cos * scale).T, (2, 1)).astype(np.float32))
    m["s1"] = np.ascontiguousarray(np.tile((sin * scale).T, (2, 1)).astype(np.float32))
    m["c2"] = np.ascontiguousarray(np.tile((cos / scale).T, (2, 1)).astype(np.float32))
    m["s2"] = np.ascontiguousarray(np.tile((sin / scale).T, (2, 1)).astype(np.float32))
    return m




_CACHE = {}

def kernel(**inputs):
    if "nc" not in _CACHE:
        nc, d = build(debug=0)
        _CACHE["nc"] = nc
    nc = _CACHE["nc"]
    base = stage_inputs(inputs, 0)
    in_maps = []
    for b in range(8):
        m = dict(base)
        m["xT"] = np.ascontiguousarray(inputs["L0_attn_x"][b].T.astype(np.float32))
        in_maps.append(m)
    res = run_bass_kernel_spmd(nc, in_maps, core_ids=list(range(8)))
    out = np.stack([np.asarray(r["outT"]).T for r in res.results], axis=0)
    return np.ascontiguousarray(out.astype(np.float32))
